# version 03
# baseline (speedup 1.0000x reference)
import contextlib
import numpy as np
import concourse.bass as bass
import concourse.mybir as mybir
from concourse.bass_utils import run_bass_kernel_spmd

F32 = mybir.dt.float32
BF16 = mybir.dt.bfloat16
ALU = mybir.AluOpType
AX = mybir.AxisListType
ACT = mybir.ActivationFunctionType

B, S, D = 2, 4096, 2048
H, KVH, HD = 16, 4, 128
KVD = KVH * HD
TPC, NT, NC, NCH = 1024, 8, 32, 8
RND = 12582912.0
QEPS = 1e-5
LNEPS = 1e-5
RMSEPS = float(np.finfo(np.float32).eps)
SMSCALE = float(1.0 / np.sqrt(np.float32(128.0)))
NEGBIG = -1.0e30


class Sem:
    def __init__(self, es, nc, name):
        self.h = es.enter_context(nc.semaphore(name))
        self.n = 0


class Prog:

    def __init__(self):
        self.ops = []
        self.chain_sems = ()
        self.auto_chain = {}

    def op(self, eng, emit, waits=(), incs=()):
        waits = list(waits)
        incs = list(incs)
        for s, amt in incs:
            if s in self.chain_sems:
                waits.append((s, s.n))
        ac = self.auto_chain.get(eng)
        if ac is not None:
            waits.append((ac, ac.n))
            if not any(s is ac for s, _ in incs):
                incs.append((ac, 1))
        for s, amt in incs:
            s.n += amt
        self.ops.append((eng, tuple(waits), emit, tuple(incs)))

    def replay(self, eng_name, eng):
        for name, waits, emit, incs in self.ops:
            if name != eng_name:
                continue
            for s, v in waits:
                eng.wait_ge(s.h, v)
            ins = emit(eng)
            for s, amt in incs:
                ins = ins.then_inc(s.h, amt)


def build_program(debug=False, stop_early=False):
    nc = bass.Bass("TRN2", target_bir_lowering=False)

    def din(name, shape, dtype=F32):
        return nc.dram_tensor(name, shape, dtype, kind="ExternalInput").ap()

    x_kv = din("x_kv", [S, D])
    x_q = din("x_q", [TPC, D])
    w_qkv = din("w_qkv", [3072, D])
    w_o = din("w_o", [D, D])
    W_T = din("W_T", [D, D])
    biasv = din("biasv", [128, D])
    rw1_T = din("rw1_T", [D, 64])
    rw2_T = din("rw2_T", [64, 32])
    rb1 = din("rb1", [128, 64])
    rb2 = din("rb2", [128, 32])
    gates = din("gates", [128, 16])
    qgain = din("qgain", [128, 16])
    A_T = din("A_T", [D, 256])
    B_mov = din("B_mov", [256, D])
    cos_q = din("cos_q", [TPC, HD])
    sin_q = din("sin_q", [TPC, HD])
    cos_k = din("cos_k", [S, HD])
    sin_k = din("sin_k", [S, HD])
    eyef = din("eyef", [128, 128])
    constsv = din("constsv", [128, 8])
    dmask = din("dmask", [128, 512])
    out = nc.dram_tensor("o", [TPC, D], F32, kind="ExternalOutput").ap()

    kk = dict(kind="ExternalOutput") if debug else {}
    K_scr = nc.dram_tensor("K_scr", [KVH, S, HD], F32, **kk).ap()
    V_scr = nc.dram_tensor("V_scr", [KVH, S, HD], F32, **kk).ap()
    Q_scr = nc.dram_tensor("Q_scr", [H, TPC, HD], F32, **kk).ap()
    if debug:
        dbg_y = nc.dram_tensor("dbg_y", [TPC, D], F32,
                               kind="ExternalOutput").ap()
        dbg_ws = nc.dram_tensor("dbg_ws", [128, 3072], F32,
                                kind="ExternalOutput").ap()
        dbg_wkv = nc.dram_tensor("dbg_wkv", [128, 16 * 1024], BF16,
                                 kind="ExternalOutput").ap()
        dbg_xq = nc.dram_tensor("dbg_xq", [128, D], F32,
                                kind="ExternalOutput").ap()
        dbg_kvd = nc.dram_tensor("dbg_kvd", [128, 1024], F32,
                                 kind="ExternalOutput").ap()
        dbg_xb = nc.dram_tensor("dbg_xb", [128, D], F32,
                                kind="ExternalOutput").ap()
        dbg_x1 = nc.dram_tensor("dbg_x1", [128, D], F32,
                                kind="ExternalOutput").ap()
        dbg_tm = nc.dram_tensor("dbg_tm", [128, D], F32,
                                kind="ExternalOutput").ap()
        dbg_xqt = nc.dram_tensor("dbg_xqt", [128, D], BF16,
                                 kind="ExternalOutput").ap()
        dbg_st = nc.dram_tensor("dbg_st", [128, 64], F32,
                                kind="ExternalOutput").ap()
    ws3_dram = nc.dram_tensor("ws3_dram", [1, 3072], F32).ap()
    wso_dram = nc.dram_tensor("wso_dram", [1, D], F32).ap()

    es = contextlib.ExitStack()
    p = Prog()
    sL = Sem(es, nc, "sL")
    sG = Sem(es, nc, "sG")
    sP = Sem(es, nc, "sP")
    sA = Sem(es, nc, "sA")
    sV = Sem(es, nc, "sV")
    sB = Sem(es, nc, "sB")
    p.chain_sems = (sL, sG)
    p.auto_chain = {"vector": sV, "scalar": sA}

    def sbg(st_, name, shape, dtype=F32):
        return st_.enter_context(nc.sbuf_tensor(name, shape, dtype))

    def psg(st_, name, shape, dtype=F32):
        return st_.enter_context(nc.psum_tensor(name, shape, dtype))

    yx = sbg(es, "yx", [128, NT * D])
    eyef_s = sbg(es, "eyef_s", [128, 128])
    stats = sbg(es, "stats", [128, 64])
    (C_S1, C_MU, C_S2, C_VAR, C_RSTD, C_AM, C_QS, C_RA, C_MU2, C_M1, C_M2,
     C_NM1, C_E2, C_DEN, C_RDEN, C_C2) = range(16)
    asc_kv = sbg(es, "asc_kv", [128, NC])
    asc_q = sbg(es, "asc_q", [128, NT])
    asc_y = sbg(es, "asc_y", [128, NT])
    qg_s = sbg(es, "qg_s", [128, 16])
    gt_s = sbg(es, "gt_s", [128, 16])
    csts = sbg(es, "csts", [128, 8])
    nc.const_aps.register(F32, LNEPS, csts[:, 0:1])
    nc.const_aps.register(F32, RMSEPS, csts[:, 1:2])
    nc.const_aps.register(F32, RND, csts[:, 2:3])
    nc.const_aps.register(F32, 0.0, csts[:, 3:4])
    nc.const_aps.register(F32, -RND, csts[:, 4:5])
    phWS = contextlib.ExitStack()
    ws3x = sbg(phWS, "ws3x", [128, 3072])

    p.op("vector", lambda e: e.memset(stats[:], 0.0), incs=[(sV, 1)])
    p.op("sync", lambda e: e.dma_start(out=eyef_s[:], in_=eyef), incs=[(sL, 16)])
    p.op("sync", lambda e: e.dma_start(out=csts[:], in_=constsv),
         incs=[(sL, 16)])
    p.op("sync", lambda e: e.dma_start(out=qg_s[:], in_=qgain),
         incs=[(sL, 16)])
    p.op("sync", lambda e: e.dma_start(out=gt_s[:], in_=gates),
         incs=[(sL, 16)])

    def st(c):
        return stats[:, c:c + 1]

    def bcast(t, off, n):
        return bass.AP(t, off, [[0, 128], [1, n]])

    def barrier():
        bw = [(sP, sP.n), (sL, sL.n), (sG, sG.n), (sA, sA.n), (sV, sV.n)]
        p.op("scalar", lambda e: e.activation(
            out=stats[:, 62:63], in_=stats[:, 62:63], func=ACT.Copy),
            waits=bw)
        p.op("vector", lambda e: e.tensor_copy(
            out=stats[:, 61:62], in_=stats[:, 61:62]), waits=bw)
        return list(bw) + [(sA, sA.n), (sV, sV.n)]

    def ln_quant(xt, tmpa, xq, asc_ap, extra_wait=(), dbg_mid=False):
        p.op("vector", lambda e: e.tensor_reduce(
            out=st(C_S1), in_=xt[:], axis=AX.X, op=ALU.add),
            waits=list(extra_wait) + [(sA, sA.n)], incs=[(sV, 1)])
        p.op("scalar", lambda e: e.activation(
            out=tmpa[:], in_=xt[:], func=ACT.Square, accum_out=st(C_S2)),
            waits=list(extra_wait) + [(sV, sV.n)])
        p.op("scalar", lambda e: e.activation(
            out=st(C_MU), in_=st(C_S1), func=ACT.Copy, scale=1.0 / D),
            waits=[(sV, sV.n)], incs=[(sA, 1)])
        p.op("scalar", lambda e: e.activation(
            out=st(C_MU2), in_=st(C_MU), func=ACT.Square),
            waits=[(sA, sA.n)], incs=[(sA, 1)])
        p.op("vector", lambda e: e.tensor_scalar(
            out=st(C_VAR), in0=st(C_S2), scalar1=1.0 / D, scalar2=st(C_MU2),
            op0=ALU.mult, op1=ALU.subtract), waits=[(sA, sA.n)],
            incs=[(sV, 1)])
        p.op("scalar", lambda e: e.activation(
            out=st(C_RSTD), in_=st(C_VAR), func=ACT.Sqrt, bias=LNEPS),
            waits=[(sV, sV.n)], incs=[(sA, 1)])
        p.op("vector", lambda e: e.reciprocal(
            out=st(C_RSTD), in_=st(C_RSTD)), waits=[(sA, sA.n)],
            incs=[(sV, 1)])
        p.op("vector", lambda e: e.tensor_scalar(
            out=tmpa[:], in0=xt[:], scalar1=st(C_MU), scalar2=st(C_RSTD),
            op0=ALU.subtract, op1=ALU.mult), waits=[(sV, sV.n)])
        p.op("vector", lambda e: e.reduce_max(
            out=st(C_AM), in_=tmpa[:], axis=AX.X, apply_absolute_value=True),
            incs=[(sV, 1)])
        p.op("vector", lambda e: e.tensor_scalar(
            out=st(C_QS), in0=st(C_AM), scalar1=1.0 / 127.0, scalar2=QEPS,
            op0=ALU.mult, op1=ALU.max), waits=[(sV, sV.n)], incs=[(sV, 1)])
        p.op("vector", lambda e: e.tensor_copy(out=asc_ap, in_=st(C_QS)),
             waits=[(sV, sV.n)])
        p.op("vector", lambda e: e.reciprocal(out=st(C_RA), in_=st(C_QS)),
             incs=[(sV, 1)])
        p.op("scalar", lambda e: e.activation(
            out=xq[:], in_=tmpa[:], func=ACT.Copy, scale=st(C_RA), bias=RND),
            waits=[(sV, sV.n)], incs=[(sA, 1)])
        if dbg_mid:
            p.op("gpsimd", lambda e: e.dma_start(out=dbg_x1, in_=xq[:]),
                 waits=[(sA, sA.n)], incs=[(sG, 16)])
            p.op("vector", lambda e: e.tensor_copy(
                out=st(C_S1), in_=st(C_S1)), waits=[(sG, sG.n)],
                incs=[(sV, 1)])
        p.op("scalar", lambda e: e.activation(
            out=tmpa[:], in_=xq[:], func=ACT.Copy, bias=-RND),
            waits=[(sA, sA.n), (sV, sV.n)], incs=[(sA, 1)])
        if dbg_mid:
            p.op("gpsimd", lambda e: e.dma_start(out=dbg_tm, in_=tmpa[:]),
                 waits=[(sA, sA.n)], incs=[(sG, 16)])
            p.op("vector", lambda e: e.tensor_copy(
                out=st(C_S1), in_=st(C_S1)), waits=[(sG, sG.n)],
                incs=[(sV, 1)])
        p.op("scalar", lambda e: e.activation(
            out=xq[:], in_=tmpa[:], func=ACT.Copy),
            waits=[(sV, sV.n)], incs=[(sA, 1)])
        p.op("vector", lambda e: e.tensor_copy(
            out=st(C_S1), in_=st(C_S1)), waits=[(sA, sA.n)], incs=[(sV, 1)])

    def quant_w_tile(wbuf, wtmp, wqb, ws_dram_ap, ws_col, rowlen):
        p.op("vector", lambda e: e.reduce_max(
            out=st(C_AM), in_=wbuf[:], axis=AX.X, apply_absolute_value=True),
            waits=[(sL, sL.n), (sG, sG.n), (sA, sA.n), (sV, sV.n)],
            incs=[(sV, 1)])
        p.op("vector", lambda e: e.tensor_scalar(
            out=st(C_QS), in0=st(C_AM), scalar1=1.0 / 127.0, scalar2=QEPS,
            op0=ALU.mult, op1=ALU.max), waits=[(sV, sV.n)], incs=[(sV, 1)])
        p.op("vector", lambda e: e.reciprocal(out=st(C_RA), in_=st(C_QS)),
             waits=[(sV, sV.n)], incs=[(sV, 1)])
        p.op("gpsimd", lambda e: e.dma_start(
            out=bass.AP(ws_dram_ap.tensor, ws_col,
                        [[rowlen, 1], [1, 128]]), in_=st(C_QS)),
            waits=[(sV, sV.n)], incs=[(sG, 16)])
        p.op("scalar", lambda e: e.activation(
            out=wtmp[:], in_=wbuf[:], func=ACT.Copy, scale=st(C_RA),
            bias=RND), waits=[(sV, sV.n)], incs=[(sA, 1)])
        p.op("scalar", lambda e: e.activation(
            out=wqb[:], in_=wtmp[:], func=ACT.Copy, bias=-RND),
            waits=[(sA, sA.n)], incs=[(sA, 1)])
        p.op("vector", lambda e: e.tensor_copy(
            out=st(C_S1), in_=st(C_S1)), waits=[(sA, sA.n)], incs=[(sV, 1)])

    def transposeN(src, col0, tp, dst_ap, nkb, src_wait):
        for kb in range(nkb):
            p.op("tensor", lambda e, kb=kb: e.transpose(
                tp[:, kb * 128:(kb + 1) * 128],
                src[:, col0 + kb * 128:col0 + (kb + 1) * 128], eyef_s[:]),
                waits=list(src_wait) if kb == 0 else (),
                incs=[(sP, 1)] if kb == nkb - 1 else ())
        p.op("scalar", lambda e: e.activation(
            out=dst_ap, in_=tp[:, 0:nkb * 128], func=ACT.Copy),
            waits=[(sP, sP.n)], incs=[(sA, 1)])

    def rope(src, dst, sc, rp, nh):
        cosx = bass.AP(rp, 0, [[2 * HD, 128], [0, nh], [1, HD]])
        sin0 = bass.AP(rp, HD, [[2 * HD, 128], [0, nh], [1, 64]])
        sin1 = bass.AP(rp, HD + 64, [[2 * HD, 128], [0, nh], [1, 64]])
        w_ = nh * HD

        def v3(t, off, inner):
            return bass.AP(t, off, [[w_ if t is not sc else nh * 64, 128],
                                    [HD if t is not sc else 64, nh],
                                    [1, inner]])

        scv = bass.AP(sc, 0, [[nh * 64, 128], [64, nh], [1, 64]])
        p.op("vector", lambda e: e.tensor_tensor(
            out=v3(dst, 0, HD), in0=v3(src, 0, HD), in1=cosx, op=ALU.mult))
        p.op("vector", lambda e: e.tensor_tensor(
            out=scv, in0=v3(src, 64, 64), in1=sin0, op=ALU.mult))
        p.op("vector", lambda e: e.tensor_tensor(
            out=v3(dst, 0, 64), in0=v3(dst, 0, 64), in1=scv, op=ALU.add))
        p.op("vector", lambda e: e.tensor_tensor(
            out=scv, in0=v3(src, 0, 64), in1=sin1, op=ALU.mult))
        p.op("vector", lambda e: e.tensor_tensor(
            out=v3(dst, 64, 64), in0=v3(dst, 64, 64), in1=scv, op=ALU.add),
            incs=[(sV, 1)])

    phKV = contextlib.ExitStack()
    wkvT = sbg(phKV, "wkvT", [128, 16 * 1024], BF16)
    with contextlib.ExitStack() as ph:
        wt = [sbg(ph, "wt0", [128, D]), sbg(ph, "wt1", [128, D])]
        wtmp = sbg(ph, "wtmp", [128, D])
        wqb = sbg(ph, "wqb", [128, D])
        tpw = psg(ph, "tpw", [128, 2048])
        for r in range(8):
            buf = wt[r % 2]
            p.op("sync", lambda e, r=r, buf=buf: e.dma_start(
                out=buf[:], in_=w_qkv[D + r * 128:D + (r + 1) * 128, :]),
                waits=[(sV, sV.n)] if r >= 2 else [], incs=[(sL, 16)])
            quant_w_tile(buf, wtmp, wqb, ws3_dram, 2048 + r * 128, 3072)
            transposeN(wqb, 0, tpw,
                       bass.AP(wkvT, r * 128,
                               [[16 * 1024, 128], [1024, 16], [1, 128]]),
                       16, src_wait=[(sV, sV.n), (sA, sA.n)])

    barrier()
    if debug:
        p.op("gpsimd", lambda e: e.dma_start(out=dbg_wkv, in_=wkvT[:]),
             waits=[(sA, sA.n)], incs=[(sG, 16)])
    p.op("gpsimd", lambda e: e.dma_start(
        out=ws3x[:, 2048:3072],
        in_=bass.AP(ws3_dram.tensor, 2048, [[3072, 1], [0, 128], [1, 1024]])),
        waits=[(sG, sG.n)], incs=[(sG, 16)])
    ws3_rep1 = sG.n

    with contextlib.ExitStack() as ph:
        xt = [sbg(ph, "xt0", [128, D]), sbg(ph, "xt1", [128, D])]
        rop = [sbg(ph, "rop0", [128, 2 * HD]), sbg(ph, "rop1", [128, 2 * HD])]
        tmpa = sbg(ph, "tmpa", [128, D])
        xqb = sbg(ph, "xqb", [128, D])
        xqT = sbg(ph, "xqT", [128, D], BF16)
        kvd = sbg(ph, "kvd", [128, 2 * KVD])
        krm = sbg(ph, "krm", [128, KVD])
        kro = sbg(ph, "kro", [128, KVD])
        ksc = sbg(ph, "ksc", [128, KVH * 64])
        tpx = psg(ph, "tpx", [128, 2048])
        pKV = psg(ph, "pKV", [128, 1024])
        for t in range(1 if stop_early else NC):
            xb, rp = xt[t % 2], rop[t % 2]
            p.op("sync", lambda e, t=t, xb=xb: e.dma_start(
                out=xb[:], in_=x_kv[t * 128:(t + 1) * 128, :]),
                waits=[(sV, sV.n)] if t >= 2 else
                [(sP, sP.n), (sA, sA.n)], incs=[(sL, 16)])
            p.op("sync", lambda e, t=t, rp=rp: e.dma_start(
                out=rp[:, 0:HD], in_=cos_k[t * 128:(t + 1) * 128, :]),
                incs=[(sL, 16)])
            p.op("sync", lambda e, t=t, rp=rp: e.dma_start(
                out=rp[:, HD:2 * HD], in_=sin_k[t * 128:(t + 1) * 128, :]),
                incs=[(sL, 16)])
            ln_quant(xb, tmpa, xqb, asc_kv[:, t:t + 1],
                     extra_wait=[(sL, sL.n)], dbg_mid=(debug and t == 0))
            if debug and t == 0:
                p.op("gpsimd", lambda e, xqb=xqb: e.dma_start(
                    out=dbg_xq, in_=xqb[:]),
                     waits=[(sV, sV.n)], incs=[(sG, 16)])
                p.op("gpsimd", lambda e, xb=xb: e.dma_start(out=dbg_xb,
                                                            in_=xb[:]),
                     incs=[(sG, 16)])
                p.op("gpsimd", lambda e: e.dma_start(out=dbg_st,
                                                     in_=stats[:]),
                     incs=[(sG, 16)])
            transposeN(xqb, 0, tpx, bass.AP(xqT, 0, [[D, 128], [1, D]]), 16,
                       src_wait=[(sV, sV.n), (sA, sA.n)])
            if debug and t == 0:
                p.op("gpsimd", lambda e, xqT=xqT: e.dma_start(
                    out=dbg_xqt, in_=xqT[:]),
                     waits=[(sA, sA.n)], incs=[(sG, 16)])
                p.op("vector", lambda e: e.tensor_copy(
                    out=st(C_S1), in_=st(C_S1)), waits=[(sG, sG.n)],
                    incs=[(sV, 1)])
            for ot in range(2):
                for kb in range(16):
                    p.op("tensor", lambda e, ot=ot, kb=kb, xqT=xqT,
                         wkvT=wkvT: e.matmul(
                        pKV[:, ot * 512:(ot + 1) * 512],
                        bass.AP(xqT, kb * 128, [[D, 128], [1, 128]]),
                        bass.AP(wkvT, kb * 1024 + ot * 512,
                                [[16 * 1024, 128], [1, 512]]),
                        start=(kb == 0), stop=(kb == 15),
                        skip_group_check=True),
                        waits=[(sA, sA.n)] if (ot == 0 and kb == 0) else (),
                        incs=[(sP, 1)] if (ot == 1 and kb == 15) else ())
            p.op("scalar", lambda e, t=t: e.activation(
                out=kvd[:], in_=pKV[:], func=ACT.Copy,
                scale=asc_kv[:, t:t + 1]), waits=[(sP, sP.n)], incs=[(sA, 1)])
            dq_wait = [(sA, sA.n)]
            if t == 0:
                dq_wait.append((sG, ws3_rep1))
            p.op("vector", lambda e: e.tensor_tensor(
                out=kvd[:], in0=kvd[:], in1=ws3x[:, 2048:3072],
                op=ALU.mult), waits=dq_wait, incs=[(sV, 1)])
            if debug and t == 0:
                p.op("gpsimd", lambda e, kvd=kvd: e.dma_start(
                    out=dbg_kvd, in_=kvd[:]),
                     waits=[(sV, sV.n)], incs=[(sG, 16)])
            p.op("scalar", lambda e: e.activation(
                out=krm[:], in_=kvd[:, 0:KVD], func=ACT.Square),
                waits=[(sV, sV.n)], incs=[(sA, 1)])
            p.op("vector", lambda e: e.tensor_reduce(
                out=stats[:, 16:20],
                in_=bass.AP(krm, 0, [[KVD, 128], [HD, KVH], [1, HD]]),
                axis=AX.X, op=ALU.add), waits=[(sA, sA.n)], incs=[(sV, 1)])
            p.op("scalar", lambda e: e.activation(
                out=stats[:, 20:24], in_=stats[:, 16:20], func=ACT.Sqrt,
                scale=1.0 / HD, bias=RMSEPS), waits=[(sV, sV.n)],
                incs=[(sA, 1)])
            p.op("vector", lambda e: e.reciprocal(
                out=stats[:, 20:24], in_=stats[:, 20:24]),
                waits=[(sA, sA.n)], incs=[(sV, 1)])
            p.op("vector", lambda e: e.tensor_tensor(
                out=krm[:], in0=kvd[:, 0:KVD],
                in1=bass.AP(stats, 20, [[64, 128], [1, KVH], [0, HD]]),
                op=ALU.mult), waits=[(sV, sV.n)])
            rope(krm, kro, ksc, rp, KVH)
            for kh in range(KVH):
                p.op("gpsimd", lambda e, t=t, kh=kh: e.dma_start(
                    out=K_scr[kh, t * 128:(t + 1) * 128, :],
                    in_=kro[:, kh * HD:(kh + 1) * HD]),
                    waits=[(sV, sV.n)] if kh == 0 else (), incs=[(sG, 16)])
                p.op("gpsimd", lambda e, t=t, kh=kh: e.dma_start(
                    out=V_scr[kh, t * 128:(t + 1) * 128, :],
                    in_=kvd[:, KVD + kh * HD:KVD + (kh + 1) * HD]),
                    incs=[(sG, 16)])
            p.op("vector", lambda e: e.tensor_copy(
                out=st(C_S1), in_=st(C_S1)), waits=[(sG, sG.n)],
                incs=[(sV, 1)])

    phKV.close()

    with contextlib.ExitStack() as ph:
        wqT = sbg(ph, "wqT", [128, 16 * 2048], BF16)
        with contextlib.ExitStack() as phw:
            wt = [sbg(phw, "wt0b", [128, D]), sbg(phw, "wt1b", [128, D])]
            wtmp = sbg(phw, "wtmpb", [128, D])
            wqb = sbg(phw, "wqbb", [128, D])
            tpw = psg(phw, "tpwb", [128, 2048])
            for r in range(16):
                buf = wt[r % 2]
                p.op("sync", lambda e, r=r, buf=buf: e.dma_start(
                    out=buf[:], in_=w_qkv[r * 128:(r + 1) * 128, :]),
                    waits=[(sV, sV.n)] if r >= 2 else
                    [(sP, sP.n), (sG, sG.n), (sA, sA.n)], incs=[(sL, 16)])
                quant_w_tile(buf, wtmp, wqb, ws3_dram, r * 128, 3072)
                transposeN(wqb, 0, tpw,
                           bass.AP(wqT, r * 128,
                                   [[16 * 2048, 128], [2048, 16], [1, 128]]),
                           16, src_wait=[(sV, sV.n), (sA, sA.n)])

        barrier()
        p.op("gpsimd", lambda e: e.dma_start(
            out=ws3x[:, 0:2048],
            in_=bass.AP(ws3_dram.tensor, 0,
                        [[3072, 1], [0, 128], [1, 2048]])),
            waits=[(sG, sG.n)], incs=[(sG, 16)])
        ws3_rep2 = sG.n

        xt = [sbg(ph, "xq0", [128, D]), sbg(ph, "xq1", [128, D])]
        rop = [sbg(ph, "ropq0", [128, 2 * HD]),
               sbg(ph, "ropq1", [128, 2 * HD])]
        tmpa = sbg(ph, "tmpaq", [128, D])
        xqb = sbg(ph, "xqbq", [128, D])
        xqT = sbg(ph, "xqTq", [128, D], BF16)
        qd = sbg(ph, "qd", [128, D])
        qn = sbg(ph, "qn", [128, D])
        qro = sbg(ph, "qro", [128, D])
        qsc = sbg(ph, "qsc", [128, H * 64])
        tpx = psg(ph, "tpxq", [128, 2048])
        pQ = psg(ph, "pQ", [128, 2048])
        for t in range(NT):
            xb, rp = xt[t % 2], rop[t % 2]
            p.op("sync", lambda e, t=t, xb=xb: e.dma_start(
                out=xb[:], in_=x_q[t * 128:(t + 1) * 128, :]),
                waits=[(sV, sV.n)] if t >= 2 else [(sV, sV.n), (sA, sA.n)],
                incs=[(sL, 16)])
            p.op("sync", lambda e, t=t, rp=rp: e.dma_start(
                out=rp[:, 0:HD], in_=cos_q[t * 128:(t + 1) * 128, :]),
                incs=[(sL, 16)])
            p.op("sync", lambda e, t=t, rp=rp: e.dma_start(
                out=rp[:, HD:2 * HD], in_=sin_q[t * 128:(t + 1) * 128, :]),
                incs=[(sL, 16)])
            ln_quant(xb, tmpa, xqb, asc_q[:, t:t + 1],
                     extra_wait=[(sL, sL.n)])
            transposeN(xqb, 0, tpx, bass.AP(xqT, 0, [[D, 128], [1, D]]), 16,
                       src_wait=[(sV, sV.n), (sA, sA.n)])
            for ot in range(4):
                for kb in range(16):
                    p.op("tensor", lambda e, ot=ot, kb=kb: e.matmul(
                        pQ[:, ot * 512:(ot + 1) * 512],
                        bass.AP(xqT, kb * 128, [[D, 128], [1, 128]]),
                        bass.AP(wqT, kb * 2048 + ot * 512,
                                [[16 * 2048, 128], [1, 512]]),
                        start=(kb == 0), stop=(kb == 15),
                        skip_group_check=True),
                        waits=[(sA, sA.n)] if (ot == 0 and kb == 0) else (),
                        incs=[(sP, 1)] if (ot == 3 and kb == 15) else ())
            p.op("scalar", lambda e, t=t: e.activation(
                out=qd[:], in_=pQ[:], func=ACT.Copy,
                scale=asc_q[:, t:t + 1]), waits=[(sP, sP.n)], incs=[(sA, 1)])
            dq_wait = [(sA, sA.n)]
            if t == 0:
                dq_wait.append((sG, ws3_rep2))
            p.op("vector", lambda e: e.tensor_tensor(
                out=qd[:], in0=qd[:], in1=ws3x[:, 0:2048], op=ALU.mult),
                waits=dq_wait, incs=[(sV, 1)])
            p.op("scalar", lambda e: e.activation(
                out=tmpa[:], in_=qd[:], func=ACT.Square),
                waits=[(sV, sV.n)], incs=[(sA, 1)])
            p.op("vector", lambda e: e.tensor_reduce(
                out=stats[:, 16:32],
                in_=bass.AP(tmpa, 0, [[D, 128], [HD, H], [1, HD]]),
                axis=AX.X, op=ALU.add), waits=[(sA, sA.n)], incs=[(sV, 1)])
            p.op("scalar", lambda e: e.activation(
                out=stats[:, 32:48], in_=stats[:, 16:32], func=ACT.Sqrt,
                scale=1.0 / HD, bias=RMSEPS), waits=[(sV, sV.n)],
                incs=[(sA, 1)])
            p.op("vector", lambda e: e.reciprocal(
                out=stats[:, 32:48], in_=stats[:, 32:48]),
                waits=[(sA, sA.n)], incs=[(sV, 1)])
            p.op("vector", lambda e: e.tensor_tensor(
                out=stats[:, 32:48], in0=stats[:, 32:48],
                in1=qg_s[:], op=ALU.mult), waits=[(sV, sV.n)],
                incs=[(sV, 1)])
            p.op("vector", lambda e: e.tensor_tensor(
                out=qn[:], in0=qd[:],
                in1=bass.AP(stats, 32, [[64, 128], [1, H], [0, HD]]),
                op=ALU.mult), waits=[(sV, sV.n)])
            rope(qn, qro, qsc, rp, H)
            for h in range(H):
                p.op("gpsimd", lambda e, t=t, h=h: e.dma_start(
                    out=Q_scr[h, t * 128:(t + 1) * 128, :],
                    in_=qro[:, h * HD:(h + 1) * HD]),
                    waits=[(sV, sV.n)] if h == 0 else (), incs=[(sG, 16)])
            p.op("vector", lambda e: e.tensor_copy(
                out=st(C_S1), in_=st(C_S1)), waits=[(sG, sG.n)],
                incs=[(sV, 1)])

    if debug:
        p.op("gpsimd", lambda e: e.dma_start(out=dbg_ws, in_=ws3x[:]),
             waits=[(sV, sV.n), (sA, sA.n)], incs=[(sG, 16)])
    barrier()
    if stop_early:
        p.op("gpsimd", lambda e: e.dma_start(
            out=bass.AP(ws3_dram.tensor, 0, [[3072, 1], [1, 16]]),
            in_=stats[0:1, 0:16]), waits=[(sG, sG.n)], incs=[(sG, 16)])
        phKV.close()
        phWS.close()
        with nc.Block() as block:
            @block.sync
            def _(eng):
                p.replay("sync", eng)

            @block.gpsimd
            def _(eng):
                p.replay("gpsimd", eng)

            @block.tensor
            def _(eng):
                p.replay("tensor", eng)

            @block.scalar
            def _(eng):
                p.replay("scalar", eng)

            @block.vector
            def _(eng):
                p.replay("vector", eng)
        es.close()
        return nc
    barrier()
    phWS.close()

    with contextlib.ExitStack() as ph:
        dmask_s = sbg(ph, "dmask_s", [128, 512])
        p.op("sync", lambda e: e.dma_start(out=dmask_s[:], in_=dmask),
             waits=[(sV, sV.n), (sP, sP.n), (sG, sG.n)], incs=[(sL, 16)])
        qsb = sbg(ph, "qsb", [128, NT * HD])
        ksb = [sbg(ph, "ksb0", [128, 512]), sbg(ph, "ksb1", [128, 512])]
        vsb = [sbg(ph, "vsb0", [128, 512]), sbg(ph, "vsb1", [128, 512])]
        QT_hi = sbg(ph, "QT_hi", [128, TPC], BF16)
        QT_lo = sbg(ph, "QT_lo", [128, TPC], BF16)
        KT_hi = sbg(ph, "KT_hi", [128, 512], BF16)
        KT_lo = sbg(ph, "KT_lo", [128, 512], BF16)
        V_hi = sbg(ph, "V_hi", [128, 512], BF16)
        V_lo = sbg(ph, "V_lo", [128, 512], BF16)
        exs = [sbg(ph, "exs0", [128, 512]), sbg(ph, "exs1", [128, 512])]
        eT_hi = sbg(ph, "eT_hi", [128, 4 * TPC], BF16)
        eT_lo = sbg(ph, "eT_lo", [128, 4 * TPC], BF16)
        rsum = sbg(ph, "rsum", [128, 64])
        rtot = sbg(ph, "rtot", [128, NT])
        rrec = sbg(ph, "rrec", [128, NT])
        ysb = sbg(ph, "ysb", [128, TPC])
        pT1 = psg(ph, "pT1", [128, 1024])
        pE = psg(ph, "pE", [128, 512])
        pS = psg(ph, "pS", [128, 1024])
        pY = psg(ph, "pY", [128, 1024])

        for h in range(H):
            kh = h // (H // KVH)
            qw = [(sP, sP.n), (sA, sA.n)]
            if h == 0:
                qw.append((sG, sG.n))
            p.op("sync", lambda e, h=h: e.dma_start(
                out=qsb[:],
                in_=Q_scr[h].rearrange("(a p) d -> p a d", p=128)),
                waits=qw, incs=[(sL, 16)])
            for g in range(2):
                transposeN(qsb, g * 512, pS,
                           bass.AP(QT_hi, g * 512, [[TPC, 128], [1, 512]]),
                           4, src_wait=[(sL, sL.n), (sV, sV.n), (sA, sA.n)])
                p.op("vector", lambda e, g=g: e.scalar_tensor_tensor(
                    out=bass.AP(QT_lo, g * 512, [[TPC, 128], [1, 512]]),
                    in0=pS[:, 0:512], scalar=1.0,
                    in1=bass.AP(QT_hi, g * 512, [[TPC, 128], [1, 512]]),
                    op0=ALU.mult, op1=ALU.subtract),
                    waits=[(sA, sA.n)], incs=[(sV, 1)])
            av_done_pe = sP.n
            for c in range(NCH):
                kb_, vb_ = ksb[c % 2], vsb[c % 2]
                p.op("sync", lambda e, c=c, kh=kh, kb_=kb_: e.dma_start(
                    out=kb_[:],
                    in_=K_scr[kh, c * 512:(c + 1) * 512, :].rearrange(
                        "(a p) d -> p a d", p=128)),
                    waits=[(sP, av_done_pe)], incs=[(sL, 16)])
                p.op("sync", lambda e, c=c, kh=kh, vb_=vb_: e.dma_start(
                    out=vb_[:],
                    in_=V_scr[kh, c * 512:(c + 1) * 512, :].rearrange(
                        "(a p) d -> p a d", p=128)),
                    incs=[(sL, 16)])
                transposeN(kb_, 0, pT1,
                           bass.AP(KT_hi, 0, [[512, 128], [1, 512]]), 4,
                           src_wait=[(sL, sL.n), (sV, sV.n), (sA, sA.n)])
                p.op("vector", lambda e: e.scalar_tensor_tensor(
                    out=KT_lo[:], in0=pT1[:, 0:512], scalar=1.0,
                    in1=KT_hi[:], op0=ALU.mult, op1=ALU.subtract),
                    waits=[(sA, sA.n)], incs=[(sV, 1)])
                p.op("scalar", lambda e, vb_=vb_: e.activation(
                    out=V_hi[:], in_=vb_[:], func=ACT.Copy),
                    waits=[(sL, sL.n), (sP, av_done_pe)], incs=[(sA, 1)])
                p.op("vector", lambda e, vb_=vb_: e.scalar_tensor_tensor(
                    out=V_lo[:], in0=vb_[:], scalar=1.0, in1=V_hi[:],
                    op0=ALU.mult, op1=ALU.subtract),
                    waits=[(sA, sA.n)], incs=[(sV, 1)])
                exp_done = {}
                for i in range(c, NT):
                    sl = pS[:, (i % 2) * 512:(i % 2) * 512 + 512]
                    tpe = pT1[:, 512:1024] if i % 2 == 0 else pE[:, 0:512]
                    wt_ = []
                    if i <= c + 1:
                        wt_ += [(sA, sA.n), (sV, sV.n)]
                    elif (i - 2) in exp_done:
                        wt_ += [(sA, exp_done[i - 2][0]),
                                (sV, exp_done[i - 2][1])]
                    for pi, (lt, rt) in enumerate(
                            ((QT_hi, KT_hi), (QT_lo, KT_hi), (QT_hi, KT_lo))):
                        p.op("tensor", lambda e, sl=sl, lt=lt, rt=rt, i=i,
                             pi=pi: e.matmul(
                                 sl,
                                 bass.AP(lt, i * 128, [[TPC, 128], [1, 128]]),
                                 rt[:], start=(pi == 0), stop=(pi == 2),
                                 skip_group_check=True),
                             waits=wt_ if pi == 0 else (),
                             incs=[(sP, 1)] if pi == 2 else ())
                    if i == c:
                        p.op("vector", lambda e, sl=sl: e.scalar_tensor_tensor(
                            out=sl, in0=dmask_s[:], scalar=NEGBIG, in1=sl,
                            op0=ALU.mult, op1=ALU.add),
                            waits=[(sP, sP.n)], incs=[(sV, 1)])
                        mwait = [(sV, sV.n)]
                    else:
                        mwait = [(sP, sP.n)]
                    eb = exs[i % 2]
                    p.op("scalar", lambda e, sl=sl, eb=eb, i=i, c=c:
                         e.activation(
                             out=eb[:], in_=sl, func=ACT.Exp, scale=SMSCALE,
                             accum_out=rsum[:, i * 8 + c:i * 8 + c + 1]),
                         waits=mwait, incs=[(sA, 1)])
                    twait = [(sA, sA.n)]
                    if i <= c + 1:
                        twait.append((sV, sV.n))
                    elif (i - 2) in exp_done:
                        twait.append((sV, exp_done[i - 2][1]))
                    for s_ in range(4):
                        p.op("tensor", lambda e, tpe=tpe, eb=eb, s_=s_:
                             e.transpose(tpe[:, s_ * 128:(s_ + 1) * 128],
                                         eb[:, s_ * 128:(s_ + 1) * 128],
                                         eyef_s[:]),
                             waits=twait if s_ == 0 else (),
                             incs=[(sP, 1)] if s_ == 3 else ())
                    dst_hi = bass.AP(eT_hi, i * 128,
                                     [[4 * TPC, 128], [TPC, 4], [1, 128]])
                    dst_lo = bass.AP(eT_lo, i * 128,
                                     [[4 * TPC, 128], [TPC, 4], [1, 128]])
                    p.op("scalar", lambda e, tpe=tpe, dst_hi=dst_hi:
                         e.activation(out=dst_hi, in_=tpe[:, 0:512],
                                      func=ACT.Copy),
                         waits=[(sP, sP.n)], incs=[(sA, 1)])
                    p.op("vector", lambda e, tpe=tpe, dst_hi=dst_hi,
                         dst_lo=dst_lo: e.scalar_tensor_tensor(
                             out=dst_lo, in0=tpe[:, 0:512], scalar=1.0,
                             in1=dst_hi, op0=ALU.mult, op1=ALU.subtract),
                         waits=[(sA, sA.n)], incs=[(sV, 1)])
                    exp_done[i] = (sA.n, sV.n)
                avw = [(sA, sA.n), (sV, sV.n)]
                mm_list = []
                for s_ in range(4):
                    for (vt, et) in ((V_hi, eT_hi), (V_lo, eT_hi),
                                     (V_hi, eT_lo)):
                        segs = ([(c * 128, 512)] if c * 128 < 512 else []) + \
                               [(max(512, c * 128), 1024)]
                        for (a0, a1) in segs:
                            mm_list.append((s_, vt, et, a0, a1))
                for n_, (s_, vt, et, a0, a1) in enumerate(mm_list):
                    st_flag = (c == 0 and n_ < 2)
                    last = (n_ == len(mm_list) - 1)
                    p.op("tensor", lambda e, s_=s_, vt=vt, et=et, a0=a0,
                         a1=a1, st_flag=st_flag: e.matmul(
                             pY[:, a0:a1], vt[:, s_ * 128:(s_ + 1) * 128],
                             bass.AP(et, s_ * TPC + a0,
                                     [[4 * TPC, 128], [1, a1 - a0]]),
                             start=st_flag, stop=False,
                             skip_group_check=True),
                         waits=avw if n_ == 0 else (),
                         incs=[(sP, 1)] if last else ())
                av_done_pe = sP.n
            for i in range(NT):
                p.op("vector", lambda e, i=i: e.tensor_reduce(
                    out=rtot[:, i:i + 1],
                    in_=rsum[:, i * 8:i * 8 + i + 1], axis=AX.X, op=ALU.add),
                    waits=[(sA, sA.n)] if i == 0 else (),
                    incs=[(sV, 1)] if i == NT - 1 else ())
            p.op("vector", lambda e: e.reciprocal(out=rrec[:], in_=rtot[:]),
                 waits=[(sV, sV.n)], incs=[(sV, 1)])
            p.op("scalar", lambda e: e.activation(
                out=ysb[:, 0:512], in_=pY[:, 0:512], func=ACT.Copy),
                waits=[(sP, av_done_pe)])
            p.op("scalar", lambda e: e.activation(
                out=ysb[:, 512:1024], in_=pY[:, 512:1024], func=ACT.Copy),
                incs=[(sA, 1)])
            for i in range(NT):
                p.op("tensor", lambda e, i=i: e.transpose(
                    pS[:, i * 128:(i + 1) * 128],
                    ysb[:, i * 128:(i + 1) * 128], eyef_s[:]),
                    waits=[(sA, sA.n)] if i == 0 else (),
                    incs=[(sP, 1)] if i == NT - 1 else ())
            for i in range(NT):
                p.op("scalar", lambda e, i=i, h=h: e.activation(
                    out=yx[:, i * D + h * HD:i * D + h * HD + HD],
                    in_=pS[:, i * 128:(i + 1) * 128], func=ACT.Copy,
                    scale=rrec[:, i:i + 1]),
                    waits=[(sP, sP.n), (sV, sV.n)] if i == 0 else (),
                    incs=[(sA, 1)] if i == NT - 1 else ())

    if debug:
        for t in range(NT):
            p.op("gpsimd", lambda e, t=t: e.dma_start(
                out=dbg_y[t * 128:(t + 1) * 128, :],
                in_=yx[:, t * D:(t + 1) * D]),
                waits=[(sA, sA.n)] if t == 0 else (), incs=[(sG, 16)])

    barrier()

    with contextlib.ExitStack() as ph:
        wsox = sbg(ph, "wsox", [128, D])
        woqT = sbg(ph, "woqT", [128, 16 * 2048], BF16)
        x0b = [sbg(ph, "x0b0", [128, D]), sbg(ph, "x0b1", [128, D])]
        yq = sbg(ph, "yq", [128, D])
        yqT = sbg(ph, "yqT", [128, D], BF16)
        od = sbg(ph, "od", [128, D])
        tpw = psg(ph, "tpwc", [128, 2048])
        pO = psg(ph, "pO", [128, 2048])
        with contextlib.ExitStack() as phw:
            wt = [sbg(phw, "wt0c", [128, D]), sbg(phw, "wt1c", [128, D])]
            wtmp = sbg(phw, "wtmpc", [128, D])
            wqb = sbg(phw, "wqbc", [128, D])
            for r in range(16):
                buf = wt[r % 2]
                p.op("sync", lambda e, r=r, buf=buf: e.dma_start(
                    out=buf[:], in_=w_o[r * 128:(r + 1) * 128, :]),
                    waits=[(sV, sV.n)] if r >= 2 else
                    [(sP, sP.n), (sA, sA.n)], incs=[(sL, 16)])
                quant_w_tile(buf, wtmp, wqb, wso_dram, r * 128, 2048)
                transposeN(wqb, 0, tpw,
                           bass.AP(woqT, r * 128,
                                   [[16 * 2048, 128], [2048, 16], [1, 128]]),
                           16, src_wait=[(sV, sV.n), (sA, sA.n)])
        barrier()
        p.op("gpsimd", lambda e: e.dma_start(
            out=wsox[:],
            in_=bass.AP(wso_dram.tensor, 0, [[2048, 1], [0, 128], [1, 2048]])),
            waits=[(sG, sG.n)], incs=[(sG, 16)])
        wso_rep = sG.n
        for t in range(NT):
            xb = x0b[t % 2]
            p.op("sync", lambda e, t=t, xb=xb: e.dma_start(
                out=xb[:], in_=x_q[t * 128:(t + 1) * 128, :]),
                waits=[(sV, sV.n)] if t >= 2 else [(sV, sV.n), (sA, sA.n)],
                incs=[(sL, 16)])
            ytile = yx[:, t * D:(t + 1) * D]
            p.op("vector", lambda e, ytile=ytile: e.reduce_max(
                out=st(C_AM), in_=ytile, axis=AX.X,
                apply_absolute_value=True), waits=[(sA, sA.n)],
                incs=[(sV, 1)])
            p.op("vector", lambda e: e.tensor_scalar(
                out=st(C_QS), in0=st(C_AM), scalar1=1.0 / 127.0, scalar2=QEPS,
                op0=ALU.mult, op1=ALU.max), waits=[(sV, sV.n)],
                incs=[(sV, 1)])
            p.op("vector", lambda e, t=t: e.tensor_copy(
                out=asc_y[:, t:t + 1], in_=st(C_QS)), waits=[(sV, sV.n)])
            p.op("vector", lambda e: e.reciprocal(out=st(C_RA), in_=st(C_QS)),
                 incs=[(sV, 1)])
            p.op("scalar", lambda e, ytile=ytile: e.activation(
                out=yq[:], in_=ytile, func=ACT.Copy, scale=st(C_RA),
                bias=RND), waits=[(sV, sV.n)], incs=[(sA, 1)])
            p.op("scalar", lambda e: e.activation(
                out=od[:], in_=yq[:], func=ACT.Copy, bias=-RND),
                waits=[(sA, sA.n)], incs=[(sA, 1)])
            p.op("vector", lambda e: e.tensor_copy(
                out=st(C_S1), in_=st(C_S1)), waits=[(sA, sA.n)],
                incs=[(sV, 1)])
            transposeN(od, 0, tpw, bass.AP(yqT, 0, [[D, 128], [1, D]]), 16,
                       src_wait=[(sV, sV.n), (sA, sA.n)])
            for ot in range(4):
                for kb in range(16):
                    p.op("tensor", lambda e, ot=ot, kb=kb: e.matmul(
                        pO[:, ot * 512:(ot + 1) * 512],
                        bass.AP(yqT, kb * 128, [[D, 128], [1, 128]]),
                        bass.AP(woqT, kb * 2048 + ot * 512,
                                [[16 * 2048, 128], [1, 512]]),
                        start=(kb == 0), stop=(kb == 15),
                        skip_group_check=True),
                        waits=[(sA, sA.n)] if (ot == 0 and kb == 0) else (),
                        incs=[(sP, 1)] if (ot == 3 and kb == 15) else ())
            p.op("scalar", lambda e, t=t: e.activation(
                out=od[:], in_=pO[:], func=ACT.Copy,
                scale=asc_y[:, t:t + 1]), waits=[(sP, sP.n)], incs=[(sA, 1)])
            dq_wait = [(sA, sA.n), (sL, sL.n)]
            if t == 0:
                dq_wait.append((sG, wso_rep))
            p.op("vector", lambda e: e.tensor_tensor(
                out=od[:], in0=od[:], in1=wsox[:], op=ALU.mult),
                waits=dq_wait)
            p.op("vector", lambda e, ytile=ytile, xb=xb: e.tensor_tensor(
                out=ytile, in0=od[:], in1=xb[:], op=ALU.add), incs=[(sV, 1)])

    barrier()

    with contextlib.ExitStack() as ph:
        rw1h = sbg(ph, "rw1h", [128, 16 * 64], BF16)
        rw1l = sbg(ph, "rw1l", [128, 16 * 64], BF16)
        rw2s = sbg(ph, "rw2s", [64, 32])
        rbg = sbg(ph, "rbg", [128, 128])
        bvs = sbg(ph, "bvs", [128, 512])
        ath = sbg(ph, "ath", [128, 16 * 256], BF16)
        atl = sbg(ph, "atl", [128, 16 * 256], BF16)
        bmh = sbg(ph, "bmh", [128, 2 * D], BF16)
        bml = sbg(ph, "bml", [128, 2 * D], BF16)
        x2Th = sbg(ph, "x2Th", [128, 16 * 1024], BF16)
        x2Tl = sbg(ph, "x2Tl", [128, 16 * 1024], BF16)
        xn2 = sbg(ph, "xn2", [128, D])
        hsb = sbg(ph, "hsb", [128, 64])
        hT = sbg(ph, "hT", [64, 128])
        lsb = sbg(ph, "lsb", [128, 32])
        lg = sbg(ph, "lg", [128, 16])
        eq1 = sbg(ph, "eq1", [128, 16])
        eq2 = sbg(ph, "eq2", [128, 16])
        lgm = sbg(ph, "lgm", [128, 16])
        cf1 = sbg(ph, "cf1", [128, 16])
        coef = sbg(ph, "coef", [128, NT * 16])
        wxa = sbg(ph, "wxa", [128, 256])
        wxTh = sbg(ph, "wxTh", [128, NT * 256], BF16)
        wxTl = sbg(ph, "wxTl", [128, NT * 256], BF16)
        wmb = [sbg(ph, "wmb0", [128, 512]), sbg(ph, "wmb1", [128, 512])]
        wmh = [sbg(ph, "wmh0", [128, 512], BF16),
               sbg(ph, "wmh1", [128, 512], BF16)]
        wml = [sbg(ph, "wml0", [128, 512], BF16),
               sbg(ph, "wml1", [128, 512], BF16)]
        osb = [sbg(ph, "osb0", [128, 512]), sbg(ph, "osb1", [128, 512])]

        with contextlib.ExitStack() as phs:
            srcb = sbg(phs, "srcb", [128, 2048])
            p.op("sync", lambda e: e.dma_start(out=rw2s[:], in_=rw2_T),
                 waits=[(sV, sV.n), (sP, sP.n), (sA, sA.n)], incs=[(sL, 16)])
            p.op("sync", lambda e: e.dma_start(out=rbg[:, 0:64], in_=rb1),
                 incs=[(sL, 16)])
            p.op("sync", lambda e: e.dma_start(out=rbg[:, 64:96], in_=rb2),
                 incs=[(sL, 16)])
            for (srcdram, dd, na, ca, hi_, lo_) in (
                    (rw1_T, 64, 16, 16, rw1h, rw1l),
                    (A_T, 256, 16, 8, ath, atl),
                    (B_mov, 2048, 2, 1, bmh, bml)):
                for a0 in range(0, na, ca):
                    ncols = ca * dd
                    p.op("sync", lambda e, srcdram=srcdram, dd=dd, a0=a0,
                         ca=ca, ncols=ncols: e.dma_start(
                             out=srcb[:, 0:ncols],
                             in_=bass.AP(srcdram.tensor, a0 * 128 * dd,
                                         [[dd, 128], [128 * dd, ca],
                                          [1, dd]])),
                         waits=[(sV, sV.n)], incs=[(sL, 16)])
                    p.op("scalar", lambda e, ncols=ncols, hi_=hi_, a0=a0,
                         dd=dd: e.activation(
                             out=hi_[:, a0 * dd:a0 * dd + ncols],
                             in_=srcb[:, 0:ncols], func=ACT.Copy),
                         waits=[(sL, sL.n)], incs=[(sA, 1)])
                    p.op("vector", lambda e, ncols=ncols, hi_=hi_, lo_=lo_,
                         a0=a0, dd=dd: e.scalar_tensor_tensor(
                             out=lo_[:, a0 * dd:a0 * dd + ncols],
                             in0=srcb[:, 0:ncols], scalar=1.0,
                             in1=hi_[:, a0 * dd:a0 * dd + ncols],
                             op0=ALU.mult, op1=ALU.subtract),
                         waits=[(sA, sA.n)], incs=[(sV, 1)])

        with contextlib.ExitStack() as php:
            tpf = psg(php, "tpf", [128, 2048])
            pR = psg(php, "pR", [128, 512])
            for t in range(NT):
                x1t = yx[:, t * D:(t + 1) * D]
                p.op("vector", lambda e, x1t=x1t: e.tensor_reduce(
                    out=st(C_S1), in_=x1t, axis=AX.X, op=ALU.add),
                    incs=[(sV, 1)])
                p.op("scalar", lambda e, x1t=x1t: e.activation(
                    out=xn2[:], in_=x1t, func=ACT.Square,
                    accum_out=st(C_S2)), waits=[(sV, sV.n)])
                p.op("scalar", lambda e: e.activation(
                    out=st(C_MU), in_=st(C_S1), func=ACT.Copy,
                    scale=1.0 / D), waits=[(sV, sV.n)], incs=[(sA, 1)])
                p.op("scalar", lambda e: e.activation(
                    out=st(C_MU2), in_=st(C_MU), func=ACT.Square),
                    waits=[(sA, sA.n)], incs=[(sA, 1)])
                p.op("vector", lambda e: e.tensor_scalar(
                    out=st(C_VAR), in0=st(C_S2), scalar1=1.0 / D,
                    scalar2=st(C_MU2), op0=ALU.mult, op1=ALU.subtract),
                    waits=[(sA, sA.n)], incs=[(sV, 1)])
                p.op("scalar", lambda e: e.activation(
                    out=st(C_RSTD), in_=st(C_VAR), func=ACT.Sqrt,
                    bias=LNEPS), waits=[(sV, sV.n)], incs=[(sA, 1)])
                p.op("vector", lambda e: e.reciprocal(
                    out=st(C_RSTD), in_=st(C_RSTD)), waits=[(sA, sA.n)],
                    incs=[(sV, 1)])
                p.op("vector", lambda e, x1t=x1t: e.tensor_scalar(
                    out=xn2[:], in0=x1t, scalar1=st(C_MU),
                    scalar2=st(C_RSTD), op0=ALU.subtract, op1=ALU.mult),
                    waits=[(sV, sV.n)], incs=[(sV, 1)])
                transposeN(xn2, 0, tpf,
                           bass.AP(x2Th, t * 128,
                                   [[16 * 1024, 128], [1024, 16], [1, 128]]),
                           16, src_wait=[(sV, sV.n), (sA, sA.n)])
                p.op("vector", lambda e, t=t: e.scalar_tensor_tensor(
                    out=bass.AP(x2Tl, t * 128,
                                [[16 * 1024, 128], [1024, 16], [1, 128]]),
                    in0=tpf[:, 0:2048], scalar=1.0,
                    in1=bass.AP(x2Th, t * 128,
                                [[16 * 1024, 128], [1024, 16], [1, 128]]),
                    op0=ALU.mult, op1=ALU.subtract),
                    waits=[(sA, sA.n)], incs=[(sV, 1)])
                for pi, (lt, rt) in enumerate(((x2Th, rw1h), (x2Tl, rw1h),
                                               (x2Th, rw1l))):
                    for kb in range(16):
                        p.op("tensor", lambda e, pi=pi, lt=lt, rt=rt, kb=kb,
                             t=t: e.matmul(
                                 pR[:, 0:64],
                                 bass.AP(lt, kb * 1024 + t * 128,
                                         [[16 * 1024, 128], [1, 128]]),
                                 bass.AP(rt, kb * 64,
                                         [[16 * 64, 128], [1, 64]]),
                                 start=(pi == 0 and kb == 0),
                                 stop=(pi == 2 and kb == 15),
                                 skip_group_check=True),
                             waits=[(sV, sV.n)] if (pi == 0 and kb == 0)
                             else (),
                             incs=[(sP, 1)] if (pi == 2 and kb == 15)
                             else ())
                p.op("vector", lambda e: e.scalar_tensor_tensor(
                    out=pR[:, 0:64], in0=rbg[:, 0:64], scalar=1.0,
                    in1=pR[:, 0:64], op0=ALU.mult, op1=ALU.add),
                    waits=[(sP, sP.n)], incs=[(sV, 1)])
                p.op("scalar", lambda e: e.activation(
                    out=hsb[:], in_=pR[:, 0:64], func=ACT.Silu),
                    waits=[(sV, sV.n)], incs=[(sA, 1)])
                p.op("tensor", lambda e: e.transpose(
                    tpf[0:64, 0:128], hsb[:], eyef_s[:]),
                    waits=[(sA, sA.n)], incs=[(sP, 1)])
                p.op("scalar", lambda e: e.activation(
                    out=hT[:], in_=tpf[0:64, 0:128], func=ACT.Copy),
                    waits=[(sP, sP.n)], incs=[(sA, 1)])
                p.op("tensor", lambda e: e.matmul(
                    pR[:, 128:160], hT[:], rw2s[:], start=True, stop=True,
                    skip_group_check=True), waits=[(sA, sA.n)],
                    incs=[(sP, 1)])
                p.op("vector", lambda e: e.scalar_tensor_tensor(
                    out=lsb[:], in0=rbg[:, 64:96], scalar=1.0,
                    in1=pR[:, 128:160], op0=ALU.mult, op1=ALU.add),
                    waits=[(sP, sP.n)])
                p.op("vector", lambda e: e.tensor_tensor(
                    out=lg[:], in0=lsb[:, 0:16], in1=gt_s[:],
                    op=ALU.add))
                p.op("vector", lambda e: e.reduce_max(
                    out=st(C_M1), in_=lg[:], axis=AX.X), incs=[(sV, 1)])
                p.op("vector", lambda e: e.tensor_scalar(
                    out=eq1[:], in0=lg[:], scalar1=st(C_M1), scalar2=None,
                    op0=ALU.is_equal), waits=[(sV, sV.n)], incs=[(sV, 1)])
                p.op("vector", lambda e: e.scalar_tensor_tensor(
                    out=lgm[:], in0=eq1[:], scalar=NEGBIG, in1=lg[:],
                    op0=ALU.mult, op1=ALU.add), waits=[(sV, sV.n)],
                    incs=[(sV, 1)])
                p.op("vector", lambda e: e.reduce_max(
                    out=st(C_M2), in_=lgm[:], axis=AX.X),
                    waits=[(sV, sV.n)], incs=[(sV, 1)])
                p.op("vector", lambda e: e.tensor_scalar_mul(
                    out=st(C_NM1), in0=st(C_M1), scalar1=-1.0),
                    incs=[(sV, 1)])
                p.op("scalar", lambda e: e.activation(
                    out=st(C_E2), in_=st(C_M2), func=ACT.Exp,
                    bias=st(C_NM1)), waits=[(sV, sV.n)], incs=[(sA, 1)])
                p.op("vector", lambda e: e.tensor_scalar_add(
                    out=st(C_DEN), in0=st(C_E2), scalar1=1.0),
                    waits=[(sA, sA.n)], incs=[(sV, 1)])
                p.op("vector", lambda e: e.reciprocal(
                    out=st(C_RDEN), in_=st(C_DEN)), waits=[(sV, sV.n)],
                    incs=[(sV, 1)])
                p.op("vector", lambda e: e.tensor_scalar(
                    out=st(C_C2), in0=st(C_E2), scalar1=st(C_RDEN),
                    scalar2=None, op0=ALU.mult), waits=[(sV, sV.n)],
                    incs=[(sV, 1)])
                p.op("vector", lambda e: e.tensor_scalar(
                    out=eq2[:], in0=lg[:], scalar1=st(C_M2), scalar2=None,
                    op0=ALU.is_equal))
                p.op("vector", lambda e: e.tensor_scalar(
                    out=cf1[:], in0=eq1[:], scalar1=st(C_RDEN), scalar2=None,
                    op0=ALU.mult), incs=[(sV, 1)])
                p.op("vector", lambda e, t=t: e.scalar_tensor_tensor(
                    out=coef[:, t * 16:(t + 1) * 16], in0=eq2[:],
                    scalar=st(C_C2), in1=cf1[:], op0=ALU.mult, op1=ALU.add),
                    waits=[(sV, sV.n)], incs=[(sV, 1)])
                for pi, (lt, rt) in enumerate(((x2Th, ath), (x2Tl, ath),
                                               (x2Th, atl))):
                    for kb in range(16):
                        p.op("tensor", lambda e, pi=pi, lt=lt, rt=rt, kb=kb,
                             t=t: e.matmul(
                                 pR[:, 256:512],
                                 bass.AP(lt, kb * 1024 + t * 128,
                                         [[16 * 1024, 128], [1, 128]]),
                                 bass.AP(rt, kb * 256,
                                         [[16 * 256, 128], [1, 256]]),
                                 start=(pi == 0 and kb == 0),
                                 stop=(pi == 2 and kb == 15),
                                 skip_group_check=True),
                             waits=[(sV, sV.n)] if (pi == 0 and kb == 0)
                             else (),
                             incs=[(sP, 1)] if (pi == 2 and kb == 15)
                             else ())
                p.op("vector", lambda e, t=t: e.scalar_tensor_tensor(
                    out=wxa[:], in0=pR[:, 256:512], scalar=1.0,
                    in1=bass.AP(coef, t * 16,
                                [[NT * 16, 128], [1, 16], [0, 16]]),
                    op0=ALU.mult, op1=ALU.mult), waits=[(sP, sP.n)],
                    incs=[(sV, 1)])
                transposeN(wxa, 0, tpf,
                           bass.AP(wxTh, t * 256,
                                   [[NT * 256, 128], [128, 2], [1, 128]]),
                           2, src_wait=[(sV, sV.n), (sA, sA.n)])
                p.op("vector", lambda e, t=t: e.scalar_tensor_tensor(
                    out=bass.AP(wxTl, t * 256,
                                [[NT * 256, 128], [128, 2], [1, 128]]),
                    in0=tpf[:, 0:256], scalar=1.0,
                    in1=bass.AP(wxTh, t * 256,
                                [[NT * 256, 128], [128, 2], [1, 128]]),
                    op0=ALU.mult, op1=ALU.subtract),
                    waits=[(sA, sA.n)], incs=[(sV, 1)])

        pBig = psg(ph, "pBig", [128, 4096])
        for ot in range(4):
            evac_sA_before = sA.n
            p.op("sync", lambda e, ot=ot: e.dma_start(
                out=bvs[:], in_=biasv[:, ot * 512:(ot + 1) * 512]),
                waits=[(sV, sV.n)], incs=[(sL, 16)])
            bvs_loaded = sL.n
            for kb in range(16):
                buf, bh, bl = wmb[kb % 2], wmh[kb % 2], wml[kb % 2]
                p.op("sync", lambda e, ot=ot, kb=kb, buf=buf: e.dma_start(
                    out=buf[:], in_=W_T[kb * 128:(kb + 1) * 128,
                                        ot * 512:(ot + 1) * 512]),
                    waits=[(sV, sV.n)], incs=[(sL, 16)])
                p.op("scalar", lambda e, buf=buf, bh=bh: e.activation(
                    out=bh[:], in_=buf[:], func=ACT.Copy),
                    waits=[(sL, sL.n), (sP, sP.n)], incs=[(sA, 1)])
                p.op("vector", lambda e, buf=buf, bh=bh, bl=bl:
                     e.scalar_tensor_tensor(out=bl[:], in0=buf[:],
                                            scalar=1.0, in1=bh[:],
                                            op0=ALU.mult, op1=ALU.subtract),
                     waits=[(sA, sA.n)], incs=[(sV, 1)])
                for t in range(NT):
                    for pi, (lt, rt) in enumerate(((x2Th, bh), (x2Tl, bh),
                                                   (x2Th, bl))):
                        p.op("tensor", lambda e, t=t, pi=pi, lt=lt, rt=rt,
                             kb=kb: e.matmul(
                                 pBig[:, t * 512:(t + 1) * 512],
                                 bass.AP(lt, kb * 1024 + t * 128,
                                         [[16 * 1024, 128], [1, 128]]),
                                 rt[:], start=(pi == 0 and kb == 0),
                                 stop=False, skip_group_check=True),
                             waits=([(sV, sV.n)] +
                                    ([(sA, evac_sA_before)] if ot > 0
                                     else []))
                             if (t == 0 and pi == 0) else (),
                             incs=[(sP, 1)] if (t == NT - 1 and pi == 2)
                             else ())
            for t in range(NT):
                for pi, (lt, rt) in enumerate(((wxTh, bmh), (wxTl, bmh),
                                               (wxTh, bml))):
                    for kb2 in range(2):
                        p.op("tensor", lambda e, t=t, pi=pi, lt=lt, rt=rt,
                             kb2=kb2, ot=ot: e.matmul(
                                 pBig[:, t * 512:(t + 1) * 512],
                                 bass.AP(lt, t * 256 + kb2 * 128,
                                         [[NT * 256, 128], [1, 128]]),
                                 bass.AP(rt, kb2 * 2048 + ot * 512,
                                         [[2 * 2048, 128], [1, 512]]),
                                 start=False, stop=(pi == 2 and kb2 == 1),
                                 skip_group_check=True),
                             incs=[(sP, 1)] if (t == NT - 1 and pi == 2 and
                                                kb2 == 1) else ())
            for t in range(NT):
                ob = osb[t % 2]
                ew = [(sP, sP.n), (sG, sG.n)]
                p.op("scalar", lambda e, t=t, ob=ob: e.activation(
                    out=ob[:], in_=pBig[:, t * 512:(t + 1) * 512],
                    func=ACT.Copy), waits=ew, incs=[(sA, 1)])
                p.op("vector", lambda e, ob=ob: e.tensor_tensor(
                    out=ob[:], in0=ob[:], in1=bvs[:],
                    op=ALU.add), waits=[(sA, sA.n), (sL, bvs_loaded)])
                p.op("vector", lambda e, t=t, ot=ot, ob=ob: e.tensor_tensor(
                    out=ob[:], in0=ob[:],
                    in1=yx[:, t * D + ot * 512:t * D + ot * 512 + 512],
                    op=ALU.add), incs=[(sV, 1)])
                p.op("gpsimd", lambda e, t=t, ot=ot, ob=ob: e.dma_start(
                    out=out[t * 128:(t + 1) * 128, ot * 512:(ot + 1) * 512],
                    in_=ob[:]), waits=[(sV, sV.n)], incs=[(sG, 16)])
        p.op("gpsimd", lambda e: e.dma_start(
            out=bass.AP(ws3_dram.tensor, 0, [[3072, 1], [1, 16]]),
            in_=stats[0:1, 0:16]), waits=[(sG, sG.n)], incs=[(sG, 16)])

    with nc.Block() as block:
        @block.sync
        def _(eng):
            p.replay("sync", eng)

        @block.gpsimd
        def _(eng):
            p.replay("gpsimd", eng)

        @block.tensor
        def _(eng):
            p.replay("tensor", eng)

        @block.scalar
        def _(eng):
            p.replay("scalar", eng)

        @block.vector
        def _(eng):
            p.replay("vector", eng)

    es.close()
    return nc


def kernel(L0_attn_x, L0_attn_w_q, L0_attn_w_k, L0_attn_w_v, L0_attn_w_o,
           L0_attn_q_gain, L0_ffn_W, L0_ffn_bias, L0_ffn_rw1, L0_ffn_rb1,
           L0_ffn_rw2, L0_ffn_rb2, L0_ffn_A, L0_ffn_B, L0_ffn_gates):
    x = np.ascontiguousarray(np.asarray(L0_attn_x, dtype=np.float32))
    w_qkv = np.ascontiguousarray(np.concatenate(
        [np.asarray(L0_attn_w_q), np.asarray(L0_attn_w_k),
         np.asarray(L0_attn_w_v)], axis=0).astype(np.float32))
    w_o = np.ascontiguousarray(np.asarray(L0_attn_w_o, dtype=np.float32))
    W_T = np.ascontiguousarray(np.asarray(L0_ffn_W, dtype=np.float32).T)
    biasv = np.ascontiguousarray(np.broadcast_to(
        np.asarray(L0_ffn_bias, dtype=np.float32).reshape(1, D), (128, D)))
    rw1_T = np.ascontiguousarray(np.asarray(L0_ffn_rw1, dtype=np.float32).T)
    rw2_T = np.ascontiguousarray(np.asarray(L0_ffn_rw2, dtype=np.float32).T)
    rb1 = np.ascontiguousarray(np.broadcast_to(
        np.asarray(L0_ffn_rb1, dtype=np.float32).reshape(1, 64), (128, 64)))
    rb2 = np.ascontiguousarray(np.broadcast_to(
        np.asarray(L0_ffn_rb2, dtype=np.float32).reshape(1, 32), (128, 32)))
    gates = np.ascontiguousarray(np.broadcast_to(
        np.asarray(L0_ffn_gates, dtype=np.float32).reshape(1, 16), (128, 16)))
    qgain = np.ascontiguousarray(np.broadcast_to(
        np.asarray(L0_attn_q_gain, dtype=np.float32).reshape(1, 16),
        (128, 16)))
    A_T = np.ascontiguousarray(
        np.asarray(L0_ffn_A, dtype=np.float32).reshape(256, D).T)
    B_mov = np.ascontiguousarray(
        np.asarray(L0_ffn_B, dtype=np.float32).transpose(0, 2, 1)
        .reshape(256, D))
    eyef = np.eye(128, dtype=np.float32)
    constsv = np.ascontiguousarray(np.broadcast_to(
        np.array([[LNEPS, RMSEPS, RND, 0.0, -RND, 0, 0, 0]],
                 dtype=np.float32), (128, 8)))

    inv_freq = 1.0 / (10000.0 ** (np.arange(0, HD, 2, dtype=np.float64) / HD))
    fr = np.outer(np.arange(S, dtype=np.float64), inv_freq)
    cos_all = np.cos(fr).astype(np.float32)
    sin_all = np.sin(fr).astype(np.float32)
    cos_k = np.ascontiguousarray(np.concatenate([cos_all, cos_all], axis=1))
    sin_k = np.ascontiguousarray(np.concatenate([sin_all, -sin_all], axis=1))

    nc = build_program()

    in_maps = []
    for core in range(8):
        b, j = divmod(core, 4)
        sel = np.arange(32)[j::4]
        xq_rows = x[b].reshape(32, 128, D)[sel].reshape(TPC, D)
        pos_q = (sel[:, None] * 128 + np.arange(128)[None, :]).reshape(-1)
        dm = (np.arange(512)[None, :] >
              (np.arange(128)[:, None] + 128 * j)).astype(np.float32)
        in_maps.append({
            "x_kv": np.ascontiguousarray(x[b]),
            "x_q": np.ascontiguousarray(xq_rows),
            "w_qkv": w_qkv, "w_o": w_o, "W_T": W_T, "biasv": biasv,
            "rw1_T": rw1_T, "rw2_T": rw2_T, "rb1": rb1, "rb2": rb2,
            "gates": gates, "qgain": qgain, "A_T": A_T, "B_mov": B_mov,
            "cos_q": np.ascontiguousarray(cos_k[pos_q]),
            "sin_q": np.ascontiguousarray(sin_k[pos_q]),
            "cos_k": cos_k, "sin_k": sin_k,
            "eyef": eyef, "dmask": np.ascontiguousarray(dm),
            "constsv": constsv,
        })

    res = run_bass_kernel_spmd(nc, in_maps, core_ids=list(range(8)))
    outv = np.empty((B, S, D), dtype=np.float32)
    for core in range(8):
        b, j = divmod(core, 4)
        sel = np.arange(32)[j::4]
        outv[b].reshape(32, 128, D)[sel] = \
            res.results[core]["o"].reshape(NT, 128, D)
    return outv


if __name__ == "__main__":
    nc = build_program()
    print("built ok")


# version 04
# speedup vs baseline: 1.0209x; 1.0209x over previous
import contextlib
import numpy as np
import concourse.bass as bass
import concourse.mybir as mybir
from concourse.bass_utils import run_bass_kernel_spmd

F32 = mybir.dt.float32
BF16 = mybir.dt.bfloat16
ALU = mybir.AluOpType
AX = mybir.AxisListType
ACT = mybir.ActivationFunctionType

B, S, D = 2, 4096, 2048
H, KVH, HD = 16, 4, 128
KVD = KVH * HD
TPC, NT, NC, NCH = 1024, 8, 32, 8
RND = 12582912.0
QEPS = 1e-5
LNEPS = 1e-5
RMSEPS = float(np.finfo(np.float32).eps)
SMSCALE = float(1.0 / np.sqrt(np.float32(128.0)))
NEGBIG = -1.0e30


class Sem:
    def __init__(self, es, nc, name):
        self.h = es.enter_context(nc.semaphore(name))
        self.n = 0


class Prog:

    def __init__(self):
        self.ops = []
        self.chain_sems = ()
        self.auto_chain = {}

    def op(self, eng, emit, waits=(), incs=()):
        waits = list(waits)
        incs = list(incs)
        for s, amt in incs:
            if s in self.chain_sems:
                waits.append((s, s.n))
        ac = self.auto_chain.get(eng)
        if ac is not None:
            waits.append((ac, max(0, ac.n - 1)))
            if not any(s is ac for s, _ in incs):
                incs.append((ac, 1))
        for s, amt in incs:
            s.n += amt
        self.ops.append((eng, tuple(waits), emit, tuple(incs)))

    def replay(self, eng_name, eng):
        for name, waits, emit, incs in self.ops:
            if name != eng_name:
                continue
            for s, v in waits:
                eng.wait_ge(s.h, v)
            ins = emit(eng)
            for s, amt in incs:
                ins = ins.then_inc(s.h, amt)


def build_program(debug=False, stop_early=False):
    nc = bass.Bass("TRN2", target_bir_lowering=False)

    def din(name, shape, dtype=F32):
        return nc.dram_tensor(name, shape, dtype, kind="ExternalInput").ap()

    x_kv = din("x_kv", [S, D])
    x_q = din("x_q", [TPC, D])
    w_qkv = din("w_qkv", [3072, D])
    w_o = din("w_o", [D, D])
    W_T = din("W_T", [D, D])
    biasv = din("biasv", [128, D])
    rw1_T = din("rw1_T", [D, 64])
    rw2_T = din("rw2_T", [64, 32])
    rb1 = din("rb1", [128, 64])
    rb2 = din("rb2", [128, 32])
    gates = din("gates", [128, 16])
    qgain = din("qgain", [128, 16])
    A_T = din("A_T", [D, 256])
    B_mov = din("B_mov", [256, D])
    cos_q = din("cos_q", [TPC, HD])
    sin_q = din("sin_q", [TPC, HD])
    cos_k = din("cos_k", [S, HD])
    sin_k = din("sin_k", [S, HD])
    eyef = din("eyef", [128, 128])
    constsv = din("constsv", [128, 8])
    dmask = din("dmask", [128, 512])
    out = nc.dram_tensor("o", [TPC, D], F32, kind="ExternalOutput").ap()

    kk = dict(kind="ExternalOutput") if debug else {}
    K_scr = nc.dram_tensor("K_scr", [KVH, S, HD], F32, **kk).ap()
    V_scr = nc.dram_tensor("V_scr", [KVH, S, HD], F32, **kk).ap()
    Q_scr = nc.dram_tensor("Q_scr", [H, TPC, HD], F32, **kk).ap()
    if debug:
        dbg_y = nc.dram_tensor("dbg_y", [TPC, D], F32,
                               kind="ExternalOutput").ap()
        dbg_ws = nc.dram_tensor("dbg_ws", [128, 3072], F32,
                                kind="ExternalOutput").ap()
        dbg_wkv = nc.dram_tensor("dbg_wkv", [128, 16 * 1024], BF16,
                                 kind="ExternalOutput").ap()
        dbg_xq = nc.dram_tensor("dbg_xq", [128, D], F32,
                                kind="ExternalOutput").ap()
        dbg_kvd = nc.dram_tensor("dbg_kvd", [128, 1024], F32,
                                 kind="ExternalOutput").ap()
        dbg_xb = nc.dram_tensor("dbg_xb", [128, D], F32,
                                kind="ExternalOutput").ap()
        dbg_x1 = nc.dram_tensor("dbg_x1", [128, D], F32,
                                kind="ExternalOutput").ap()
        dbg_tm = nc.dram_tensor("dbg_tm", [128, D], F32,
                                kind="ExternalOutput").ap()
        dbg_xqt = nc.dram_tensor("dbg_xqt", [128, D], BF16,
                                 kind="ExternalOutput").ap()
        dbg_st = nc.dram_tensor("dbg_st", [128, 64], F32,
                                kind="ExternalOutput").ap()
    ws3_dram = nc.dram_tensor("ws3_dram", [1, 3072], F32).ap()
    wso_dram = nc.dram_tensor("wso_dram", [1, D], F32).ap()

    es = contextlib.ExitStack()
    p = Prog()
    sL = Sem(es, nc, "sL")
    sG = Sem(es, nc, "sG")
    sP = Sem(es, nc, "sP")
    sA = Sem(es, nc, "sA")
    sV = Sem(es, nc, "sV")
    sB = Sem(es, nc, "sB")
    p.chain_sems = (sL, sG)
    p.auto_chain = {"vector": sV, "scalar": sA}

    def sbg(st_, name, shape, dtype=F32):
        return st_.enter_context(nc.sbuf_tensor(name, shape, dtype))

    def psg(st_, name, shape, dtype=F32):
        return st_.enter_context(nc.psum_tensor(name, shape, dtype))

    yx = sbg(es, "yx", [128, NT * D])
    eyef_s = sbg(es, "eyef_s", [128, 128])
    stats = sbg(es, "stats", [128, 64])
    (C_S1, C_MU, C_S2, C_VAR, C_RSTD, C_AM, C_QS, C_RA, C_MU2, C_M1, C_M2,
     C_NM1, C_E2, C_DEN, C_RDEN, C_C2) = range(16)
    asc_kv = sbg(es, "asc_kv", [128, NC])
    asc_q = sbg(es, "asc_q", [128, NT])
    asc_y = sbg(es, "asc_y", [128, NT])
    qg_s = sbg(es, "qg_s", [128, 16])
    gt_s = sbg(es, "gt_s", [128, 16])
    csts = sbg(es, "csts", [128, 8])
    nc.const_aps.register(F32, LNEPS, csts[:, 0:1])
    nc.const_aps.register(F32, RMSEPS, csts[:, 1:2])
    nc.const_aps.register(F32, RND, csts[:, 2:3])
    nc.const_aps.register(F32, 0.0, csts[:, 3:4])
    nc.const_aps.register(F32, -RND, csts[:, 4:5])
    phWS = contextlib.ExitStack()
    ws3x = sbg(phWS, "ws3x", [128, 3072])

    p.op("vector", lambda e: e.memset(stats[:], 0.0), incs=[(sV, 1)])
    p.op("sync", lambda e: e.dma_start(out=eyef_s[:], in_=eyef), incs=[(sL, 16)])
    p.op("sync", lambda e: e.dma_start(out=csts[:], in_=constsv),
         incs=[(sL, 16)])
    p.op("sync", lambda e: e.dma_start(out=qg_s[:], in_=qgain),
         incs=[(sL, 16)])
    p.op("sync", lambda e: e.dma_start(out=gt_s[:], in_=gates),
         incs=[(sL, 16)])

    def st(c):
        return stats[:, c:c + 1]

    def bcast(t, off, n):
        return bass.AP(t, off, [[0, 128], [1, n]])

    def barrier():
        bw = [(sP, sP.n), (sL, sL.n), (sG, sG.n), (sA, sA.n), (sV, sV.n)]
        p.op("scalar", lambda e: e.activation(
            out=stats[:, 62:63], in_=stats[:, 62:63], func=ACT.Copy),
            waits=bw)
        p.op("vector", lambda e: e.tensor_copy(
            out=stats[:, 61:62], in_=stats[:, 61:62]), waits=bw)
        return list(bw) + [(sA, sA.n), (sV, sV.n)]

    def ln_quant(xt, tmpa, xq, asc_ap, extra_wait=(), dbg_mid=False):
        p.op("vector", lambda e: e.tensor_reduce(
            out=st(C_S1), in_=xt[:], axis=AX.X, op=ALU.add),
            waits=list(extra_wait) + [(sA, sA.n)], incs=[(sV, 1)])
        p.op("scalar", lambda e: e.activation(
            out=tmpa[:], in_=xt[:], func=ACT.Square, accum_out=st(C_S2)),
            waits=list(extra_wait) + [(sV, sV.n)])
        p.op("scalar", lambda e: e.activation(
            out=st(C_MU), in_=st(C_S1), func=ACT.Copy, scale=1.0 / D),
            waits=[(sV, sV.n)], incs=[(sA, 1)])
        p.op("scalar", lambda e: e.activation(
            out=st(C_MU2), in_=st(C_MU), func=ACT.Square),
            waits=[(sA, sA.n)], incs=[(sA, 1)])
        p.op("vector", lambda e: e.tensor_scalar(
            out=st(C_VAR), in0=st(C_S2), scalar1=1.0 / D, scalar2=st(C_MU2),
            op0=ALU.mult, op1=ALU.subtract), waits=[(sA, sA.n)],
            incs=[(sV, 1)])
        p.op("scalar", lambda e: e.activation(
            out=st(C_RSTD), in_=st(C_VAR), func=ACT.Sqrt, bias=LNEPS),
            waits=[(sV, sV.n)], incs=[(sA, 1)])
        p.op("vector", lambda e: e.reciprocal(
            out=st(C_RSTD), in_=st(C_RSTD)), waits=[(sA, sA.n)],
            incs=[(sV, 1)])
        p.op("vector", lambda e: e.tensor_scalar(
            out=tmpa[:], in0=xt[:], scalar1=st(C_MU), scalar2=st(C_RSTD),
            op0=ALU.subtract, op1=ALU.mult), waits=[(sV, sV.n)])
        p.op("vector", lambda e: e.reduce_max(
            out=st(C_AM), in_=tmpa[:], axis=AX.X, apply_absolute_value=True),
            incs=[(sV, 1)])
        p.op("vector", lambda e: e.tensor_scalar(
            out=st(C_QS), in0=st(C_AM), scalar1=1.0 / 127.0, scalar2=QEPS,
            op0=ALU.mult, op1=ALU.max), waits=[(sV, sV.n)], incs=[(sV, 1)])
        p.op("vector", lambda e: e.tensor_copy(out=asc_ap, in_=st(C_QS)),
             waits=[(sV, sV.n)])
        p.op("vector", lambda e: e.reciprocal(out=st(C_RA), in_=st(C_QS)),
             incs=[(sV, 1)])
        p.op("scalar", lambda e: e.activation(
            out=xq[:], in_=tmpa[:], func=ACT.Copy, scale=st(C_RA), bias=RND),
            waits=[(sV, sV.n)], incs=[(sA, 1)])
        if dbg_mid:
            p.op("gpsimd", lambda e: e.dma_start(out=dbg_x1, in_=xq[:]),
                 waits=[(sA, sA.n)], incs=[(sG, 16)])
            p.op("vector", lambda e: e.tensor_copy(
                out=st(C_S1), in_=st(C_S1)), waits=[(sG, sG.n)],
                incs=[(sV, 1)])
        p.op("scalar", lambda e: e.activation(
            out=tmpa[:], in_=xq[:], func=ACT.Copy, bias=-RND),
            waits=[(sA, sA.n), (sV, sV.n)], incs=[(sA, 1)])
        if dbg_mid:
            p.op("gpsimd", lambda e: e.dma_start(out=dbg_tm, in_=tmpa[:]),
                 waits=[(sA, sA.n)], incs=[(sG, 16)])
            p.op("vector", lambda e: e.tensor_copy(
                out=st(C_S1), in_=st(C_S1)), waits=[(sG, sG.n)],
                incs=[(sV, 1)])
        p.op("scalar", lambda e: e.activation(
            out=xq[:], in_=tmpa[:], func=ACT.Copy),
            waits=[(sV, sV.n)], incs=[(sA, 1)])
        p.op("vector", lambda e: e.tensor_copy(
            out=st(C_S1), in_=st(C_S1)), waits=[(sA, sA.n)], incs=[(sV, 1)])

    def quant_w_tile(wbuf, wtmp, wqb, ws_dram_ap, ws_col, rowlen):
        p.op("vector", lambda e: e.reduce_max(
            out=st(C_AM), in_=wbuf[:], axis=AX.X, apply_absolute_value=True),
            waits=[(sL, sL.n), (sG, sG.n), (sA, sA.n), (sV, sV.n)],
            incs=[(sV, 1)])
        p.op("vector", lambda e: e.tensor_scalar(
            out=st(C_QS), in0=st(C_AM), scalar1=1.0 / 127.0, scalar2=QEPS,
            op0=ALU.mult, op1=ALU.max), waits=[(sV, sV.n)], incs=[(sV, 1)])
        p.op("vector", lambda e: e.reciprocal(out=st(C_RA), in_=st(C_QS)),
             waits=[(sV, sV.n)], incs=[(sV, 1)])
        p.op("gpsimd", lambda e: e.dma_start(
            out=bass.AP(ws_dram_ap.tensor, ws_col,
                        [[rowlen, 1], [1, 128]]), in_=st(C_QS)),
            waits=[(sV, sV.n)], incs=[(sG, 16)])
        p.op("scalar", lambda e: e.activation(
            out=wtmp[:], in_=wbuf[:], func=ACT.Copy, scale=st(C_RA),
            bias=RND), waits=[(sV, sV.n)], incs=[(sA, 1)])
        p.op("scalar", lambda e: e.activation(
            out=wqb[:], in_=wtmp[:], func=ACT.Copy, bias=-RND),
            waits=[(sA, sA.n)], incs=[(sA, 1)])
        p.op("vector", lambda e: e.tensor_copy(
            out=st(C_S1), in_=st(C_S1)), waits=[(sA, sA.n)], incs=[(sV, 1)])

    def transposeN(src, col0, tp, dst_ap, nkb, src_wait):
        for kb in range(nkb):
            p.op("tensor", lambda e, kb=kb: e.transpose(
                tp[:, kb * 128:(kb + 1) * 128],
                src[:, col0 + kb * 128:col0 + (kb + 1) * 128], eyef_s[:]),
                waits=list(src_wait) if kb == 0 else (),
                incs=[(sP, 1)] if kb == nkb - 1 else ())
        p.op("scalar", lambda e: e.activation(
            out=dst_ap, in_=tp[:, 0:nkb * 128], func=ACT.Copy),
            waits=[(sP, sP.n)], incs=[(sA, 1)])

    def rope(src, dst, sc, rp, nh):
        cosx = bass.AP(rp, 0, [[2 * HD, 128], [0, nh], [1, HD]])
        sin0 = bass.AP(rp, HD, [[2 * HD, 128], [0, nh], [1, 64]])
        sin1 = bass.AP(rp, HD + 64, [[2 * HD, 128], [0, nh], [1, 64]])
        w_ = nh * HD

        def v3(t, off, inner):
            return bass.AP(t, off, [[w_ if t is not sc else nh * 64, 128],
                                    [HD if t is not sc else 64, nh],
                                    [1, inner]])

        scv = bass.AP(sc, 0, [[nh * 64, 128], [64, nh], [1, 64]])
        p.op("vector", lambda e: e.tensor_tensor(
            out=v3(dst, 0, HD), in0=v3(src, 0, HD), in1=cosx, op=ALU.mult))
        p.op("vector", lambda e: e.tensor_tensor(
            out=scv, in0=v3(src, 64, 64), in1=sin0, op=ALU.mult))
        p.op("vector", lambda e: e.tensor_tensor(
            out=v3(dst, 0, 64), in0=v3(dst, 0, 64), in1=scv, op=ALU.add))
        p.op("vector", lambda e: e.tensor_tensor(
            out=scv, in0=v3(src, 0, 64), in1=sin1, op=ALU.mult))
        p.op("vector", lambda e: e.tensor_tensor(
            out=v3(dst, 64, 64), in0=v3(dst, 64, 64), in1=scv, op=ALU.add),
            incs=[(sV, 1)])

    phKV = contextlib.ExitStack()
    wkvT = sbg(phKV, "wkvT", [128, 16 * 1024], BF16)
    with contextlib.ExitStack() as ph:
        wt = [sbg(ph, "wt0", [128, D]), sbg(ph, "wt1", [128, D])]
        wtmp = sbg(ph, "wtmp", [128, D])
        wqb = sbg(ph, "wqb", [128, D])
        tpw = psg(ph, "tpw", [128, 2048])
        for r in range(8):
            buf = wt[r % 2]
            p.op("sync", lambda e, r=r, buf=buf: e.dma_start(
                out=buf[:], in_=w_qkv[D + r * 128:D + (r + 1) * 128, :]),
                waits=[(sV, sV.n)] if r >= 2 else [], incs=[(sL, 16)])
            quant_w_tile(buf, wtmp, wqb, ws3_dram, 2048 + r * 128, 3072)
            transposeN(wqb, 0, tpw,
                       bass.AP(wkvT, r * 128,
                               [[16 * 1024, 128], [1024, 16], [1, 128]]),
                       16, src_wait=[(sV, sV.n), (sA, sA.n)])

    barrier()
    if debug:
        p.op("gpsimd", lambda e: e.dma_start(out=dbg_wkv, in_=wkvT[:]),
             waits=[(sA, sA.n)], incs=[(sG, 16)])
    p.op("gpsimd", lambda e: e.dma_start(
        out=ws3x[:, 2048:3072],
        in_=bass.AP(ws3_dram.tensor, 2048, [[3072, 1], [0, 128], [1, 1024]])),
        waits=[(sG, sG.n)], incs=[(sG, 16)])
    ws3_rep1 = sG.n

    with contextlib.ExitStack() as ph:
        xt = [sbg(ph, "xt0", [128, D]), sbg(ph, "xt1", [128, D])]
        rop = [sbg(ph, "rop0", [128, 2 * HD]), sbg(ph, "rop1", [128, 2 * HD])]
        tmpa = sbg(ph, "tmpa", [128, D])
        xqb = sbg(ph, "xqb", [128, D])
        xqT = sbg(ph, "xqT", [128, D], BF16)
        kvd = sbg(ph, "kvd", [128, 2 * KVD])
        krm = sbg(ph, "krm", [128, KVD])
        kro = sbg(ph, "kro", [128, KVD])
        ksc = sbg(ph, "ksc", [128, KVH * 64])
        tpx = psg(ph, "tpx", [128, 2048])
        pKV = psg(ph, "pKV", [128, 1024])
        for t in range(1 if stop_early else NC):
            xb, rp = xt[t % 2], rop[t % 2]
            p.op("sync", lambda e, t=t, xb=xb: e.dma_start(
                out=xb[:], in_=x_kv[t * 128:(t + 1) * 128, :]),
                waits=[(sV, sV.n)] if t >= 2 else
                [(sP, sP.n), (sA, sA.n)], incs=[(sL, 16)])
            p.op("sync", lambda e, t=t, rp=rp: e.dma_start(
                out=rp[:, 0:HD], in_=cos_k[t * 128:(t + 1) * 128, :]),
                incs=[(sL, 16)])
            p.op("sync", lambda e, t=t, rp=rp: e.dma_start(
                out=rp[:, HD:2 * HD], in_=sin_k[t * 128:(t + 1) * 128, :]),
                incs=[(sL, 16)])
            ln_quant(xb, tmpa, xqb, asc_kv[:, t:t + 1],
                     extra_wait=[(sL, sL.n)], dbg_mid=(debug and t == 0))
            if debug and t == 0:
                p.op("gpsimd", lambda e, xqb=xqb: e.dma_start(
                    out=dbg_xq, in_=xqb[:]),
                     waits=[(sV, sV.n)], incs=[(sG, 16)])
                p.op("gpsimd", lambda e, xb=xb: e.dma_start(out=dbg_xb,
                                                            in_=xb[:]),
                     incs=[(sG, 16)])
                p.op("gpsimd", lambda e: e.dma_start(out=dbg_st,
                                                     in_=stats[:]),
                     incs=[(sG, 16)])
            transposeN(xqb, 0, tpx, bass.AP(xqT, 0, [[D, 128], [1, D]]), 16,
                       src_wait=[(sV, sV.n), (sA, sA.n)])
            if debug and t == 0:
                p.op("gpsimd", lambda e, xqT=xqT: e.dma_start(
                    out=dbg_xqt, in_=xqT[:]),
                     waits=[(sA, sA.n)], incs=[(sG, 16)])
                p.op("vector", lambda e: e.tensor_copy(
                    out=st(C_S1), in_=st(C_S1)), waits=[(sG, sG.n)],
                    incs=[(sV, 1)])
            for ot in range(2):
                for kb in range(16):
                    p.op("tensor", lambda e, ot=ot, kb=kb, xqT=xqT,
                         wkvT=wkvT: e.matmul(
                        pKV[:, ot * 512:(ot + 1) * 512],
                        bass.AP(xqT, kb * 128, [[D, 128], [1, 128]]),
                        bass.AP(wkvT, kb * 1024 + ot * 512,
                                [[16 * 1024, 128], [1, 512]]),
                        start=(kb == 0), stop=(kb == 15),
                        skip_group_check=True),
                        waits=[(sA, sA.n)] if (ot == 0 and kb == 0) else (),
                        incs=[(sP, 1)] if (ot == 1 and kb == 15) else ())
            p.op("scalar", lambda e, t=t: e.activation(
                out=kvd[:], in_=pKV[:], func=ACT.Copy,
                scale=asc_kv[:, t:t + 1]), waits=[(sP, sP.n)], incs=[(sA, 1)])
            dq_wait = [(sA, sA.n)]
            if t == 0:
                dq_wait.append((sG, ws3_rep1))
            p.op("vector", lambda e: e.tensor_tensor(
                out=kvd[:], in0=kvd[:], in1=ws3x[:, 2048:3072],
                op=ALU.mult), waits=dq_wait, incs=[(sV, 1)])
            if debug and t == 0:
                p.op("gpsimd", lambda e, kvd=kvd: e.dma_start(
                    out=dbg_kvd, in_=kvd[:]),
                     waits=[(sV, sV.n)], incs=[(sG, 16)])
            p.op("scalar", lambda e: e.activation(
                out=krm[:], in_=kvd[:, 0:KVD], func=ACT.Square),
                waits=[(sV, sV.n)], incs=[(sA, 1)])
            p.op("vector", lambda e: e.tensor_reduce(
                out=stats[:, 16:20],
                in_=bass.AP(krm, 0, [[KVD, 128], [HD, KVH], [1, HD]]),
                axis=AX.X, op=ALU.add), waits=[(sA, sA.n)], incs=[(sV, 1)])
            p.op("scalar", lambda e: e.activation(
                out=stats[:, 20:24], in_=stats[:, 16:20], func=ACT.Sqrt,
                scale=1.0 / HD, bias=RMSEPS), waits=[(sV, sV.n)],
                incs=[(sA, 1)])
            p.op("vector", lambda e: e.reciprocal(
                out=stats[:, 20:24], in_=stats[:, 20:24]),
                waits=[(sA, sA.n)], incs=[(sV, 1)])
            p.op("vector", lambda e: e.tensor_tensor(
                out=krm[:], in0=kvd[:, 0:KVD],
                in1=bass.AP(stats, 20, [[64, 128], [1, KVH], [0, HD]]),
                op=ALU.mult), waits=[(sV, sV.n)])
            rope(krm, kro, ksc, rp, KVH)
            for kh in range(KVH):
                p.op("gpsimd", lambda e, t=t, kh=kh: e.dma_start(
                    out=K_scr[kh, t * 128:(t + 1) * 128, :],
                    in_=kro[:, kh * HD:(kh + 1) * HD]),
                    waits=[(sV, sV.n)] if kh == 0 else (), incs=[(sG, 16)])
                p.op("gpsimd", lambda e, t=t, kh=kh: e.dma_start(
                    out=V_scr[kh, t * 128:(t + 1) * 128, :],
                    in_=kvd[:, KVD + kh * HD:KVD + (kh + 1) * HD]),
                    incs=[(sG, 16)])
            p.op("vector", lambda e: e.tensor_copy(
                out=st(C_S1), in_=st(C_S1)), waits=[(sG, sG.n)],
                incs=[(sV, 1)])

    phKV.close()

    with contextlib.ExitStack() as ph:
        wqT = sbg(ph, "wqT", [128, 16 * 2048], BF16)
        with contextlib.ExitStack() as phw:
            wt = [sbg(phw, "wt0b", [128, D]), sbg(phw, "wt1b", [128, D])]
            wtmp = sbg(phw, "wtmpb", [128, D])
            wqb = sbg(phw, "wqbb", [128, D])
            tpw = psg(phw, "tpwb", [128, 2048])
            for r in range(16):
                buf = wt[r % 2]
                p.op("sync", lambda e, r=r, buf=buf: e.dma_start(
                    out=buf[:], in_=w_qkv[r * 128:(r + 1) * 128, :]),
                    waits=[(sV, sV.n)] if r >= 2 else
                    [(sP, sP.n), (sG, sG.n), (sA, sA.n)], incs=[(sL, 16)])
                quant_w_tile(buf, wtmp, wqb, ws3_dram, r * 128, 3072)
                transposeN(wqb, 0, tpw,
                           bass.AP(wqT, r * 128,
                                   [[16 * 2048, 128], [2048, 16], [1, 128]]),
                           16, src_wait=[(sV, sV.n), (sA, sA.n)])

        barrier()
        p.op("gpsimd", lambda e: e.dma_start(
            out=ws3x[:, 0:2048],
            in_=bass.AP(ws3_dram.tensor, 0,
                        [[3072, 1], [0, 128], [1, 2048]])),
            waits=[(sG, sG.n)], incs=[(sG, 16)])
        ws3_rep2 = sG.n

        xt = [sbg(ph, "xq0", [128, D]), sbg(ph, "xq1", [128, D])]
        rop = [sbg(ph, "ropq0", [128, 2 * HD]),
               sbg(ph, "ropq1", [128, 2 * HD])]
        tmpa = sbg(ph, "tmpaq", [128, D])
        xqb = sbg(ph, "xqbq", [128, D])
        xqT = sbg(ph, "xqTq", [128, D], BF16)
        qd = sbg(ph, "qd", [128, D])
        qn = sbg(ph, "qn", [128, D])
        qro = sbg(ph, "qro", [128, D])
        qsc = sbg(ph, "qsc", [128, H * 64])
        tpx = psg(ph, "tpxq", [128, 2048])
        pQ = psg(ph, "pQ", [128, 2048])
        for t in range(NT):
            xb, rp = xt[t % 2], rop[t % 2]
            p.op("sync", lambda e, t=t, xb=xb: e.dma_start(
                out=xb[:], in_=x_q[t * 128:(t + 1) * 128, :]),
                waits=[(sV, sV.n)] if t >= 2 else [(sV, sV.n), (sA, sA.n)],
                incs=[(sL, 16)])
            p.op("sync", lambda e, t=t, rp=rp: e.dma_start(
                out=rp[:, 0:HD], in_=cos_q[t * 128:(t + 1) * 128, :]),
                incs=[(sL, 16)])
            p.op("sync", lambda e, t=t, rp=rp: e.dma_start(
                out=rp[:, HD:2 * HD], in_=sin_q[t * 128:(t + 1) * 128, :]),
                incs=[(sL, 16)])
            ln_quant(xb, tmpa, xqb, asc_q[:, t:t + 1],
                     extra_wait=[(sL, sL.n)])
            transposeN(xqb, 0, tpx, bass.AP(xqT, 0, [[D, 128], [1, D]]), 16,
                       src_wait=[(sV, sV.n), (sA, sA.n)])
            for ot in range(4):
                for kb in range(16):
                    p.op("tensor", lambda e, ot=ot, kb=kb: e.matmul(
                        pQ[:, ot * 512:(ot + 1) * 512],
                        bass.AP(xqT, kb * 128, [[D, 128], [1, 128]]),
                        bass.AP(wqT, kb * 2048 + ot * 512,
                                [[16 * 2048, 128], [1, 512]]),
                        start=(kb == 0), stop=(kb == 15),
                        skip_group_check=True),
                        waits=[(sA, sA.n)] if (ot == 0 and kb == 0) else (),
                        incs=[(sP, 1)] if (ot == 3 and kb == 15) else ())
            p.op("scalar", lambda e, t=t: e.activation(
                out=qd[:], in_=pQ[:], func=ACT.Copy,
                scale=asc_q[:, t:t + 1]), waits=[(sP, sP.n)], incs=[(sA, 1)])
            dq_wait = [(sA, sA.n)]
            if t == 0:
                dq_wait.append((sG, ws3_rep2))
            p.op("vector", lambda e: e.tensor_tensor(
                out=qd[:], in0=qd[:], in1=ws3x[:, 0:2048], op=ALU.mult),
                waits=dq_wait, incs=[(sV, 1)])
            p.op("scalar", lambda e: e.activation(
                out=tmpa[:], in_=qd[:], func=ACT.Square),
                waits=[(sV, sV.n)], incs=[(sA, 1)])
            p.op("vector", lambda e: e.tensor_reduce(
                out=stats[:, 16:32],
                in_=bass.AP(tmpa, 0, [[D, 128], [HD, H], [1, HD]]),
                axis=AX.X, op=ALU.add), waits=[(sA, sA.n)], incs=[(sV, 1)])
            p.op("scalar", lambda e: e.activation(
                out=stats[:, 32:48], in_=stats[:, 16:32], func=ACT.Sqrt,
                scale=1.0 / HD, bias=RMSEPS), waits=[(sV, sV.n)],
                incs=[(sA, 1)])
            p.op("vector", lambda e: e.reciprocal(
                out=stats[:, 32:48], in_=stats[:, 32:48]),
                waits=[(sA, sA.n)], incs=[(sV, 1)])
            p.op("vector", lambda e: e.tensor_tensor(
                out=stats[:, 32:48], in0=stats[:, 32:48],
                in1=qg_s[:], op=ALU.mult), waits=[(sV, sV.n)],
                incs=[(sV, 1)])
            p.op("vector", lambda e: e.tensor_tensor(
                out=qn[:], in0=qd[:],
                in1=bass.AP(stats, 32, [[64, 128], [1, H], [0, HD]]),
                op=ALU.mult), waits=[(sV, sV.n)])
            rope(qn, qro, qsc, rp, H)
            for h in range(H):
                p.op("gpsimd", lambda e, t=t, h=h: e.dma_start(
                    out=Q_scr[h, t * 128:(t + 1) * 128, :],
                    in_=qro[:, h * HD:(h + 1) * HD]),
                    waits=[(sV, sV.n)] if h == 0 else (), incs=[(sG, 16)])
            p.op("vector", lambda e: e.tensor_copy(
                out=st(C_S1), in_=st(C_S1)), waits=[(sG, sG.n)],
                incs=[(sV, 1)])

    if debug:
        p.op("gpsimd", lambda e: e.dma_start(out=dbg_ws, in_=ws3x[:]),
             waits=[(sV, sV.n), (sA, sA.n)], incs=[(sG, 16)])
    barrier()
    if stop_early:
        p.op("gpsimd", lambda e: e.dma_start(
            out=bass.AP(ws3_dram.tensor, 0, [[3072, 1], [1, 16]]),
            in_=stats[0:1, 0:16]), waits=[(sG, sG.n)], incs=[(sG, 16)])
        phKV.close()
        phWS.close()
        with nc.Block() as block:
            @block.sync
            def _(eng):
                p.replay("sync", eng)

            @block.gpsimd
            def _(eng):
                p.replay("gpsimd", eng)

            @block.tensor
            def _(eng):
                p.replay("tensor", eng)

            @block.scalar
            def _(eng):
                p.replay("scalar", eng)

            @block.vector
            def _(eng):
                p.replay("vector", eng)
        es.close()
        return nc
    barrier()
    phWS.close()

    with contextlib.ExitStack() as ph:
        dmask_s = sbg(ph, "dmask_s", [128, 512])
        p.op("sync", lambda e: e.dma_start(out=dmask_s[:], in_=dmask),
             waits=[(sV, sV.n), (sP, sP.n), (sG, sG.n)], incs=[(sL, 16)])
        qsb = sbg(ph, "qsb", [128, NT * HD])
        ksb = [sbg(ph, "ksb0", [128, 512]), sbg(ph, "ksb1", [128, 512])]
        vsb = [sbg(ph, "vsb0", [128, 512]), sbg(ph, "vsb1", [128, 512])]
        QT_hi = sbg(ph, "QT_hi", [128, TPC], BF16)
        QT_lo = sbg(ph, "QT_lo", [128, TPC], BF16)
        KT_hi = sbg(ph, "KT_hi", [128, 512], BF16)
        KT_lo = sbg(ph, "KT_lo", [128, 512], BF16)
        V_hi = sbg(ph, "V_hi", [128, 512], BF16)
        V_lo = sbg(ph, "V_lo", [128, 512], BF16)
        exs = [sbg(ph, "exs0", [128, 512]), sbg(ph, "exs1", [128, 512])]
        eT_hi = sbg(ph, "eT_hi", [128, 4 * TPC], BF16)
        eT_lo = sbg(ph, "eT_lo", [128, 4 * TPC], BF16)
        rsum = sbg(ph, "rsum", [128, 64])
        rtot = sbg(ph, "rtot", [128, NT])
        rrec = sbg(ph, "rrec", [128, NT])
        ysb = sbg(ph, "ysb", [128, TPC])
        pT1 = psg(ph, "pT1", [128, 1024])
        pE = psg(ph, "pE", [128, 512])
        pS = psg(ph, "pS", [128, 1024])
        pY = psg(ph, "pY", [128, 1024])

        for h in range(H):
            kh = h // (H // KVH)
            qw = [(sP, sP.n), (sA, sA.n)]
            if h == 0:
                qw.append((sG, sG.n))
            p.op("sync", lambda e, h=h: e.dma_start(
                out=qsb[:],
                in_=Q_scr[h].rearrange("(a p) d -> p a d", p=128)),
                waits=qw, incs=[(sL, 16)])
            for g in range(2):
                transposeN(qsb, g * 512, pS,
                           bass.AP(QT_hi, g * 512, [[TPC, 128], [1, 512]]),
                           4, src_wait=[(sL, sL.n), (sV, sV.n), (sA, sA.n)])
                p.op("vector", lambda e, g=g: e.scalar_tensor_tensor(
                    out=bass.AP(QT_lo, g * 512, [[TPC, 128], [1, 512]]),
                    in0=pS[:, 0:512], scalar=1.0,
                    in1=bass.AP(QT_hi, g * 512, [[TPC, 128], [1, 512]]),
                    op0=ALU.mult, op1=ALU.subtract),
                    waits=[(sA, sA.n)], incs=[(sV, 1)])
            av_done_pe = sP.n
            for c in range(NCH):
                kb_, vb_ = ksb[c % 2], vsb[c % 2]
                p.op("sync", lambda e, c=c, kh=kh, kb_=kb_: e.dma_start(
                    out=kb_[:],
                    in_=K_scr[kh, c * 512:(c + 1) * 512, :].rearrange(
                        "(a p) d -> p a d", p=128)),
                    waits=[(sP, av_done_pe)], incs=[(sL, 16)])
                p.op("sync", lambda e, c=c, kh=kh, vb_=vb_: e.dma_start(
                    out=vb_[:],
                    in_=V_scr[kh, c * 512:(c + 1) * 512, :].rearrange(
                        "(a p) d -> p a d", p=128)),
                    incs=[(sL, 16)])
                transposeN(kb_, 0, pT1,
                           bass.AP(KT_hi, 0, [[512, 128], [1, 512]]), 4,
                           src_wait=[(sL, sL.n), (sV, sV.n), (sA, sA.n)])
                p.op("vector", lambda e: e.scalar_tensor_tensor(
                    out=KT_lo[:], in0=pT1[:, 0:512], scalar=1.0,
                    in1=KT_hi[:], op0=ALU.mult, op1=ALU.subtract),
                    waits=[(sA, sA.n)], incs=[(sV, 1)])
                p.op("scalar", lambda e, vb_=vb_: e.activation(
                    out=V_hi[:], in_=vb_[:], func=ACT.Copy),
                    waits=[(sL, sL.n), (sP, av_done_pe)], incs=[(sA, 1)])
                p.op("vector", lambda e, vb_=vb_: e.scalar_tensor_tensor(
                    out=V_lo[:], in0=vb_[:], scalar=1.0, in1=V_hi[:],
                    op0=ALU.mult, op1=ALU.subtract),
                    waits=[(sA, sA.n)], incs=[(sV, 1)])
                exp_done = {}
                for i in range(c, NT):
                    sl = pS[:, (i % 2) * 512:(i % 2) * 512 + 512]
                    tpe = pT1[:, 512:1024] if i % 2 == 0 else pE[:, 0:512]
                    wt_ = []
                    if i <= c + 1:
                        wt_ += [(sA, sA.n), (sV, sV.n)]
                    elif (i - 2) in exp_done:
                        wt_ += [(sA, exp_done[i - 2][0]),
                                (sV, exp_done[i - 2][1])]
                    for pi, (lt, rt) in enumerate(
                            ((QT_hi, KT_hi), (QT_lo, KT_hi), (QT_hi, KT_lo))):
                        p.op("tensor", lambda e, sl=sl, lt=lt, rt=rt, i=i,
                             pi=pi: e.matmul(
                                 sl,
                                 bass.AP(lt, i * 128, [[TPC, 128], [1, 128]]),
                                 rt[:], start=(pi == 0), stop=(pi == 2),
                                 skip_group_check=True),
                             waits=wt_ if pi == 0 else (),
                             incs=[(sP, 1)] if pi == 2 else ())
                    if i == c:
                        p.op("vector", lambda e, sl=sl: e.scalar_tensor_tensor(
                            out=sl, in0=dmask_s[:], scalar=NEGBIG, in1=sl,
                            op0=ALU.mult, op1=ALU.add),
                            waits=[(sP, sP.n)], incs=[(sV, 1)])
                        mwait = [(sV, sV.n)]
                    else:
                        mwait = [(sP, sP.n)]
                    eb = exs[i % 2]
                    p.op("scalar", lambda e, sl=sl, eb=eb, i=i, c=c:
                         e.activation(
                             out=eb[:], in_=sl, func=ACT.Exp, scale=SMSCALE,
                             accum_out=rsum[:, i * 8 + c:i * 8 + c + 1]),
                         waits=mwait, incs=[(sA, 1)])
                    twait = [(sA, sA.n)]
                    if i <= c + 1:
                        twait.append((sV, sV.n))
                    elif (i - 2) in exp_done:
                        twait.append((sV, exp_done[i - 2][1]))
                    for s_ in range(4):
                        p.op("tensor", lambda e, tpe=tpe, eb=eb, s_=s_:
                             e.transpose(tpe[:, s_ * 128:(s_ + 1) * 128],
                                         eb[:, s_ * 128:(s_ + 1) * 128],
                                         eyef_s[:]),
                             waits=twait if s_ == 0 else (),
                             incs=[(sP, 1)] if s_ == 3 else ())
                    dst_hi = bass.AP(eT_hi, i * 128,
                                     [[4 * TPC, 128], [TPC, 4], [1, 128]])
                    dst_lo = bass.AP(eT_lo, i * 128,
                                     [[4 * TPC, 128], [TPC, 4], [1, 128]])
                    p.op("scalar", lambda e, tpe=tpe, dst_hi=dst_hi:
                         e.activation(out=dst_hi, in_=tpe[:, 0:512],
                                      func=ACT.Copy),
                         waits=[(sP, sP.n)], incs=[(sA, 1)])
                    p.op("vector", lambda e, tpe=tpe, dst_hi=dst_hi,
                         dst_lo=dst_lo: e.scalar_tensor_tensor(
                             out=dst_lo, in0=tpe[:, 0:512], scalar=1.0,
                             in1=dst_hi, op0=ALU.mult, op1=ALU.subtract),
                         waits=[(sA, sA.n)], incs=[(sV, 1)])
                    exp_done[i] = (sA.n, sV.n)
                avw = [(sA, sA.n), (sV, sV.n)]
                mm_list = []
                for s_ in range(4):
                    for (vt, et) in ((V_hi, eT_hi), (V_lo, eT_hi),
                                     (V_hi, eT_lo)):
                        segs = ([(c * 128, 512)] if c * 128 < 512 else []) + \
                               [(max(512, c * 128), 1024)]
                        for (a0, a1) in segs:
                            mm_list.append((s_, vt, et, a0, a1))
                for n_, (s_, vt, et, a0, a1) in enumerate(mm_list):
                    st_flag = (c == 0 and n_ < 2)
                    last = (n_ == len(mm_list) - 1)
                    p.op("tensor", lambda e, s_=s_, vt=vt, et=et, a0=a0,
                         a1=a1, st_flag=st_flag: e.matmul(
                             pY[:, a0:a1], vt[:, s_ * 128:(s_ + 1) * 128],
                             bass.AP(et, s_ * TPC + a0,
                                     [[4 * TPC, 128], [1, a1 - a0]]),
                             start=st_flag, stop=False,
                             skip_group_check=True),
                         waits=avw if n_ == 0 else (),
                         incs=[(sP, 1)] if last else ())
                av_done_pe = sP.n
            for i in range(NT):
                p.op("vector", lambda e, i=i: e.tensor_reduce(
                    out=rtot[:, i:i + 1],
                    in_=rsum[:, i * 8:i * 8 + i + 1], axis=AX.X, op=ALU.add),
                    waits=[(sA, sA.n)] if i == 0 else (),
                    incs=[(sV, 1)] if i == NT - 1 else ())
            p.op("vector", lambda e: e.reciprocal(out=rrec[:], in_=rtot[:]),
                 waits=[(sV, sV.n)], incs=[(sV, 1)])
            p.op("scalar", lambda e: e.activation(
                out=ysb[:, 0:512], in_=pY[:, 0:512], func=ACT.Copy),
                waits=[(sP, av_done_pe)])
            p.op("scalar", lambda e: e.activation(
                out=ysb[:, 512:1024], in_=pY[:, 512:1024], func=ACT.Copy),
                incs=[(sA, 1)])
            for i in range(NT):
                p.op("tensor", lambda e, i=i: e.transpose(
                    pS[:, i * 128:(i + 1) * 128],
                    ysb[:, i * 128:(i + 1) * 128], eyef_s[:]),
                    waits=[(sA, sA.n)] if i == 0 else (),
                    incs=[(sP, 1)] if i == NT - 1 else ())
            for i in range(NT):
                p.op("scalar", lambda e, i=i, h=h: e.activation(
                    out=yx[:, i * D + h * HD:i * D + h * HD + HD],
                    in_=pS[:, i * 128:(i + 1) * 128], func=ACT.Copy,
                    scale=rrec[:, i:i + 1]),
                    waits=[(sP, sP.n), (sV, sV.n)] if i == 0 else (),
                    incs=[(sA, 1)] if i == NT - 1 else ())

    if debug:
        for t in range(NT):
            p.op("gpsimd", lambda e, t=t: e.dma_start(
                out=dbg_y[t * 128:(t + 1) * 128, :],
                in_=yx[:, t * D:(t + 1) * D]),
                waits=[(sA, sA.n)] if t == 0 else (), incs=[(sG, 16)])

    barrier()

    with contextlib.ExitStack() as ph:
        wsox = sbg(ph, "wsox", [128, D])
        woqT = sbg(ph, "woqT", [128, 16 * 2048], BF16)
        x0b = [sbg(ph, "x0b0", [128, D]), sbg(ph, "x0b1", [128, D])]
        yq = sbg(ph, "yq", [128, D])
        yqT = sbg(ph, "yqT", [128, D], BF16)
        od = sbg(ph, "od", [128, D])
        tpw = psg(ph, "tpwc", [128, 2048])
        pO = psg(ph, "pO", [128, 2048])
        with contextlib.ExitStack() as phw:
            wt = [sbg(phw, "wt0c", [128, D]), sbg(phw, "wt1c", [128, D])]
            wtmp = sbg(phw, "wtmpc", [128, D])
            wqb = sbg(phw, "wqbc", [128, D])
            for r in range(16):
                buf = wt[r % 2]
                p.op("sync", lambda e, r=r, buf=buf: e.dma_start(
                    out=buf[:], in_=w_o[r * 128:(r + 1) * 128, :]),
                    waits=[(sV, sV.n)] if r >= 2 else
                    [(sP, sP.n), (sA, sA.n)], incs=[(sL, 16)])
                quant_w_tile(buf, wtmp, wqb, wso_dram, r * 128, 2048)
                transposeN(wqb, 0, tpw,
                           bass.AP(woqT, r * 128,
                                   [[16 * 2048, 128], [2048, 16], [1, 128]]),
                           16, src_wait=[(sV, sV.n), (sA, sA.n)])
        barrier()
        p.op("gpsimd", lambda e: e.dma_start(
            out=wsox[:],
            in_=bass.AP(wso_dram.tensor, 0, [[2048, 1], [0, 128], [1, 2048]])),
            waits=[(sG, sG.n)], incs=[(sG, 16)])
        wso_rep = sG.n
        for t in range(NT):
            xb = x0b[t % 2]
            p.op("sync", lambda e, t=t, xb=xb: e.dma_start(
                out=xb[:], in_=x_q[t * 128:(t + 1) * 128, :]),
                waits=[(sV, sV.n)] if t >= 2 else [(sV, sV.n), (sA, sA.n)],
                incs=[(sL, 16)])
            ytile = yx[:, t * D:(t + 1) * D]
            p.op("vector", lambda e, ytile=ytile: e.reduce_max(
                out=st(C_AM), in_=ytile, axis=AX.X,
                apply_absolute_value=True), waits=[(sA, sA.n)],
                incs=[(sV, 1)])
            p.op("vector", lambda e: e.tensor_scalar(
                out=st(C_QS), in0=st(C_AM), scalar1=1.0 / 127.0, scalar2=QEPS,
                op0=ALU.mult, op1=ALU.max), waits=[(sV, sV.n)],
                incs=[(sV, 1)])
            p.op("vector", lambda e, t=t: e.tensor_copy(
                out=asc_y[:, t:t + 1], in_=st(C_QS)), waits=[(sV, sV.n)])
            p.op("vector", lambda e: e.reciprocal(out=st(C_RA), in_=st(C_QS)),
                 incs=[(sV, 1)])
            p.op("scalar", lambda e, ytile=ytile: e.activation(
                out=yq[:], in_=ytile, func=ACT.Copy, scale=st(C_RA),
                bias=RND), waits=[(sV, sV.n)], incs=[(sA, 1)])
            p.op("scalar", lambda e: e.activation(
                out=od[:], in_=yq[:], func=ACT.Copy, bias=-RND),
                waits=[(sA, sA.n)], incs=[(sA, 1)])
            p.op("vector", lambda e: e.tensor_copy(
                out=st(C_S1), in_=st(C_S1)), waits=[(sA, sA.n)],
                incs=[(sV, 1)])
            transposeN(od, 0, tpw, bass.AP(yqT, 0, [[D, 128], [1, D]]), 16,
                       src_wait=[(sV, sV.n), (sA, sA.n)])
            for ot in range(4):
                for kb in range(16):
                    p.op("tensor", lambda e, ot=ot, kb=kb: e.matmul(
                        pO[:, ot * 512:(ot + 1) * 512],
                        bass.AP(yqT, kb * 128, [[D, 128], [1, 128]]),
                        bass.AP(woqT, kb * 2048 + ot * 512,
                                [[16 * 2048, 128], [1, 512]]),
                        start=(kb == 0), stop=(kb == 15),
                        skip_group_check=True),
                        waits=[(sA, sA.n)] if (ot == 0 and kb == 0) else (),
                        incs=[(sP, 1)] if (ot == 3 and kb == 15) else ())
            p.op("scalar", lambda e, t=t: e.activation(
                out=od[:], in_=pO[:], func=ACT.Copy,
                scale=asc_y[:, t:t + 1]), waits=[(sP, sP.n)], incs=[(sA, 1)])
            dq_wait = [(sA, sA.n), (sL, sL.n)]
            if t == 0:
                dq_wait.append((sG, wso_rep))
            p.op("vector", lambda e: e.tensor_tensor(
                out=od[:], in0=od[:], in1=wsox[:], op=ALU.mult),
                waits=dq_wait)
            p.op("vector", lambda e, ytile=ytile, xb=xb: e.tensor_tensor(
                out=ytile, in0=od[:], in1=xb[:], op=ALU.add), incs=[(sV, 1)])

    barrier()

    with contextlib.ExitStack() as ph:
        rw1h = sbg(ph, "rw1h", [128, 16 * 64], BF16)
        rw1l = sbg(ph, "rw1l", [128, 16 * 64], BF16)
        rw2s = sbg(ph, "rw2s", [64, 32])
        rbg = sbg(ph, "rbg", [128, 128])
        bvs = sbg(ph, "bvs", [128, 512])
        ath = sbg(ph, "ath", [128, 16 * 256], BF16)
        atl = sbg(ph, "atl", [128, 16 * 256], BF16)
        bmh = sbg(ph, "bmh", [128, 2 * D], BF16)
        bml = sbg(ph, "bml", [128, 2 * D], BF16)
        x2Th = sbg(ph, "x2Th", [128, 16 * 1024], BF16)
        x2Tl = sbg(ph, "x2Tl", [128, 16 * 1024], BF16)
        xn2 = sbg(ph, "xn2", [128, D])
        hsb = sbg(ph, "hsb", [128, 64])
        hT = sbg(ph, "hT", [64, 128])
        lsb = sbg(ph, "lsb", [128, 32])
        lg = sbg(ph, "lg", [128, 16])
        eq1 = sbg(ph, "eq1", [128, 16])
        eq2 = sbg(ph, "eq2", [128, 16])
        lgm = sbg(ph, "lgm", [128, 16])
        cf1 = sbg(ph, "cf1", [128, 16])
        coef = sbg(ph, "coef", [128, NT * 16])
        wxa = sbg(ph, "wxa", [128, 256])
        wxTh = sbg(ph, "wxTh", [128, NT * 256], BF16)
        wxTl = sbg(ph, "wxTl", [128, NT * 256], BF16)
        wmb = [sbg(ph, "wmb0", [128, 512]), sbg(ph, "wmb1", [128, 512])]
        wmh = [sbg(ph, "wmh0", [128, 512], BF16),
               sbg(ph, "wmh1", [128, 512], BF16)]
        wml = [sbg(ph, "wml0", [128, 512], BF16),
               sbg(ph, "wml1", [128, 512], BF16)]
        osb = [sbg(ph, "osb0", [128, 512]), sbg(ph, "osb1", [128, 512])]

        with contextlib.ExitStack() as phs:
            srcb = sbg(phs, "srcb", [128, 2048])
            p.op("sync", lambda e: e.dma_start(out=rw2s[:], in_=rw2_T),
                 waits=[(sV, sV.n), (sP, sP.n), (sA, sA.n)], incs=[(sL, 16)])
            p.op("sync", lambda e: e.dma_start(out=rbg[:, 0:64], in_=rb1),
                 incs=[(sL, 16)])
            p.op("sync", lambda e: e.dma_start(out=rbg[:, 64:96], in_=rb2),
                 incs=[(sL, 16)])
            for (srcdram, dd, na, ca, hi_, lo_) in (
                    (rw1_T, 64, 16, 16, rw1h, rw1l),
                    (A_T, 256, 16, 8, ath, atl),
                    (B_mov, 2048, 2, 1, bmh, bml)):
                for a0 in range(0, na, ca):
                    ncols = ca * dd
                    p.op("sync", lambda e, srcdram=srcdram, dd=dd, a0=a0,
                         ca=ca, ncols=ncols: e.dma_start(
                             out=srcb[:, 0:ncols],
                             in_=bass.AP(srcdram.tensor, a0 * 128 * dd,
                                         [[dd, 128], [128 * dd, ca],
                                          [1, dd]])),
                         waits=[(sV, sV.n)], incs=[(sL, 16)])
                    p.op("scalar", lambda e, ncols=ncols, hi_=hi_, a0=a0,
                         dd=dd: e.activation(
                             out=hi_[:, a0 * dd:a0 * dd + ncols],
                             in_=srcb[:, 0:ncols], func=ACT.Copy),
                         waits=[(sL, sL.n)], incs=[(sA, 1)])
                    p.op("vector", lambda e, ncols=ncols, hi_=hi_, lo_=lo_,
                         a0=a0, dd=dd: e.scalar_tensor_tensor(
                             out=lo_[:, a0 * dd:a0 * dd + ncols],
                             in0=srcb[:, 0:ncols], scalar=1.0,
                             in1=hi_[:, a0 * dd:a0 * dd + ncols],
                             op0=ALU.mult, op1=ALU.subtract),
                         waits=[(sA, sA.n)], incs=[(sV, 1)])

        with contextlib.ExitStack() as php:
            tpf = psg(php, "tpf", [128, 2048])
            pR = psg(php, "pR", [128, 512])
            for t in range(NT):
                x1t = yx[:, t * D:(t + 1) * D]
                p.op("vector", lambda e, x1t=x1t: e.tensor_reduce(
                    out=st(C_S1), in_=x1t, axis=AX.X, op=ALU.add),
                    incs=[(sV, 1)])
                p.op("scalar", lambda e, x1t=x1t: e.activation(
                    out=xn2[:], in_=x1t, func=ACT.Square,
                    accum_out=st(C_S2)), waits=[(sV, sV.n)])
                p.op("scalar", lambda e: e.activation(
                    out=st(C_MU), in_=st(C_S1), func=ACT.Copy,
                    scale=1.0 / D), waits=[(sV, sV.n)], incs=[(sA, 1)])
                p.op("scalar", lambda e: e.activation(
                    out=st(C_MU2), in_=st(C_MU), func=ACT.Square),
                    waits=[(sA, sA.n)], incs=[(sA, 1)])
                p.op("vector", lambda e: e.tensor_scalar(
                    out=st(C_VAR), in0=st(C_S2), scalar1=1.0 / D,
                    scalar2=st(C_MU2), op0=ALU.mult, op1=ALU.subtract),
                    waits=[(sA, sA.n)], incs=[(sV, 1)])
                p.op("scalar", lambda e: e.activation(
                    out=st(C_RSTD), in_=st(C_VAR), func=ACT.Sqrt,
                    bias=LNEPS), waits=[(sV, sV.n)], incs=[(sA, 1)])
                p.op("vector", lambda e: e.reciprocal(
                    out=st(C_RSTD), in_=st(C_RSTD)), waits=[(sA, sA.n)],
                    incs=[(sV, 1)])
                p.op("vector", lambda e, x1t=x1t: e.tensor_scalar(
                    out=xn2[:], in0=x1t, scalar1=st(C_MU),
                    scalar2=st(C_RSTD), op0=ALU.subtract, op1=ALU.mult),
                    waits=[(sV, sV.n)], incs=[(sV, 1)])
                transposeN(xn2, 0, tpf,
                           bass.AP(x2Th, t * 128,
                                   [[16 * 1024, 128], [1024, 16], [1, 128]]),
                           16, src_wait=[(sV, sV.n), (sA, sA.n)])
                p.op("vector", lambda e, t=t: e.scalar_tensor_tensor(
                    out=bass.AP(x2Tl, t * 128,
                                [[16 * 1024, 128], [1024, 16], [1, 128]]),
                    in0=tpf[:, 0:2048], scalar=1.0,
                    in1=bass.AP(x2Th, t * 128,
                                [[16 * 1024, 128], [1024, 16], [1, 128]]),
                    op0=ALU.mult, op1=ALU.subtract),
                    waits=[(sA, sA.n)], incs=[(sV, 1)])
                for pi, (lt, rt) in enumerate(((x2Th, rw1h), (x2Tl, rw1h),
                                               (x2Th, rw1l))):
                    for kb in range(16):
                        p.op("tensor", lambda e, pi=pi, lt=lt, rt=rt, kb=kb,
                             t=t: e.matmul(
                                 pR[:, 0:64],
                                 bass.AP(lt, kb * 1024 + t * 128,
                                         [[16 * 1024, 128], [1, 128]]),
                                 bass.AP(rt, kb * 64,
                                         [[16 * 64, 128], [1, 64]]),
                                 start=(pi == 0 and kb == 0),
                                 stop=(pi == 2 and kb == 15),
                                 skip_group_check=True),
                             waits=[(sV, sV.n)] if (pi == 0 and kb == 0)
                             else (),
                             incs=[(sP, 1)] if (pi == 2 and kb == 15)
                             else ())
                p.op("vector", lambda e: e.scalar_tensor_tensor(
                    out=pR[:, 0:64], in0=rbg[:, 0:64], scalar=1.0,
                    in1=pR[:, 0:64], op0=ALU.mult, op1=ALU.add),
                    waits=[(sP, sP.n)], incs=[(sV, 1)])
                p.op("scalar", lambda e: e.activation(
                    out=hsb[:], in_=pR[:, 0:64], func=ACT.Silu),
                    waits=[(sV, sV.n)], incs=[(sA, 1)])
                p.op("tensor", lambda e: e.transpose(
                    tpf[0:64, 0:128], hsb[:], eyef_s[:]),
                    waits=[(sA, sA.n)], incs=[(sP, 1)])
                p.op("scalar", lambda e: e.activation(
                    out=hT[:], in_=tpf[0:64, 0:128], func=ACT.Copy),
                    waits=[(sP, sP.n)], incs=[(sA, 1)])
                p.op("tensor", lambda e: e.matmul(
                    pR[:, 128:160], hT[:], rw2s[:], start=True, stop=True,
                    skip_group_check=True), waits=[(sA, sA.n)],
                    incs=[(sP, 1)])
                p.op("vector", lambda e: e.scalar_tensor_tensor(
                    out=lsb[:], in0=rbg[:, 64:96], scalar=1.0,
                    in1=pR[:, 128:160], op0=ALU.mult, op1=ALU.add),
                    waits=[(sP, sP.n)])
                p.op("vector", lambda e: e.tensor_tensor(
                    out=lg[:], in0=lsb[:, 0:16], in1=gt_s[:],
                    op=ALU.add))
                p.op("vector", lambda e: e.reduce_max(
                    out=st(C_M1), in_=lg[:], axis=AX.X), incs=[(sV, 1)])
                p.op("vector", lambda e: e.tensor_scalar(
                    out=eq1[:], in0=lg[:], scalar1=st(C_M1), scalar2=None,
                    op0=ALU.is_equal), waits=[(sV, sV.n)], incs=[(sV, 1)])
                p.op("vector", lambda e: e.scalar_tensor_tensor(
                    out=lgm[:], in0=eq1[:], scalar=NEGBIG, in1=lg[:],
                    op0=ALU.mult, op1=ALU.add), waits=[(sV, sV.n)],
                    incs=[(sV, 1)])
                p.op("vector", lambda e: e.reduce_max(
                    out=st(C_M2), in_=lgm[:], axis=AX.X),
                    waits=[(sV, sV.n)], incs=[(sV, 1)])
                p.op("vector", lambda e: e.tensor_scalar_mul(
                    out=st(C_NM1), in0=st(C_M1), scalar1=-1.0),
                    incs=[(sV, 1)])
                p.op("scalar", lambda e: e.activation(
                    out=st(C_E2), in_=st(C_M2), func=ACT.Exp,
                    bias=st(C_NM1)), waits=[(sV, sV.n)], incs=[(sA, 1)])
                p.op("vector", lambda e: e.tensor_scalar_add(
                    out=st(C_DEN), in0=st(C_E2), scalar1=1.0),
                    waits=[(sA, sA.n)], incs=[(sV, 1)])
                p.op("vector", lambda e: e.reciprocal(
                    out=st(C_RDEN), in_=st(C_DEN)), waits=[(sV, sV.n)],
                    incs=[(sV, 1)])
                p.op("vector", lambda e: e.tensor_scalar(
                    out=st(C_C2), in0=st(C_E2), scalar1=st(C_RDEN),
                    scalar2=None, op0=ALU.mult), waits=[(sV, sV.n)],
                    incs=[(sV, 1)])
                p.op("vector", lambda e: e.tensor_scalar(
                    out=eq2[:], in0=lg[:], scalar1=st(C_M2), scalar2=None,
                    op0=ALU.is_equal))
                p.op("vector", lambda e: e.tensor_scalar(
                    out=cf1[:], in0=eq1[:], scalar1=st(C_RDEN), scalar2=None,
                    op0=ALU.mult), incs=[(sV, 1)])
                p.op("vector", lambda e, t=t: e.scalar_tensor_tensor(
                    out=coef[:, t * 16:(t + 1) * 16], in0=eq2[:],
                    scalar=st(C_C2), in1=cf1[:], op0=ALU.mult, op1=ALU.add),
                    waits=[(sV, sV.n)], incs=[(sV, 1)])
                for pi, (lt, rt) in enumerate(((x2Th, ath), (x2Tl, ath),
                                               (x2Th, atl))):
                    for kb in range(16):
                        p.op("tensor", lambda e, pi=pi, lt=lt, rt=rt, kb=kb,
                             t=t: e.matmul(
                                 pR[:, 256:512],
                                 bass.AP(lt, kb * 1024 + t * 128,
                                         [[16 * 1024, 128], [1, 128]]),
                                 bass.AP(rt, kb * 256,
                                         [[16 * 256, 128], [1, 256]]),
                                 start=(pi == 0 and kb == 0),
                                 stop=(pi == 2 and kb == 15),
                                 skip_group_check=True),
                             waits=[(sV, sV.n)] if (pi == 0 and kb == 0)
                             else (),
                             incs=[(sP, 1)] if (pi == 2 and kb == 15)
                             else ())
                p.op("vector", lambda e, t=t: e.scalar_tensor_tensor(
                    out=wxa[:], in0=pR[:, 256:512], scalar=1.0,
                    in1=bass.AP(coef, t * 16,
                                [[NT * 16, 128], [1, 16], [0, 16]]),
                    op0=ALU.mult, op1=ALU.mult), waits=[(sP, sP.n)],
                    incs=[(sV, 1)])
                transposeN(wxa, 0, tpf,
                           bass.AP(wxTh, t * 256,
                                   [[NT * 256, 128], [128, 2], [1, 128]]),
                           2, src_wait=[(sV, sV.n), (sA, sA.n)])
                p.op("vector", lambda e, t=t: e.scalar_tensor_tensor(
                    out=bass.AP(wxTl, t * 256,
                                [[NT * 256, 128], [128, 2], [1, 128]]),
                    in0=tpf[:, 0:256], scalar=1.0,
                    in1=bass.AP(wxTh, t * 256,
                                [[NT * 256, 128], [128, 2], [1, 128]]),
                    op0=ALU.mult, op1=ALU.subtract),
                    waits=[(sA, sA.n)], incs=[(sV, 1)])

        pBig = psg(ph, "pBig", [128, 4096])
        for ot in range(4):
            evac_sA_before = sA.n
            p.op("sync", lambda e, ot=ot: e.dma_start(
                out=bvs[:], in_=biasv[:, ot * 512:(ot + 1) * 512]),
                waits=[(sV, sV.n)], incs=[(sL, 16)])
            bvs_loaded = sL.n
            for kb in range(16):
                buf, bh, bl = wmb[kb % 2], wmh[kb % 2], wml[kb % 2]
                p.op("sync", lambda e, ot=ot, kb=kb, buf=buf: e.dma_start(
                    out=buf[:], in_=W_T[kb * 128:(kb + 1) * 128,
                                        ot * 512:(ot + 1) * 512]),
                    waits=[(sV, sV.n)], incs=[(sL, 16)])
                p.op("scalar", lambda e, buf=buf, bh=bh: e.activation(
                    out=bh[:], in_=buf[:], func=ACT.Copy),
                    waits=[(sL, sL.n), (sP, sP.n)], incs=[(sA, 1)])
                p.op("vector", lambda e, buf=buf, bh=bh, bl=bl:
                     e.scalar_tensor_tensor(out=bl[:], in0=buf[:],
                                            scalar=1.0, in1=bh[:],
                                            op0=ALU.mult, op1=ALU.subtract),
                     waits=[(sA, sA.n)], incs=[(sV, 1)])
                for t in range(NT):
                    for pi, (lt, rt) in enumerate(((x2Th, bh), (x2Tl, bh),
                                                   (x2Th, bl))):
                        p.op("tensor", lambda e, t=t, pi=pi, lt=lt, rt=rt,
                             kb=kb: e.matmul(
                                 pBig[:, t * 512:(t + 1) * 512],
                                 bass.AP(lt, kb * 1024 + t * 128,
                                         [[16 * 1024, 128], [1, 128]]),
                                 rt[:], start=(pi == 0 and kb == 0),
                                 stop=False, skip_group_check=True),
                             waits=([(sV, sV.n)] +
                                    ([(sA, evac_sA_before)] if ot > 0
                                     else []))
                             if (t == 0 and pi == 0) else (),
                             incs=[(sP, 1)] if (t == NT - 1 and pi == 2)
                             else ())
            for t in range(NT):
                for pi, (lt, rt) in enumerate(((wxTh, bmh), (wxTl, bmh),
                                               (wxTh, bml))):
                    for kb2 in range(2):
                        p.op("tensor", lambda e, t=t, pi=pi, lt=lt, rt=rt,
                             kb2=kb2, ot=ot: e.matmul(
                                 pBig[:, t * 512:(t + 1) * 512],
                                 bass.AP(lt, t * 256 + kb2 * 128,
                                         [[NT * 256, 128], [1, 128]]),
                                 bass.AP(rt, kb2 * 2048 + ot * 512,
                                         [[2 * 2048, 128], [1, 512]]),
                                 start=False, stop=(pi == 2 and kb2 == 1),
                                 skip_group_check=True),
                             incs=[(sP, 1)] if (t == NT - 1 and pi == 2 and
                                                kb2 == 1) else ())
            for t in range(NT):
                ob = osb[t % 2]
                ew = [(sP, sP.n), (sG, sG.n)]
                p.op("scalar", lambda e, t=t, ob=ob: e.activation(
                    out=ob[:], in_=pBig[:, t * 512:(t + 1) * 512],
                    func=ACT.Copy), waits=ew, incs=[(sA, 1)])
                p.op("vector", lambda e, ob=ob: e.tensor_tensor(
                    out=ob[:], in0=ob[:], in1=bvs[:],
                    op=ALU.add), waits=[(sA, sA.n), (sL, bvs_loaded)])
                p.op("vector", lambda e, t=t, ot=ot, ob=ob: e.tensor_tensor(
                    out=ob[:], in0=ob[:],
                    in1=yx[:, t * D + ot * 512:t * D + ot * 512 + 512],
                    op=ALU.add), incs=[(sV, 1)])
                p.op("gpsimd", lambda e, t=t, ot=ot, ob=ob: e.dma_start(
                    out=out[t * 128:(t + 1) * 128, ot * 512:(ot + 1) * 512],
                    in_=ob[:]), waits=[(sV, sV.n)], incs=[(sG, 16)])
        p.op("gpsimd", lambda e: e.dma_start(
            out=bass.AP(ws3_dram.tensor, 0, [[3072, 1], [1, 16]]),
            in_=stats[0:1, 0:16]), waits=[(sG, sG.n)], incs=[(sG, 16)])

    with nc.Block() as block:
        @block.sync
        def _(eng):
            p.replay("sync", eng)

        @block.gpsimd
        def _(eng):
            p.replay("gpsimd", eng)

        @block.tensor
        def _(eng):
            p.replay("tensor", eng)

        @block.scalar
        def _(eng):
            p.replay("scalar", eng)

        @block.vector
        def _(eng):
            p.replay("vector", eng)

    es.close()
    return nc


def kernel(L0_attn_x, L0_attn_w_q, L0_attn_w_k, L0_attn_w_v, L0_attn_w_o,
           L0_attn_q_gain, L0_ffn_W, L0_ffn_bias, L0_ffn_rw1, L0_ffn_rb1,
           L0_ffn_rw2, L0_ffn_rb2, L0_ffn_A, L0_ffn_B, L0_ffn_gates):
    x = np.ascontiguousarray(np.asarray(L0_attn_x, dtype=np.float32))
    w_qkv = np.ascontiguousarray(np.concatenate(
        [np.asarray(L0_attn_w_q), np.asarray(L0_attn_w_k),
         np.asarray(L0_attn_w_v)], axis=0).astype(np.float32))
    w_o = np.ascontiguousarray(np.asarray(L0_attn_w_o, dtype=np.float32))
    W_T = np.ascontiguousarray(np.asarray(L0_ffn_W, dtype=np.float32).T)
    biasv = np.ascontiguousarray(np.broadcast_to(
        np.asarray(L0_ffn_bias, dtype=np.float32).reshape(1, D), (128, D)))
    rw1_T = np.ascontiguousarray(np.asarray(L0_ffn_rw1, dtype=np.float32).T)
    rw2_T = np.ascontiguousarray(np.asarray(L0_ffn_rw2, dtype=np.float32).T)
    rb1 = np.ascontiguousarray(np.broadcast_to(
        np.asarray(L0_ffn_rb1, dtype=np.float32).reshape(1, 64), (128, 64)))
    rb2 = np.ascontiguousarray(np.broadcast_to(
        np.asarray(L0_ffn_rb2, dtype=np.float32).reshape(1, 32), (128, 32)))
    gates = np.ascontiguousarray(np.broadcast_to(
        np.asarray(L0_ffn_gates, dtype=np.float32).reshape(1, 16), (128, 16)))
    qgain = np.ascontiguousarray(np.broadcast_to(
        np.asarray(L0_attn_q_gain, dtype=np.float32).reshape(1, 16),
        (128, 16)))
    A_T = np.ascontiguousarray(
        np.asarray(L0_ffn_A, dtype=np.float32).reshape(256, D).T)
    B_mov = np.ascontiguousarray(
        np.asarray(L0_ffn_B, dtype=np.float32).transpose(0, 2, 1)
        .reshape(256, D))
    eyef = np.eye(128, dtype=np.float32)
    constsv = np.ascontiguousarray(np.broadcast_to(
        np.array([[LNEPS, RMSEPS, RND, 0.0, -RND, 0, 0, 0]],
                 dtype=np.float32), (128, 8)))

    inv_freq = 1.0 / (10000.0 ** (np.arange(0, HD, 2, dtype=np.float64) / HD))
    fr = np.outer(np.arange(S, dtype=np.float64), inv_freq)
    cos_all = np.cos(fr).astype(np.float32)
    sin_all = np.sin(fr).astype(np.float32)
    cos_k = np.ascontiguousarray(np.concatenate([cos_all, cos_all], axis=1))
    sin_k = np.ascontiguousarray(np.concatenate([sin_all, -sin_all], axis=1))

    nc = build_program()

    in_maps = []
    for core in range(8):
        b, j = divmod(core, 4)
        sel = np.arange(32)[j::4]
        xq_rows = x[b].reshape(32, 128, D)[sel].reshape(TPC, D)
        pos_q = (sel[:, None] * 128 + np.arange(128)[None, :]).reshape(-1)
        dm = (np.arange(512)[None, :] >
              (np.arange(128)[:, None] + 128 * j)).astype(np.float32)
        in_maps.append({
            "x_kv": np.ascontiguousarray(x[b]),
            "x_q": np.ascontiguousarray(xq_rows),
            "w_qkv": w_qkv, "w_o": w_o, "W_T": W_T, "biasv": biasv,
            "rw1_T": rw1_T, "rw2_T": rw2_T, "rb1": rb1, "rb2": rb2,
            "gates": gates, "qgain": qgain, "A_T": A_T, "B_mov": B_mov,
            "cos_q": np.ascontiguousarray(cos_k[pos_q]),
            "sin_q": np.ascontiguousarray(sin_k[pos_q]),
            "cos_k": cos_k, "sin_k": sin_k,
            "eyef": eyef, "dmask": np.ascontiguousarray(dm),
            "constsv": constsv,
        })

    res = run_bass_kernel_spmd(nc, in_maps, core_ids=list(range(8)))
    outv = np.empty((B, S, D), dtype=np.float32)
    for core in range(8):
        b, j = divmod(core, 4)
        sel = np.arange(32)[j::4]
        outv[b].reshape(32, 128, D)[sel] = \
            res.results[core]["o"].reshape(NT, 128, D)
    return outv


if __name__ == "__main__":
    nc = build_program()
    print("built ok")


# version 07
# speedup vs baseline: 1.0509x; 1.0294x over previous
import contextlib
import numpy as np
import concourse.bass as bass
import concourse.mybir as mybir
from concourse.bass_utils import run_bass_kernel_spmd

F32 = mybir.dt.float32
BF16 = mybir.dt.bfloat16
ALU = mybir.AluOpType
AX = mybir.AxisListType
ACT = mybir.ActivationFunctionType

B, S, D = 2, 4096, 2048
H, KVH, HD = 16, 4, 128
KVD = KVH * HD
TPC, NT, NC, NCH = 1024, 8, 32, 8
RND = 12582912.0
QEPS = 1e-5
LNEPS = 1e-5
RMSEPS = float(np.finfo(np.float32).eps)
SMSCALE = float(1.0 / np.sqrt(np.float32(128.0)))
NEGBIG = -1.0e30


class Sem:
    def __init__(self, es, nc, name):
        self.h = es.enter_context(nc.semaphore(name))
        self.n = 0


class Prog:

    def __init__(self):
        self.ops = []
        self.chain_sems = ()
        self.auto_chain = {}

    def op(self, eng, emit, waits=(), incs=()):
        waits = list(waits)
        incs = list(incs)
        for s, amt in incs:
            if s in self.chain_sems:
                waits.append((s, s.n))
        ac = self.auto_chain.get(eng)
        if ac is not None:
            waits.append((ac, max(0, ac.n - 1)))
            if not any(s is ac for s, _ in incs):
                incs.append((ac, 1))
        for s, amt in incs:
            s.n += amt
        self.ops.append((eng, tuple(waits), emit, tuple(incs)))

    def replay(self, eng_name, eng):
        for name, waits, emit, incs in self.ops:
            if name != eng_name:
                continue
            for s, v in waits:
                eng.wait_ge(s.h, v)
            ins = emit(eng)
            for s, amt in incs:
                ins = ins.then_inc(s.h, amt)


def build_program(debug=False, stop_early=False):
    nc = bass.Bass("TRN2", target_bir_lowering=False)

    def din(name, shape, dtype=F32):
        return nc.dram_tensor(name, shape, dtype, kind="ExternalInput").ap()

    x_kv = din("x_kv", [S, D])
    x_q = din("x_q", [TPC, D])
    w_qkv = din("w_qkv", [3072, D])
    w_o = din("w_o", [D, D])
    W_T = din("W_T", [D, D])
    biasv = din("biasv", [128, D])
    rw1_T = din("rw1_T", [D, 64])
    rw2_T = din("rw2_T", [64, 32])
    rb1 = din("rb1", [128, 64])
    rb2 = din("rb2", [128, 32])
    gates = din("gates", [128, 16])
    qgain = din("qgain", [128, 16])
    A_T = din("A_T", [D, 256])
    B_mov = din("B_mov", [256, D])
    cos_q = din("cos_q", [TPC, HD])
    sin_q = din("sin_q", [TPC, HD])
    cos_k = din("cos_k", [S, HD])
    sin_k = din("sin_k", [S, HD])
    eyef = din("eyef", [128, 128])
    constsv = din("constsv", [128, 8])
    dmask = din("dmask", [128, 512])
    out = nc.dram_tensor("o", [TPC, D], F32, kind="ExternalOutput").ap()

    kk = dict(kind="ExternalOutput") if debug else {}
    K_scr = nc.dram_tensor("K_scr", [KVH, S, HD], F32, **kk).ap()
    V_scr = nc.dram_tensor("V_scr", [KVH, S, HD], F32, **kk).ap()
    Q_scr = nc.dram_tensor("Q_scr", [H, TPC, HD], F32, **kk).ap()
    if debug:
        dbg_y = nc.dram_tensor("dbg_y", [TPC, D], F32,
                               kind="ExternalOutput").ap()
        dbg_ws = nc.dram_tensor("dbg_ws", [128, 3072], F32,
                                kind="ExternalOutput").ap()
        dbg_wkv = nc.dram_tensor("dbg_wkv", [128, 16 * 1024], BF16,
                                 kind="ExternalOutput").ap()
        dbg_xq = nc.dram_tensor("dbg_xq", [128, D], F32,
                                kind="ExternalOutput").ap()
        dbg_kvd = nc.dram_tensor("dbg_kvd", [128, 1024], F32,
                                 kind="ExternalOutput").ap()
        dbg_xb = nc.dram_tensor("dbg_xb", [128, D], F32,
                                kind="ExternalOutput").ap()
        dbg_x1 = nc.dram_tensor("dbg_x1", [128, D], F32,
                                kind="ExternalOutput").ap()
        dbg_tm = nc.dram_tensor("dbg_tm", [128, D], F32,
                                kind="ExternalOutput").ap()
        dbg_xqt = nc.dram_tensor("dbg_xqt", [128, D], BF16,
                                 kind="ExternalOutput").ap()
        dbg_st = nc.dram_tensor("dbg_st", [128, 64], F32,
                                kind="ExternalOutput").ap()
    ws3_dram = nc.dram_tensor("ws3_dram", [1, 3072], F32).ap()
    wso_dram = nc.dram_tensor("wso_dram", [1, D], F32).ap()

    es = contextlib.ExitStack()
    p = Prog()
    sL = Sem(es, nc, "sL")
    sG = Sem(es, nc, "sG")
    sP = Sem(es, nc, "sP")
    sA = Sem(es, nc, "sA")
    sV = Sem(es, nc, "sV")
    sB = Sem(es, nc, "sB")
    p.chain_sems = (sL, sG)
    p.auto_chain = {"vector": sV, "scalar": sA}

    def sbg(st_, name, shape, dtype=F32):
        return st_.enter_context(nc.sbuf_tensor(name, shape, dtype))

    def psg(st_, name, shape, dtype=F32):
        return st_.enter_context(nc.psum_tensor(name, shape, dtype))

    yx = sbg(es, "yx", [128, NT * D])
    eyef_s = sbg(es, "eyef_s", [128, 128])
    stats = sbg(es, "stats", [128, 64])
    (C_S1, C_MU, C_S2, C_VAR, C_RSTD, C_AM, C_QS, C_RA, C_MU2, C_M1, C_M2,
     C_NM1, C_E2, C_DEN, C_RDEN, C_C2) = range(16)
    asc_kv = sbg(es, "asc_kv", [128, NC])
    asc_q = sbg(es, "asc_q", [128, NT])
    asc_y = sbg(es, "asc_y", [128, NT])
    qg_s = sbg(es, "qg_s", [128, 16])
    gt_s = sbg(es, "gt_s", [128, 16])
    csts = sbg(es, "csts", [128, 8])
    nc.const_aps.register(F32, LNEPS, csts[:, 0:1])
    nc.const_aps.register(F32, RMSEPS, csts[:, 1:2])
    nc.const_aps.register(F32, RND, csts[:, 2:3])
    nc.const_aps.register(F32, 0.0, csts[:, 3:4])
    nc.const_aps.register(F32, -RND, csts[:, 4:5])
    phWS = contextlib.ExitStack()
    ws3x = sbg(phWS, "ws3x", [128, 3072])

    p.op("vector", lambda e: e.memset(stats[:], 0.0), incs=[(sV, 1)])
    p.op("sync", lambda e: e.dma_start(out=eyef_s[:], in_=eyef), incs=[(sL, 16)])
    p.op("sync", lambda e: e.dma_start(out=csts[:], in_=constsv),
         incs=[(sL, 16)])
    p.op("sync", lambda e: e.dma_start(out=qg_s[:], in_=qgain),
         incs=[(sL, 16)])
    p.op("sync", lambda e: e.dma_start(out=gt_s[:], in_=gates),
         incs=[(sL, 16)])

    def st(c):
        return stats[:, c:c + 1]

    def bcast(t, off, n):
        return bass.AP(t, off, [[0, 128], [1, n]])

    def barrier():
        bw = [(sP, sP.n), (sL, sL.n), (sG, sG.n), (sA, sA.n), (sV, sV.n)]
        p.op("scalar", lambda e: e.activation(
            out=stats[:, 62:63], in_=stats[:, 62:63], func=ACT.Copy),
            waits=bw)
        p.op("vector", lambda e: e.tensor_copy(
            out=stats[:, 61:62], in_=stats[:, 61:62]), waits=bw)
        return list(bw) + [(sA, sA.n), (sV, sV.n)]

    def ln_quant(xt, tmpa, xq, asc_ap, extra_wait=(), dbg_mid=False):
        p.op("vector", lambda e: e.tensor_reduce(
            out=st(C_S1), in_=xt[:], axis=AX.X, op=ALU.add),
            waits=list(extra_wait) + [(sA, sA.n)], incs=[(sV, 1)])
        p.op("scalar", lambda e: e.activation(
            out=tmpa[:], in_=xt[:], func=ACT.Square, accum_out=st(C_S2)),
            waits=list(extra_wait) + [(sV, sV.n)])
        p.op("scalar", lambda e: e.activation(
            out=st(C_MU), in_=st(C_S1), func=ACT.Copy, scale=1.0 / D),
            waits=[(sV, sV.n)], incs=[(sA, 1)])
        p.op("scalar", lambda e: e.activation(
            out=st(C_MU2), in_=st(C_MU), func=ACT.Square),
            waits=[(sA, sA.n)], incs=[(sA, 1)])
        p.op("vector", lambda e: e.tensor_scalar(
            out=st(C_VAR), in0=st(C_S2), scalar1=1.0 / D, scalar2=st(C_MU2),
            op0=ALU.mult, op1=ALU.subtract), waits=[(sA, sA.n)],
            incs=[(sV, 1)])
        p.op("scalar", lambda e: e.activation(
            out=st(C_RSTD), in_=st(C_VAR), func=ACT.Sqrt, bias=LNEPS),
            waits=[(sV, sV.n)], incs=[(sA, 1)])
        p.op("vector", lambda e: e.reciprocal(
            out=st(C_RSTD), in_=st(C_RSTD)), waits=[(sA, sA.n)],
            incs=[(sV, 1)])
        p.op("vector", lambda e: e.tensor_scalar(
            out=tmpa[:], in0=xt[:], scalar1=st(C_MU), scalar2=st(C_RSTD),
            op0=ALU.subtract, op1=ALU.mult), waits=[(sV, sV.n)])
        p.op("vector", lambda e: e.reduce_max(
            out=st(C_AM), in_=tmpa[:], axis=AX.X, apply_absolute_value=True),
            incs=[(sV, 1)])
        p.op("vector", lambda e: e.tensor_scalar(
            out=st(C_QS), in0=st(C_AM), scalar1=1.0 / 127.0, scalar2=QEPS,
            op0=ALU.mult, op1=ALU.max), waits=[(sV, sV.n)], incs=[(sV, 1)])
        p.op("vector", lambda e: e.tensor_copy(out=asc_ap, in_=st(C_QS)),
             waits=[(sV, sV.n)])
        p.op("vector", lambda e: e.reciprocal(out=st(C_RA), in_=st(C_QS)),
             incs=[(sV, 1)])
        p.op("scalar", lambda e: e.activation(
            out=xq[:], in_=tmpa[:], func=ACT.Copy, scale=st(C_RA), bias=RND),
            waits=[(sV, sV.n)], incs=[(sA, 1)])
        if dbg_mid:
            p.op("gpsimd", lambda e: e.dma_start(out=dbg_x1, in_=xq[:]),
                 waits=[(sA, sA.n)], incs=[(sG, 16)])
            p.op("vector", lambda e: e.tensor_copy(
                out=st(C_S1), in_=st(C_S1)), waits=[(sG, sG.n)],
                incs=[(sV, 1)])
        p.op("scalar", lambda e: e.activation(
            out=tmpa[:], in_=xq[:], func=ACT.Copy, bias=-RND),
            waits=[(sA, sA.n), (sV, sV.n)], incs=[(sA, 1)])
        if dbg_mid:
            p.op("gpsimd", lambda e: e.dma_start(out=dbg_tm, in_=tmpa[:]),
                 waits=[(sA, sA.n)], incs=[(sG, 16)])
            p.op("vector", lambda e: e.tensor_copy(
                out=st(C_S1), in_=st(C_S1)), waits=[(sG, sG.n)],
                incs=[(sV, 1)])
        p.op("scalar", lambda e: e.activation(
            out=xq[:], in_=tmpa[:], func=ACT.Copy),
            waits=[(sV, sV.n)], incs=[(sA, 1)])
        p.op("vector", lambda e: e.tensor_copy(
            out=st(C_S1), in_=st(C_S1)), waits=[(sA, sA.n)], incs=[(sV, 1)])

    def quant_w_tile(wbuf, wtmp, wqb, ws_dram_ap, ws_col, rowlen):
        p.op("vector", lambda e: e.reduce_max(
            out=st(C_AM), in_=wbuf[:], axis=AX.X, apply_absolute_value=True),
            waits=[(sL, sL.n), (sG, sG.n), (sA, sA.n), (sV, sV.n)],
            incs=[(sV, 1)])
        p.op("vector", lambda e: e.tensor_scalar(
            out=st(C_QS), in0=st(C_AM), scalar1=1.0 / 127.0, scalar2=QEPS,
            op0=ALU.mult, op1=ALU.max), waits=[(sV, sV.n)], incs=[(sV, 1)])
        p.op("vector", lambda e: e.reciprocal(out=st(C_RA), in_=st(C_QS)),
             waits=[(sV, sV.n)], incs=[(sV, 1)])
        p.op("gpsimd", lambda e: e.dma_start(
            out=bass.AP(ws_dram_ap.tensor, ws_col,
                        [[rowlen, 1], [1, 128]]), in_=st(C_QS)),
            waits=[(sV, sV.n)], incs=[(sG, 16)])
        p.op("scalar", lambda e: e.activation(
            out=wtmp[:], in_=wbuf[:], func=ACT.Copy, scale=st(C_RA),
            bias=RND), waits=[(sV, sV.n)], incs=[(sA, 1)])
        p.op("scalar", lambda e: e.activation(
            out=wqb[:], in_=wtmp[:], func=ACT.Copy, bias=-RND),
            waits=[(sA, sA.n)], incs=[(sA, 1)])
        p.op("vector", lambda e: e.tensor_copy(
            out=st(C_S1), in_=st(C_S1)), waits=[(sA, sA.n)], incs=[(sV, 1)])

    def transposeN(src, col0, tp, dst_ap, nkb, src_wait):
        for kb in range(nkb):
            p.op("tensor", lambda e, kb=kb: e.transpose(
                tp[:, kb * 128:(kb + 1) * 128],
                src[:, col0 + kb * 128:col0 + (kb + 1) * 128], eyef_s[:]),
                waits=list(src_wait) if kb == 0 else (),
                incs=[(sP, 1)] if kb == nkb - 1 else ())
        p.op("scalar", lambda e: e.activation(
            out=dst_ap, in_=tp[:, 0:nkb * 128], func=ACT.Copy),
            waits=[(sP, sP.n)], incs=[(sA, 1)])

    def rope(src, dst, sc, rp, nh):
        cosx = bass.AP(rp, 0, [[2 * HD, 128], [0, nh], [1, HD]])
        sin0 = bass.AP(rp, HD, [[2 * HD, 128], [0, nh], [1, 64]])
        sin1 = bass.AP(rp, HD + 64, [[2 * HD, 128], [0, nh], [1, 64]])
        w_ = nh * HD

        def v3(t, off, inner):
            return bass.AP(t, off, [[w_ if t is not sc else nh * 64, 128],
                                    [HD if t is not sc else 64, nh],
                                    [1, inner]])

        scv = bass.AP(sc, 0, [[nh * 64, 128], [64, nh], [1, 64]])
        p.op("vector", lambda e: e.tensor_tensor(
            out=v3(dst, 0, HD), in0=v3(src, 0, HD), in1=cosx, op=ALU.mult))
        p.op("vector", lambda e: e.tensor_tensor(
            out=scv, in0=v3(src, 64, 64), in1=sin0, op=ALU.mult))
        p.op("vector", lambda e: e.tensor_tensor(
            out=v3(dst, 0, 64), in0=v3(dst, 0, 64), in1=scv, op=ALU.add))
        p.op("vector", lambda e: e.tensor_tensor(
            out=scv, in0=v3(src, 0, 64), in1=sin1, op=ALU.mult))
        p.op("vector", lambda e: e.tensor_tensor(
            out=v3(dst, 64, 64), in0=v3(dst, 64, 64), in1=scv, op=ALU.add),
            incs=[(sV, 1)])

    phKV = contextlib.ExitStack()
    wkvT = sbg(phKV, "wkvT", [128, 16 * 1024], BF16)
    with contextlib.ExitStack() as ph:
        wt = [sbg(ph, "wt0", [128, D]), sbg(ph, "wt1", [128, D])]
        wtmp = sbg(ph, "wtmp", [128, D])
        wqb = sbg(ph, "wqb", [128, D])
        tpw = psg(ph, "tpw", [128, 2048])
        for r in range(8):
            buf = wt[r % 2]
            p.op("sync", lambda e, r=r, buf=buf: e.dma_start(
                out=buf[:], in_=w_qkv[D + r * 128:D + (r + 1) * 128, :]),
                waits=[(sV, sV.n)] if r >= 2 else [], incs=[(sL, 16)])
            quant_w_tile(buf, wtmp, wqb, ws3_dram, 2048 + r * 128, 3072)
            transposeN(wqb, 0, tpw,
                       bass.AP(wkvT, r * 128,
                               [[16 * 1024, 128], [1024, 16], [1, 128]]),
                       16, src_wait=[(sV, sV.n), (sA, sA.n)])

    barrier()
    if debug:
        p.op("gpsimd", lambda e: e.dma_start(out=dbg_wkv, in_=wkvT[:]),
             waits=[(sA, sA.n)], incs=[(sG, 16)])
    p.op("gpsimd", lambda e: e.dma_start(
        out=ws3x[:, 2048:3072],
        in_=bass.AP(ws3_dram.tensor, 2048, [[3072, 1], [0, 128], [1, 1024]])),
        waits=[(sG, sG.n)], incs=[(sG, 16)])
    ws3_rep1 = sG.n

    with contextlib.ExitStack() as ph:
        xt = [sbg(ph, "xt0", [128, D]), sbg(ph, "xt1", [128, D])]
        rop = [sbg(ph, "rop0", [128, 2 * HD]), sbg(ph, "rop1", [128, 2 * HD])]
        tmpa = sbg(ph, "tmpa", [128, D])
        xqb = sbg(ph, "xqb", [128, D])
        xqT = sbg(ph, "xqT", [128, D], BF16)
        kvd = sbg(ph, "kvd", [128, 2 * KVD])
        krm = sbg(ph, "krm", [128, KVD])
        kro = sbg(ph, "kro", [128, KVD])
        ksc = sbg(ph, "ksc", [128, KVH * 64])
        tpx = psg(ph, "tpx", [128, 2048])
        pKV = psg(ph, "pKV", [128, 1024])
        for t in range(1 if stop_early else NC):
            xb, rp = xt[t % 2], rop[t % 2]
            p.op("sync", lambda e, t=t, xb=xb: e.dma_start(
                out=xb[:], in_=x_kv[t * 128:(t + 1) * 128, :]),
                waits=[(sV, sV.n)] if t >= 2 else
                [(sP, sP.n), (sA, sA.n)], incs=[(sL, 16)])
            p.op("sync", lambda e, t=t, rp=rp: e.dma_start(
                out=rp[:, 0:HD], in_=cos_k[t * 128:(t + 1) * 128, :]),
                incs=[(sL, 16)])
            p.op("sync", lambda e, t=t, rp=rp: e.dma_start(
                out=rp[:, HD:2 * HD], in_=sin_k[t * 128:(t + 1) * 128, :]),
                incs=[(sL, 16)])
            ln_quant(xb, tmpa, xqb, asc_kv[:, t:t + 1],
                     extra_wait=[(sL, sL.n)], dbg_mid=(debug and t == 0))
            if debug and t == 0:
                p.op("gpsimd", lambda e, xqb=xqb: e.dma_start(
                    out=dbg_xq, in_=xqb[:]),
                     waits=[(sV, sV.n)], incs=[(sG, 16)])
                p.op("gpsimd", lambda e, xb=xb: e.dma_start(out=dbg_xb,
                                                            in_=xb[:]),
                     incs=[(sG, 16)])
                p.op("gpsimd", lambda e: e.dma_start(out=dbg_st,
                                                     in_=stats[:]),
                     incs=[(sG, 16)])
            transposeN(xqb, 0, tpx, bass.AP(xqT, 0, [[D, 128], [1, D]]), 16,
                       src_wait=[(sV, sV.n), (sA, sA.n)])
            if debug and t == 0:
                p.op("gpsimd", lambda e, xqT=xqT: e.dma_start(
                    out=dbg_xqt, in_=xqT[:]),
                     waits=[(sA, sA.n)], incs=[(sG, 16)])
                p.op("vector", lambda e: e.tensor_copy(
                    out=st(C_S1), in_=st(C_S1)), waits=[(sG, sG.n)],
                    incs=[(sV, 1)])
            for ot in range(2):
                for kb in range(16):
                    p.op("tensor", lambda e, ot=ot, kb=kb, xqT=xqT,
                         wkvT=wkvT: e.matmul(
                        pKV[:, ot * 512:(ot + 1) * 512],
                        bass.AP(xqT, kb * 128, [[D, 128], [1, 128]]),
                        bass.AP(wkvT, kb * 1024 + ot * 512,
                                [[16 * 1024, 128], [1, 512]]),
                        start=(kb == 0), stop=(kb == 15),
                        skip_group_check=True),
                        waits=[(sA, sA.n)] if (ot == 0 and kb == 0) else (),
                        incs=[(sP, 1)] if (ot == 1 and kb == 15) else ())
            p.op("scalar", lambda e, t=t: e.activation(
                out=kvd[:], in_=pKV[:], func=ACT.Copy,
                scale=asc_kv[:, t:t + 1]), waits=[(sP, sP.n)], incs=[(sA, 1)])
            dq_wait = [(sA, sA.n)]
            if t == 0:
                dq_wait.append((sG, ws3_rep1))
            p.op("vector", lambda e: e.tensor_tensor(
                out=kvd[:], in0=kvd[:], in1=ws3x[:, 2048:3072],
                op=ALU.mult), waits=dq_wait, incs=[(sV, 1)])
            if debug and t == 0:
                p.op("gpsimd", lambda e, kvd=kvd: e.dma_start(
                    out=dbg_kvd, in_=kvd[:]),
                     waits=[(sV, sV.n)], incs=[(sG, 16)])
            p.op("scalar", lambda e: e.activation(
                out=krm[:], in_=kvd[:, 0:KVD], func=ACT.Square),
                waits=[(sV, sV.n)], incs=[(sA, 1)])
            p.op("vector", lambda e: e.tensor_reduce(
                out=stats[:, 16:20],
                in_=bass.AP(krm, 0, [[KVD, 128], [HD, KVH], [1, HD]]),
                axis=AX.X, op=ALU.add), waits=[(sA, sA.n)], incs=[(sV, 1)])
            p.op("scalar", lambda e: e.activation(
                out=stats[:, 20:24], in_=stats[:, 16:20], func=ACT.Sqrt,
                scale=1.0 / HD, bias=RMSEPS), waits=[(sV, sV.n)],
                incs=[(sA, 1)])
            p.op("vector", lambda e: e.reciprocal(
                out=stats[:, 20:24], in_=stats[:, 20:24]),
                waits=[(sA, sA.n)], incs=[(sV, 1)])
            p.op("vector", lambda e: e.tensor_tensor(
                out=krm[:], in0=kvd[:, 0:KVD],
                in1=bass.AP(stats, 20, [[64, 128], [1, KVH], [0, HD]]),
                op=ALU.mult), waits=[(sV, sV.n)])
            rope(krm, kro, ksc, rp, KVH)
            for kh in range(KVH):
                p.op("gpsimd", lambda e, t=t, kh=kh: e.dma_start(
                    out=K_scr[kh, t * 128:(t + 1) * 128, :],
                    in_=kro[:, kh * HD:(kh + 1) * HD]),
                    waits=[(sV, sV.n)] if kh == 0 else (), incs=[(sG, 16)])
                p.op("gpsimd", lambda e, t=t, kh=kh: e.dma_start(
                    out=V_scr[kh, t * 128:(t + 1) * 128, :],
                    in_=kvd[:, KVD + kh * HD:KVD + (kh + 1) * HD]),
                    incs=[(sG, 16)])
            p.op("vector", lambda e: e.tensor_copy(
                out=st(C_S1), in_=st(C_S1)), waits=[(sG, sG.n)],
                incs=[(sV, 1)])

    phKV.close()

    with contextlib.ExitStack() as ph:
        wqT = sbg(ph, "wqT", [128, 16 * 2048], BF16)
        with contextlib.ExitStack() as phw:
            wt = [sbg(phw, "wt0b", [128, D]), sbg(phw, "wt1b", [128, D])]
            wtmp = sbg(phw, "wtmpb", [128, D])
            wqb = sbg(phw, "wqbb", [128, D])
            tpw = psg(phw, "tpwb", [128, 2048])
            for r in range(16):
                buf = wt[r % 2]
                p.op("sync", lambda e, r=r, buf=buf: e.dma_start(
                    out=buf[:], in_=w_qkv[r * 128:(r + 1) * 128, :]),
                    waits=[(sV, sV.n)] if r >= 2 else
                    [(sP, sP.n), (sG, sG.n), (sA, sA.n)], incs=[(sL, 16)])
                quant_w_tile(buf, wtmp, wqb, ws3_dram, r * 128, 3072)
                transposeN(wqb, 0, tpw,
                           bass.AP(wqT, r * 128,
                                   [[16 * 2048, 128], [2048, 16], [1, 128]]),
                           16, src_wait=[(sV, sV.n), (sA, sA.n)])

        barrier()
        p.op("gpsimd", lambda e: e.dma_start(
            out=ws3x[:, 0:2048],
            in_=bass.AP(ws3_dram.tensor, 0,
                        [[3072, 1], [0, 128], [1, 2048]])),
            waits=[(sG, sG.n)], incs=[(sG, 16)])
        ws3_rep2 = sG.n

        xt = [sbg(ph, "xq0", [128, D]), sbg(ph, "xq1", [128, D])]
        rop = [sbg(ph, "ropq0", [128, 2 * HD]),
               sbg(ph, "ropq1", [128, 2 * HD])]
        tmpa = sbg(ph, "tmpaq", [128, D])
        xqb = sbg(ph, "xqbq", [128, D])
        xqT = sbg(ph, "xqTq", [128, D], BF16)
        qd = sbg(ph, "qd", [128, D])
        qn = sbg(ph, "qn", [128, D])
        qro = sbg(ph, "qro", [128, D])
        qsc = sbg(ph, "qsc", [128, H * 64])
        tpx = psg(ph, "tpxq", [128, 2048])
        pQ = psg(ph, "pQ", [128, 2048])
        for t in range(NT):
            xb, rp = xt[t % 2], rop[t % 2]
            p.op("sync", lambda e, t=t, xb=xb: e.dma_start(
                out=xb[:], in_=x_q[t * 128:(t + 1) * 128, :]),
                waits=[(sV, sV.n)] if t >= 2 else [(sV, sV.n), (sA, sA.n)],
                incs=[(sL, 16)])
            p.op("sync", lambda e, t=t, rp=rp: e.dma_start(
                out=rp[:, 0:HD], in_=cos_q[t * 128:(t + 1) * 128, :]),
                incs=[(sL, 16)])
            p.op("sync", lambda e, t=t, rp=rp: e.dma_start(
                out=rp[:, HD:2 * HD], in_=sin_q[t * 128:(t + 1) * 128, :]),
                incs=[(sL, 16)])
            ln_quant(xb, tmpa, xqb, asc_q[:, t:t + 1],
                     extra_wait=[(sL, sL.n)])
            transposeN(xqb, 0, tpx, bass.AP(xqT, 0, [[D, 128], [1, D]]), 16,
                       src_wait=[(sV, sV.n), (sA, sA.n)])
            for ot in range(4):
                for kb in range(16):
                    p.op("tensor", lambda e, ot=ot, kb=kb: e.matmul(
                        pQ[:, ot * 512:(ot + 1) * 512],
                        bass.AP(xqT, kb * 128, [[D, 128], [1, 128]]),
                        bass.AP(wqT, kb * 2048 + ot * 512,
                                [[16 * 2048, 128], [1, 512]]),
                        start=(kb == 0), stop=(kb == 15),
                        skip_group_check=True),
                        waits=[(sA, sA.n)] if (ot == 0 and kb == 0) else (),
                        incs=[(sP, 1)] if (ot == 3 and kb == 15) else ())
            p.op("scalar", lambda e, t=t: e.activation(
                out=qd[:], in_=pQ[:], func=ACT.Copy,
                scale=asc_q[:, t:t + 1]), waits=[(sP, sP.n)], incs=[(sA, 1)])
            dq_wait = [(sA, sA.n)]
            if t == 0:
                dq_wait.append((sG, ws3_rep2))
            p.op("vector", lambda e: e.tensor_tensor(
                out=qd[:], in0=qd[:], in1=ws3x[:, 0:2048], op=ALU.mult),
                waits=dq_wait, incs=[(sV, 1)])
            p.op("scalar", lambda e: e.activation(
                out=tmpa[:], in_=qd[:], func=ACT.Square),
                waits=[(sV, sV.n)], incs=[(sA, 1)])
            p.op("vector", lambda e: e.tensor_reduce(
                out=stats[:, 16:32],
                in_=bass.AP(tmpa, 0, [[D, 128], [HD, H], [1, HD]]),
                axis=AX.X, op=ALU.add), waits=[(sA, sA.n)], incs=[(sV, 1)])
            p.op("scalar", lambda e: e.activation(
                out=stats[:, 32:48], in_=stats[:, 16:32], func=ACT.Sqrt,
                scale=1.0 / HD, bias=RMSEPS), waits=[(sV, sV.n)],
                incs=[(sA, 1)])
            p.op("vector", lambda e: e.reciprocal(
                out=stats[:, 32:48], in_=stats[:, 32:48]),
                waits=[(sA, sA.n)], incs=[(sV, 1)])
            p.op("vector", lambda e: e.tensor_tensor(
                out=stats[:, 32:48], in0=stats[:, 32:48],
                in1=qg_s[:], op=ALU.mult), waits=[(sV, sV.n)],
                incs=[(sV, 1)])
            p.op("vector", lambda e: e.tensor_tensor(
                out=qn[:], in0=qd[:],
                in1=bass.AP(stats, 32, [[64, 128], [1, H], [0, HD]]),
                op=ALU.mult), waits=[(sV, sV.n)])
            rope(qn, qro, qsc, rp, H)
            for h in range(H):
                p.op("gpsimd", lambda e, t=t, h=h: e.dma_start(
                    out=Q_scr[h, t * 128:(t + 1) * 128, :],
                    in_=qro[:, h * HD:(h + 1) * HD]),
                    waits=[(sV, sV.n)] if h == 0 else (), incs=[(sG, 16)])
            p.op("vector", lambda e: e.tensor_copy(
                out=st(C_S1), in_=st(C_S1)), waits=[(sG, sG.n)],
                incs=[(sV, 1)])

    if debug:
        p.op("gpsimd", lambda e: e.dma_start(out=dbg_ws, in_=ws3x[:]),
             waits=[(sV, sV.n), (sA, sA.n)], incs=[(sG, 16)])
    barrier()
    if stop_early:
        p.op("gpsimd", lambda e: e.dma_start(
            out=bass.AP(ws3_dram.tensor, 0, [[3072, 1], [1, 16]]),
            in_=stats[0:1, 0:16]), waits=[(sG, sG.n)], incs=[(sG, 16)])
        phKV.close()
        phWS.close()
        with nc.Block() as block:
            @block.sync
            def _(eng):
                p.replay("sync", eng)

            @block.gpsimd
            def _(eng):
                p.replay("gpsimd", eng)

            @block.tensor
            def _(eng):
                p.replay("tensor", eng)

            @block.scalar
            def _(eng):
                p.replay("scalar", eng)

            @block.vector
            def _(eng):
                p.replay("vector", eng)
        es.close()
        return nc
    barrier()
    phWS.close()

    with contextlib.ExitStack() as ph:
        dmask_s = sbg(ph, "dmask_s", [128, 512])
        p.op("sync", lambda e: e.dma_start(out=dmask_s[:], in_=dmask),
             waits=[(sV, sV.n), (sP, sP.n), (sG, sG.n)], incs=[(sL, 16)])
        qsb = sbg(ph, "qsb", [128, NT * HD])
        ksb = [sbg(ph, "ksb0", [128, 512]), sbg(ph, "ksb1", [128, 512])]
        vsb = [sbg(ph, "vsb0", [128, 512]), sbg(ph, "vsb1", [128, 512])]
        QT_hi = sbg(ph, "QT_hi", [128, TPC], BF16)
        QT_lo = sbg(ph, "QT_lo", [128, TPC], BF16)
        KT_hi = sbg(ph, "KT_hi", [128, 512], BF16)
        KT_lo = sbg(ph, "KT_lo", [128, 512], BF16)
        V_hi = sbg(ph, "V_hi", [128, 512], BF16)
        V_lo = sbg(ph, "V_lo", [128, 512], BF16)
        exs = [sbg(ph, "exs0", [128, 512]), sbg(ph, "exs1", [128, 512])]
        eT_hi = sbg(ph, "eT_hi", [128, 4 * TPC], BF16)
        eT_lo = sbg(ph, "eT_lo", [128, 4 * TPC], BF16)
        rsum = sbg(ph, "rsum", [128, 64])
        rtot = sbg(ph, "rtot", [128, NT])
        rrec = sbg(ph, "rrec", [128, NT])
        ysb = sbg(ph, "ysb", [128, TPC])
        pT1 = psg(ph, "pT1", [128, 1024])
        pE = psg(ph, "pE", [128, 512])
        pS = psg(ph, "pS", [128, 1024])
        pY = psg(ph, "pY", [128, 1024])

        for h in range(H):
            kh = h // (H // KVH)
            qw = [(sP, sP.n), (sA, sA.n)]
            if h == 0:
                qw.append((sG, sG.n))
            p.op("sync", lambda e, h=h: e.dma_start(
                out=qsb[:],
                in_=Q_scr[h].rearrange("(a p) d -> p a d", p=128)),
                waits=qw, incs=[(sL, 16)])
            for g in range(2):
                transposeN(qsb, g * 512, pS,
                           bass.AP(QT_hi, g * 512, [[TPC, 128], [1, 512]]),
                           4, src_wait=[(sL, sL.n), (sV, sV.n), (sA, sA.n)])
                p.op("vector", lambda e, g=g: e.scalar_tensor_tensor(
                    out=bass.AP(QT_lo, g * 512, [[TPC, 128], [1, 512]]),
                    in0=pS[:, 0:512], scalar=1.0,
                    in1=bass.AP(QT_hi, g * 512, [[TPC, 128], [1, 512]]),
                    op0=ALU.mult, op1=ALU.subtract),
                    waits=[(sA, sA.n)], incs=[(sV, 1)])
            av_done_pe = sP.n
            for c in range(NCH):
                kb_, vb_ = ksb[c % 2], vsb[c % 2]
                p.op("sync", lambda e, c=c, kh=kh, kb_=kb_: e.dma_start(
                    out=kb_[:],
                    in_=K_scr[kh, c * 512:(c + 1) * 512, :].rearrange(
                        "(a p) d -> p a d", p=128)),
                    waits=[(sP, av_done_pe)], incs=[(sL, 16)])
                p.op("gpsimd", lambda e, c=c, kh=kh, vb_=vb_: e.dma_start(
                    out=vb_[:],
                    in_=V_scr[kh, c * 512:(c + 1) * 512, :].rearrange(
                        "(a p) d -> p a d", p=128)),
                    waits=[(sP, av_done_pe)], incs=[(sG, 16)])
                transposeN(kb_, 0, pT1,
                           bass.AP(KT_hi, 0, [[512, 128], [1, 512]]), 4,
                           src_wait=[(sL, sL.n), (sV, sV.n), (sA, sA.n)])
                p.op("vector", lambda e: e.scalar_tensor_tensor(
                    out=KT_lo[:], in0=pT1[:, 0:512], scalar=1.0,
                    in1=KT_hi[:], op0=ALU.mult, op1=ALU.subtract),
                    waits=[(sA, sA.n)], incs=[(sV, 1)])
                p.op("scalar", lambda e, vb_=vb_: e.activation(
                    out=V_hi[:], in_=vb_[:], func=ACT.Copy),
                    waits=[(sG, sG.n), (sP, av_done_pe)], incs=[(sA, 1)])
                p.op("vector", lambda e, vb_=vb_: e.scalar_tensor_tensor(
                    out=V_lo[:], in0=vb_[:], scalar=1.0, in1=V_hi[:],
                    op0=ALU.mult, op1=ALU.subtract),
                    waits=[(sA, sA.n)], incs=[(sV, 1)])
                exp_done = {}
                for i in range(c, NT):
                    sl = pS[:, (i % 2) * 512:(i % 2) * 512 + 512]
                    tpe = pT1[:, 512:1024] if i % 2 == 0 else pE[:, 0:512]
                    wt_ = []
                    if i <= c + 1:
                        wt_ += [(sA, sA.n), (sV, sV.n)]
                    elif (i - 2) in exp_done:
                        wt_ += [(sA, exp_done[i - 2][0]),
                                (sV, exp_done[i - 2][1])]
                    for pi, (lt, rt) in enumerate(
                            ((QT_hi, KT_hi), (QT_lo, KT_hi), (QT_hi, KT_lo))):
                        p.op("tensor", lambda e, sl=sl, lt=lt, rt=rt, i=i,
                             pi=pi: e.matmul(
                                 sl,
                                 bass.AP(lt, i * 128, [[TPC, 128], [1, 128]]),
                                 rt[:], start=(pi == 0), stop=(pi == 2),
                                 skip_group_check=True),
                             waits=wt_ if pi == 0 else (),
                             incs=[(sP, 1)] if pi == 2 else ())
                    if i == c:
                        p.op("vector", lambda e, sl=sl: e.scalar_tensor_tensor(
                            out=sl, in0=dmask_s[:], scalar=NEGBIG, in1=sl,
                            op0=ALU.mult, op1=ALU.add),
                            waits=[(sP, sP.n)], incs=[(sV, 1)])
                        mwait = [(sV, sV.n)]
                    else:
                        mwait = [(sP, sP.n)]
                    eb = exs[i % 2]
                    p.op("scalar", lambda e, sl=sl, eb=eb, i=i, c=c:
                         e.activation(
                             out=eb[:], in_=sl, func=ACT.Exp, scale=SMSCALE,
                             accum_out=rsum[:, i * 8 + c:i * 8 + c + 1]),
                         waits=mwait, incs=[(sA, 1)])
                    twait = [(sA, sA.n)]
                    if i <= c + 1:
                        twait.append((sV, sV.n))
                    elif (i - 2) in exp_done:
                        twait.append((sV, exp_done[i - 2][1]))
                    for s_ in range(4):
                        p.op("tensor", lambda e, tpe=tpe, eb=eb, s_=s_:
                             e.transpose(tpe[:, s_ * 128:(s_ + 1) * 128],
                                         eb[:, s_ * 128:(s_ + 1) * 128],
                                         eyef_s[:]),
                             waits=twait if s_ == 0 else (),
                             incs=[(sP, 1)] if s_ == 3 else ())
                    dst_hi = bass.AP(eT_hi, i * 128,
                                     [[4 * TPC, 128], [TPC, 4], [1, 128]])
                    dst_lo = bass.AP(eT_lo, i * 128,
                                     [[4 * TPC, 128], [TPC, 4], [1, 128]])
                    p.op("scalar", lambda e, tpe=tpe, dst_hi=dst_hi:
                         e.activation(out=dst_hi, in_=tpe[:, 0:512],
                                      func=ACT.Copy),
                         waits=[(sP, sP.n)], incs=[(sA, 1)])
                    p.op("vector", lambda e, tpe=tpe, dst_hi=dst_hi,
                         dst_lo=dst_lo: e.scalar_tensor_tensor(
                             out=dst_lo, in0=tpe[:, 0:512], scalar=1.0,
                             in1=dst_hi, op0=ALU.mult, op1=ALU.subtract),
                         waits=[(sA, sA.n)], incs=[(sV, 1)])
                    exp_done[i] = (sA.n, sV.n)
                avw = [(sA, sA.n), (sV, sV.n)]
                mm_list = []
                for s_ in range(4):
                    for (vt, et) in ((V_hi, eT_hi), (V_lo, eT_hi),
                                     (V_hi, eT_lo)):
                        segs = ([(c * 128, 512)] if c * 128 < 512 else []) + \
                               [(max(512, c * 128), 1024)]
                        for (a0, a1) in segs:
                            mm_list.append((s_, vt, et, a0, a1))
                for n_, (s_, vt, et, a0, a1) in enumerate(mm_list):
                    st_flag = (c == 0 and n_ < 2)
                    last = (n_ == len(mm_list) - 1)
                    p.op("tensor", lambda e, s_=s_, vt=vt, et=et, a0=a0,
                         a1=a1, st_flag=st_flag: e.matmul(
                             pY[:, a0:a1], vt[:, s_ * 128:(s_ + 1) * 128],
                             bass.AP(et, s_ * TPC + a0,
                                     [[4 * TPC, 128], [1, a1 - a0]]),
                             start=st_flag, stop=False,
                             skip_group_check=True),
                         waits=avw if n_ == 0 else (),
                         incs=[(sP, 1)] if last else ())
                av_done_pe = sP.n
            for i in range(NT):
                p.op("vector", lambda e, i=i: e.tensor_reduce(
                    out=rtot[:, i:i + 1],
                    in_=rsum[:, i * 8:i * 8 + i + 1], axis=AX.X, op=ALU.add),
                    waits=[(sA, sA.n)] if i == 0 else (),
                    incs=[(sV, 1)] if i == NT - 1 else ())
            p.op("vector", lambda e: e.reciprocal(out=rrec[:], in_=rtot[:]),
                 waits=[(sV, sV.n)], incs=[(sV, 1)])
            p.op("scalar", lambda e: e.activation(
                out=ysb[:, 0:512], in_=pY[:, 0:512], func=ACT.Copy),
                waits=[(sP, av_done_pe)])
            p.op("scalar", lambda e: e.activation(
                out=ysb[:, 512:1024], in_=pY[:, 512:1024], func=ACT.Copy),
                incs=[(sA, 1)])
            for i in range(NT):
                p.op("tensor", lambda e, i=i: e.transpose(
                    pS[:, i * 128:(i + 1) * 128],
                    ysb[:, i * 128:(i + 1) * 128], eyef_s[:]),
                    waits=[(sA, sA.n)] if i == 0 else (),
                    incs=[(sP, 1)] if i == NT - 1 else ())
            for i in range(NT):
                p.op("scalar", lambda e, i=i, h=h: e.activation(
                    out=yx[:, i * D + h * HD:i * D + h * HD + HD],
                    in_=pS[:, i * 128:(i + 1) * 128], func=ACT.Copy,
                    scale=rrec[:, i:i + 1]),
                    waits=[(sP, sP.n), (sV, sV.n)] if i == 0 else (),
                    incs=[(sA, 1)] if i == NT - 1 else ())

    if debug:
        for t in range(NT):
            p.op("gpsimd", lambda e, t=t: e.dma_start(
                out=dbg_y[t * 128:(t + 1) * 128, :],
                in_=yx[:, t * D:(t + 1) * D]),
                waits=[(sA, sA.n)] if t == 0 else (), incs=[(sG, 16)])

    barrier()

    with contextlib.ExitStack() as ph:
        wsox = sbg(ph, "wsox", [128, D])
        woqT = sbg(ph, "woqT", [128, 16 * 2048], BF16)
        x0b = [sbg(ph, "x0b0", [128, D]), sbg(ph, "x0b1", [128, D])]
        yq = sbg(ph, "yq", [128, D])
        yqT = sbg(ph, "yqT", [128, D], BF16)
        od = sbg(ph, "od", [128, D])
        tpw = psg(ph, "tpwc", [128, 2048])
        pO = psg(ph, "pO", [128, 2048])
        with contextlib.ExitStack() as phw:
            wt = [sbg(phw, "wt0c", [128, D]), sbg(phw, "wt1c", [128, D])]
            wtmp = sbg(phw, "wtmpc", [128, D])
            wqb = sbg(phw, "wqbc", [128, D])
            for r in range(16):
                buf = wt[r % 2]
                p.op("sync", lambda e, r=r, buf=buf: e.dma_start(
                    out=buf[:], in_=w_o[r * 128:(r + 1) * 128, :]),
                    waits=[(sV, sV.n)] if r >= 2 else
                    [(sP, sP.n), (sA, sA.n)], incs=[(sL, 16)])
                quant_w_tile(buf, wtmp, wqb, wso_dram, r * 128, 2048)
                transposeN(wqb, 0, tpw,
                           bass.AP(woqT, r * 128,
                                   [[16 * 2048, 128], [2048, 16], [1, 128]]),
                           16, src_wait=[(sV, sV.n), (sA, sA.n)])
        barrier()
        p.op("gpsimd", lambda e: e.dma_start(
            out=wsox[:],
            in_=bass.AP(wso_dram.tensor, 0, [[2048, 1], [0, 128], [1, 2048]])),
            waits=[(sG, sG.n)], incs=[(sG, 16)])
        wso_rep = sG.n
        for t in range(NT):
            xb = x0b[t % 2]
            p.op("sync", lambda e, t=t, xb=xb: e.dma_start(
                out=xb[:], in_=x_q[t * 128:(t + 1) * 128, :]),
                waits=[(sV, sV.n)] if t >= 2 else [(sV, sV.n), (sA, sA.n)],
                incs=[(sL, 16)])
            ytile = yx[:, t * D:(t + 1) * D]
            p.op("vector", lambda e, ytile=ytile: e.reduce_max(
                out=st(C_AM), in_=ytile, axis=AX.X,
                apply_absolute_value=True), waits=[(sA, sA.n)],
                incs=[(sV, 1)])
            p.op("vector", lambda e: e.tensor_scalar(
                out=st(C_QS), in0=st(C_AM), scalar1=1.0 / 127.0, scalar2=QEPS,
                op0=ALU.mult, op1=ALU.max), waits=[(sV, sV.n)],
                incs=[(sV, 1)])
            p.op("vector", lambda e, t=t: e.tensor_copy(
                out=asc_y[:, t:t + 1], in_=st(C_QS)), waits=[(sV, sV.n)])
            p.op("vector", lambda e: e.reciprocal(out=st(C_RA), in_=st(C_QS)),
                 incs=[(sV, 1)])
            p.op("scalar", lambda e, ytile=ytile: e.activation(
                out=yq[:], in_=ytile, func=ACT.Copy, scale=st(C_RA),
                bias=RND), waits=[(sV, sV.n)], incs=[(sA, 1)])
            p.op("scalar", lambda e: e.activation(
                out=od[:], in_=yq[:], func=ACT.Copy, bias=-RND),
                waits=[(sA, sA.n)], incs=[(sA, 1)])
            p.op("vector", lambda e: e.tensor_copy(
                out=st(C_S1), in_=st(C_S1)), waits=[(sA, sA.n)],
                incs=[(sV, 1)])
            transposeN(od, 0, tpw, bass.AP(yqT, 0, [[D, 128], [1, D]]), 16,
                       src_wait=[(sV, sV.n), (sA, sA.n)])
            for ot in range(4):
                for kb in range(16):
                    p.op("tensor", lambda e, ot=ot, kb=kb: e.matmul(
                        pO[:, ot * 512:(ot + 1) * 512],
                        bass.AP(yqT, kb * 128, [[D, 128], [1, 128]]),
                        bass.AP(woqT, kb * 2048 + ot * 512,
                                [[16 * 2048, 128], [1, 512]]),
                        start=(kb == 0), stop=(kb == 15),
                        skip_group_check=True),
                        waits=[(sA, sA.n)] if (ot == 0 and kb == 0) else (),
                        incs=[(sP, 1)] if (ot == 3 and kb == 15) else ())
            p.op("scalar", lambda e, t=t: e.activation(
                out=od[:], in_=pO[:], func=ACT.Copy,
                scale=asc_y[:, t:t + 1]), waits=[(sP, sP.n)], incs=[(sA, 1)])
            dq_wait = [(sA, sA.n), (sL, sL.n)]
            if t == 0:
                dq_wait.append((sG, wso_rep))
            p.op("vector", lambda e: e.tensor_tensor(
                out=od[:], in0=od[:], in1=wsox[:], op=ALU.mult),
                waits=dq_wait)
            p.op("vector", lambda e, ytile=ytile, xb=xb: e.tensor_tensor(
                out=ytile, in0=od[:], in1=xb[:], op=ALU.add), incs=[(sV, 1)])

    barrier()

    with contextlib.ExitStack() as ph:
        rw1h = sbg(ph, "rw1h", [128, 16 * 64], BF16)
        rw1l = sbg(ph, "rw1l", [128, 16 * 64], BF16)
        rw2s = sbg(ph, "rw2s", [64, 32])
        rbg = sbg(ph, "rbg", [128, 128])
        bvs = sbg(ph, "bvs", [128, 512])
        ath = sbg(ph, "ath", [128, 16 * 256], BF16)
        atl = sbg(ph, "atl", [128, 16 * 256], BF16)
        bmh = sbg(ph, "bmh", [128, 2 * D], BF16)
        bml = sbg(ph, "bml", [128, 2 * D], BF16)
        x2Th = sbg(ph, "x2Th", [128, 16 * 1024], BF16)
        x2Tl = sbg(ph, "x2Tl", [128, 16 * 1024], BF16)
        xn2 = sbg(ph, "xn2", [128, D])
        hsb = sbg(ph, "hsb", [128, 64])
        hT = sbg(ph, "hT", [64, 128])
        lsb = sbg(ph, "lsb", [128, 32])
        lg = sbg(ph, "lg", [128, 16])
        eq1 = sbg(ph, "eq1", [128, 16])
        eq2 = sbg(ph, "eq2", [128, 16])
        lgm = sbg(ph, "lgm", [128, 16])
        cf1 = sbg(ph, "cf1", [128, 16])
        coef = sbg(ph, "coef", [128, NT * 16])
        wxa = sbg(ph, "wxa", [128, 256])
        wxTh = sbg(ph, "wxTh", [128, NT * 256], BF16)
        wxTl = sbg(ph, "wxTl", [128, NT * 256], BF16)
        wmb = [sbg(ph, "wmb0", [128, 512]), sbg(ph, "wmb1", [128, 512])]
        wmh = [sbg(ph, "wmh0", [128, 512], BF16),
               sbg(ph, "wmh1", [128, 512], BF16)]
        wml = [sbg(ph, "wml0", [128, 512], BF16),
               sbg(ph, "wml1", [128, 512], BF16)]
        osb = [sbg(ph, "osb0", [128, 512]), sbg(ph, "osb1", [128, 512])]

        with contextlib.ExitStack() as phs:
            srcb = sbg(phs, "srcb", [128, 2048])
            p.op("sync", lambda e: e.dma_start(out=rw2s[:], in_=rw2_T),
                 waits=[(sV, sV.n), (sP, sP.n), (sA, sA.n)], incs=[(sL, 16)])
            p.op("sync", lambda e: e.dma_start(out=rbg[:, 0:64], in_=rb1),
                 incs=[(sL, 16)])
            p.op("sync", lambda e: e.dma_start(out=rbg[:, 64:96], in_=rb2),
                 incs=[(sL, 16)])
            for (srcdram, dd, na, ca, hi_, lo_) in (
                    (rw1_T, 64, 16, 16, rw1h, rw1l),
                    (A_T, 256, 16, 8, ath, atl),
                    (B_mov, 2048, 2, 1, bmh, bml)):
                for a0 in range(0, na, ca):
                    ncols = ca * dd
                    p.op("sync", lambda e, srcdram=srcdram, dd=dd, a0=a0,
                         ca=ca, ncols=ncols: e.dma_start(
                             out=srcb[:, 0:ncols],
                             in_=bass.AP(srcdram.tensor, a0 * 128 * dd,
                                         [[dd, 128], [128 * dd, ca],
                                          [1, dd]])),
                         waits=[(sV, sV.n)], incs=[(sL, 16)])
                    p.op("scalar", lambda e, ncols=ncols, hi_=hi_, a0=a0,
                         dd=dd: e.activation(
                             out=hi_[:, a0 * dd:a0 * dd + ncols],
                             in_=srcb[:, 0:ncols], func=ACT.Copy),
                         waits=[(sL, sL.n)], incs=[(sA, 1)])
                    p.op("vector", lambda e, ncols=ncols, hi_=hi_, lo_=lo_,
                         a0=a0, dd=dd: e.scalar_tensor_tensor(
                             out=lo_[:, a0 * dd:a0 * dd + ncols],
                             in0=srcb[:, 0:ncols], scalar=1.0,
                             in1=hi_[:, a0 * dd:a0 * dd + ncols],
                             op0=ALU.mult, op1=ALU.subtract),
                         waits=[(sA, sA.n)], incs=[(sV, 1)])

        with contextlib.ExitStack() as php:
            tpf = psg(php, "tpf", [128, 2048])
            pR = psg(php, "pR", [128, 512])
            for t in range(NT):
                x1t = yx[:, t * D:(t + 1) * D]
                p.op("vector", lambda e, x1t=x1t: e.tensor_reduce(
                    out=st(C_S1), in_=x1t, axis=AX.X, op=ALU.add),
                    incs=[(sV, 1)])
                p.op("scalar", lambda e, x1t=x1t: e.activation(
                    out=xn2[:], in_=x1t, func=ACT.Square,
                    accum_out=st(C_S2)), waits=[(sV, sV.n)])
                p.op("scalar", lambda e: e.activation(
                    out=st(C_MU), in_=st(C_S1), func=ACT.Copy,
                    scale=1.0 / D), waits=[(sV, sV.n)], incs=[(sA, 1)])
                p.op("scalar", lambda e: e.activation(
                    out=st(C_MU2), in_=st(C_MU), func=ACT.Square),
                    waits=[(sA, sA.n)], incs=[(sA, 1)])
                p.op("vector", lambda e: e.tensor_scalar(
                    out=st(C_VAR), in0=st(C_S2), scalar1=1.0 / D,
                    scalar2=st(C_MU2), op0=ALU.mult, op1=ALU.subtract),
                    waits=[(sA, sA.n)], incs=[(sV, 1)])
                p.op("scalar", lambda e: e.activation(
                    out=st(C_RSTD), in_=st(C_VAR), func=ACT.Sqrt,
                    bias=LNEPS), waits=[(sV, sV.n)], incs=[(sA, 1)])
                p.op("vector", lambda e: e.reciprocal(
                    out=st(C_RSTD), in_=st(C_RSTD)), waits=[(sA, sA.n)],
                    incs=[(sV, 1)])
                p.op("vector", lambda e, x1t=x1t: e.tensor_scalar(
                    out=xn2[:], in0=x1t, scalar1=st(C_MU),
                    scalar2=st(C_RSTD), op0=ALU.subtract, op1=ALU.mult),
                    waits=[(sV, sV.n)], incs=[(sV, 1)])
                transposeN(xn2, 0, tpf,
                           bass.AP(x2Th, t * 128,
                                   [[16 * 1024, 128], [1024, 16], [1, 128]]),
                           16, src_wait=[(sV, sV.n), (sA, sA.n)])
                p.op("vector", lambda e, t=t: e.scalar_tensor_tensor(
                    out=bass.AP(x2Tl, t * 128,
                                [[16 * 1024, 128], [1024, 16], [1, 128]]),
                    in0=tpf[:, 0:2048], scalar=1.0,
                    in1=bass.AP(x2Th, t * 128,
                                [[16 * 1024, 128], [1024, 16], [1, 128]]),
                    op0=ALU.mult, op1=ALU.subtract),
                    waits=[(sA, sA.n)], incs=[(sV, 1)])
                for pi, (lt, rt) in enumerate(((x2Th, rw1h), (x2Tl, rw1h),
                                               (x2Th, rw1l))):
                    for kb in range(16):
                        p.op("tensor", lambda e, pi=pi, lt=lt, rt=rt, kb=kb,
                             t=t: e.matmul(
                                 pR[:, 0:64],
                                 bass.AP(lt, kb * 1024 + t * 128,
                                         [[16 * 1024, 128], [1, 128]]),
                                 bass.AP(rt, kb * 64,
                                         [[16 * 64, 128], [1, 64]]),
                                 start=(pi == 0 and kb == 0),
                                 stop=(pi == 2 and kb == 15),
                                 skip_group_check=True),
                             waits=[(sV, sV.n)] if (pi == 0 and kb == 0)
                             else (),
                             incs=[(sP, 1)] if (pi == 2 and kb == 15)
                             else ())
                p.op("vector", lambda e: e.scalar_tensor_tensor(
                    out=pR[:, 0:64], in0=rbg[:, 0:64], scalar=1.0,
                    in1=pR[:, 0:64], op0=ALU.mult, op1=ALU.add),
                    waits=[(sP, sP.n)], incs=[(sV, 1)])
                p.op("scalar", lambda e: e.activation(
                    out=hsb[:], in_=pR[:, 0:64], func=ACT.Silu),
                    waits=[(sV, sV.n)], incs=[(sA, 1)])
                p.op("tensor", lambda e: e.transpose(
                    tpf[0:64, 0:128], hsb[:], eyef_s[:]),
                    waits=[(sA, sA.n)], incs=[(sP, 1)])
                p.op("scalar", lambda e: e.activation(
                    out=hT[:], in_=tpf[0:64, 0:128], func=ACT.Copy),
                    waits=[(sP, sP.n)], incs=[(sA, 1)])
                p.op("tensor", lambda e: e.matmul(
                    pR[:, 128:160], hT[:], rw2s[:], start=True, stop=True,
                    skip_group_check=True), waits=[(sA, sA.n)],
                    incs=[(sP, 1)])
                p.op("vector", lambda e: e.scalar_tensor_tensor(
                    out=lsb[:], in0=rbg[:, 64:96], scalar=1.0,
                    in1=pR[:, 128:160], op0=ALU.mult, op1=ALU.add),
                    waits=[(sP, sP.n)])
                p.op("vector", lambda e: e.tensor_tensor(
                    out=lg[:], in0=lsb[:, 0:16], in1=gt_s[:],
                    op=ALU.add))
                p.op("vector", lambda e: e.reduce_max(
                    out=st(C_M1), in_=lg[:], axis=AX.X), incs=[(sV, 1)])
                p.op("vector", lambda e: e.tensor_scalar(
                    out=eq1[:], in0=lg[:], scalar1=st(C_M1), scalar2=None,
                    op0=ALU.is_equal), waits=[(sV, sV.n)], incs=[(sV, 1)])
                p.op("vector", lambda e: e.scalar_tensor_tensor(
                    out=lgm[:], in0=eq1[:], scalar=NEGBIG, in1=lg[:],
                    op0=ALU.mult, op1=ALU.add), waits=[(sV, sV.n)],
                    incs=[(sV, 1)])
                p.op("vector", lambda e: e.reduce_max(
                    out=st(C_M2), in_=lgm[:], axis=AX.X),
                    waits=[(sV, sV.n)], incs=[(sV, 1)])
                p.op("vector", lambda e: e.tensor_scalar_mul(
                    out=st(C_NM1), in0=st(C_M1), scalar1=-1.0),
                    incs=[(sV, 1)])
                p.op("scalar", lambda e: e.activation(
                    out=st(C_E2), in_=st(C_M2), func=ACT.Exp,
                    bias=st(C_NM1)), waits=[(sV, sV.n)], incs=[(sA, 1)])
                p.op("vector", lambda e: e.tensor_scalar_add(
                    out=st(C_DEN), in0=st(C_E2), scalar1=1.0),
                    waits=[(sA, sA.n)], incs=[(sV, 1)])
                p.op("vector", lambda e: e.reciprocal(
                    out=st(C_RDEN), in_=st(C_DEN)), waits=[(sV, sV.n)],
                    incs=[(sV, 1)])
                p.op("vector", lambda e: e.tensor_scalar(
                    out=st(C_C2), in0=st(C_E2), scalar1=st(C_RDEN),
                    scalar2=None, op0=ALU.mult), waits=[(sV, sV.n)],
                    incs=[(sV, 1)])
                p.op("vector", lambda e: e.tensor_scalar(
                    out=eq2[:], in0=lg[:], scalar1=st(C_M2), scalar2=None,
                    op0=ALU.is_equal))
                p.op("vector", lambda e: e.tensor_scalar(
                    out=cf1[:], in0=eq1[:], scalar1=st(C_RDEN), scalar2=None,
                    op0=ALU.mult), incs=[(sV, 1)])
                p.op("vector", lambda e, t=t: e.scalar_tensor_tensor(
                    out=coef[:, t * 16:(t + 1) * 16], in0=eq2[:],
                    scalar=st(C_C2), in1=cf1[:], op0=ALU.mult, op1=ALU.add),
                    waits=[(sV, sV.n)], incs=[(sV, 1)])
                for pi, (lt, rt) in enumerate(((x2Th, ath), (x2Tl, ath),
                                               (x2Th, atl))):
                    for kb in range(16):
                        p.op("tensor", lambda e, pi=pi, lt=lt, rt=rt, kb=kb,
                             t=t: e.matmul(
                                 pR[:, 256:512],
                                 bass.AP(lt, kb * 1024 + t * 128,
                                         [[16 * 1024, 128], [1, 128]]),
                                 bass.AP(rt, kb * 256,
                                         [[16 * 256, 128], [1, 256]]),
                                 start=(pi == 0 and kb == 0),
                                 stop=(pi == 2 and kb == 15),
                                 skip_group_check=True),
                             waits=[(sV, sV.n)] if (pi == 0 and kb == 0)
                             else (),
                             incs=[(sP, 1)] if (pi == 2 and kb == 15)
                             else ())
                p.op("vector", lambda e, t=t: e.scalar_tensor_tensor(
                    out=wxa[:], in0=pR[:, 256:512], scalar=1.0,
                    in1=bass.AP(coef, t * 16,
                                [[NT * 16, 128], [1, 16], [0, 16]]),
                    op0=ALU.mult, op1=ALU.mult), waits=[(sP, sP.n)],
                    incs=[(sV, 1)])
                transposeN(wxa, 0, tpf,
                           bass.AP(wxTh, t * 256,
                                   [[NT * 256, 128], [128, 2], [1, 128]]),
                           2, src_wait=[(sV, sV.n), (sA, sA.n)])
                p.op("vector", lambda e, t=t: e.scalar_tensor_tensor(
                    out=bass.AP(wxTl, t * 256,
                                [[NT * 256, 128], [128, 2], [1, 128]]),
                    in0=tpf[:, 0:256], scalar=1.0,
                    in1=bass.AP(wxTh, t * 256,
                                [[NT * 256, 128], [128, 2], [1, 128]]),
                    op0=ALU.mult, op1=ALU.subtract),
                    waits=[(sA, sA.n)], incs=[(sV, 1)])

        pBig = psg(ph, "pBig", [128, 4096])
        for ot in range(4):
            evac_sA_before = sA.n
            p.op("sync", lambda e, ot=ot: e.dma_start(
                out=bvs[:], in_=biasv[:, ot * 512:(ot + 1) * 512]),
                waits=[(sV, sV.n)], incs=[(sL, 16)])
            bvs_loaded = sL.n
            for kb in range(16):
                buf, bh, bl = wmb[kb % 2], wmh[kb % 2], wml[kb % 2]
                p.op("sync", lambda e, ot=ot, kb=kb, buf=buf: e.dma_start(
                    out=buf[:], in_=W_T[kb * 128:(kb + 1) * 128,
                                        ot * 512:(ot + 1) * 512]),
                    waits=[(sV, sV.n)], incs=[(sL, 16)])
                p.op("scalar", lambda e, buf=buf, bh=bh: e.activation(
                    out=bh[:], in_=buf[:], func=ACT.Copy),
                    waits=[(sL, sL.n), (sP, sP.n)], incs=[(sA, 1)])
                p.op("vector", lambda e, buf=buf, bh=bh, bl=bl:
                     e.scalar_tensor_tensor(out=bl[:], in0=buf[:],
                                            scalar=1.0, in1=bh[:],
                                            op0=ALU.mult, op1=ALU.subtract),
                     waits=[(sA, sA.n)], incs=[(sV, 1)])
                for t in range(NT):
                    for pi, (lt, rt) in enumerate(((x2Th, bh), (x2Tl, bh),
                                                   (x2Th, bl))):
                        p.op("tensor", lambda e, t=t, pi=pi, lt=lt, rt=rt,
                             kb=kb: e.matmul(
                                 pBig[:, t * 512:(t + 1) * 512],
                                 bass.AP(lt, kb * 1024 + t * 128,
                                         [[16 * 1024, 128], [1, 128]]),
                                 rt[:], start=(pi == 0 and kb == 0),
                                 stop=False, skip_group_check=True),
                             waits=([(sV, sV.n)] +
                                    ([(sA, evac_sA_before)] if ot > 0
                                     else []))
                             if (t == 0 and pi == 0) else (),
                             incs=[(sP, 1)] if (t == NT - 1 and pi == 2)
                             else ())
            for t in range(NT):
                for pi, (lt, rt) in enumerate(((wxTh, bmh), (wxTl, bmh),
                                               (wxTh, bml))):
                    for kb2 in range(2):
                        p.op("tensor", lambda e, t=t, pi=pi, lt=lt, rt=rt,
                             kb2=kb2, ot=ot: e.matmul(
                                 pBig[:, t * 512:(t + 1) * 512],
                                 bass.AP(lt, t * 256 + kb2 * 128,
                                         [[NT * 256, 128], [1, 128]]),
                                 bass.AP(rt, kb2 * 2048 + ot * 512,
                                         [[2 * 2048, 128], [1, 512]]),
                                 start=False, stop=(pi == 2 and kb2 == 1),
                                 skip_group_check=True),
                             incs=[(sP, 1)] if (t == NT - 1 and pi == 2 and
                                                kb2 == 1) else ())
            for t in range(NT):
                ob = osb[t % 2]
                ew = [(sP, sP.n), (sG, sG.n)]
                p.op("scalar", lambda e, t=t, ob=ob: e.activation(
                    out=ob[:], in_=pBig[:, t * 512:(t + 1) * 512],
                    func=ACT.Copy), waits=ew, incs=[(sA, 1)])
                p.op("vector", lambda e, ob=ob: e.tensor_tensor(
                    out=ob[:], in0=ob[:], in1=bvs[:],
                    op=ALU.add), waits=[(sA, sA.n), (sL, bvs_loaded)])
                p.op("vector", lambda e, t=t, ot=ot, ob=ob: e.tensor_tensor(
                    out=ob[:], in0=ob[:],
                    in1=yx[:, t * D + ot * 512:t * D + ot * 512 + 512],
                    op=ALU.add), incs=[(sV, 1)])
                p.op("gpsimd", lambda e, t=t, ot=ot, ob=ob: e.dma_start(
                    out=out[t * 128:(t + 1) * 128, ot * 512:(ot + 1) * 512],
                    in_=ob[:]), waits=[(sV, sV.n)], incs=[(sG, 16)])
        p.op("gpsimd", lambda e: e.dma_start(
            out=bass.AP(ws3_dram.tensor, 0, [[3072, 1], [1, 16]]),
            in_=stats[0:1, 0:16]), waits=[(sG, sG.n)], incs=[(sG, 16)])

    with nc.Block() as block:
        @block.sync
        def _(eng):
            p.replay("sync", eng)

        @block.gpsimd
        def _(eng):
            p.replay("gpsimd", eng)

        @block.tensor
        def _(eng):
            p.replay("tensor", eng)

        @block.scalar
        def _(eng):
            p.replay("scalar", eng)

        @block.vector
        def _(eng):
            p.replay("vector", eng)

    es.close()
    return nc


def kernel(L0_attn_x, L0_attn_w_q, L0_attn_w_k, L0_attn_w_v, L0_attn_w_o,
           L0_attn_q_gain, L0_ffn_W, L0_ffn_bias, L0_ffn_rw1, L0_ffn_rb1,
           L0_ffn_rw2, L0_ffn_rb2, L0_ffn_A, L0_ffn_B, L0_ffn_gates):
    x = np.ascontiguousarray(np.asarray(L0_attn_x, dtype=np.float32))
    w_qkv = np.ascontiguousarray(np.concatenate(
        [np.asarray(L0_attn_w_q), np.asarray(L0_attn_w_k),
         np.asarray(L0_attn_w_v)], axis=0).astype(np.float32))
    w_o = np.ascontiguousarray(np.asarray(L0_attn_w_o, dtype=np.float32))
    W_T = np.ascontiguousarray(np.asarray(L0_ffn_W, dtype=np.float32).T)
    biasv = np.ascontiguousarray(np.broadcast_to(
        np.asarray(L0_ffn_bias, dtype=np.float32).reshape(1, D), (128, D)))
    rw1_T = np.ascontiguousarray(np.asarray(L0_ffn_rw1, dtype=np.float32).T)
    rw2_T = np.ascontiguousarray(np.asarray(L0_ffn_rw2, dtype=np.float32).T)
    rb1 = np.ascontiguousarray(np.broadcast_to(
        np.asarray(L0_ffn_rb1, dtype=np.float32).reshape(1, 64), (128, 64)))
    rb2 = np.ascontiguousarray(np.broadcast_to(
        np.asarray(L0_ffn_rb2, dtype=np.float32).reshape(1, 32), (128, 32)))
    gates = np.ascontiguousarray(np.broadcast_to(
        np.asarray(L0_ffn_gates, dtype=np.float32).reshape(1, 16), (128, 16)))
    qgain = np.ascontiguousarray(np.broadcast_to(
        np.asarray(L0_attn_q_gain, dtype=np.float32).reshape(1, 16),
        (128, 16)))
    A_T = np.ascontiguousarray(
        np.asarray(L0_ffn_A, dtype=np.float32).reshape(256, D).T)
    B_mov = np.ascontiguousarray(
        np.asarray(L0_ffn_B, dtype=np.float32).transpose(0, 2, 1)
        .reshape(256, D))
    eyef = np.eye(128, dtype=np.float32)
    constsv = np.ascontiguousarray(np.broadcast_to(
        np.array([[LNEPS, RMSEPS, RND, 0.0, -RND, 0, 0, 0]],
                 dtype=np.float32), (128, 8)))

    inv_freq = 1.0 / (10000.0 ** (np.arange(0, HD, 2, dtype=np.float64) / HD))
    fr = np.outer(np.arange(S, dtype=np.float64), inv_freq)
    cos_all = np.cos(fr).astype(np.float32)
    sin_all = np.sin(fr).astype(np.float32)
    cos_k = np.ascontiguousarray(np.concatenate([cos_all, cos_all], axis=1))
    sin_k = np.ascontiguousarray(np.concatenate([sin_all, -sin_all], axis=1))

    nc = build_program()

    in_maps = []
    for core in range(8):
        b, j = divmod(core, 4)
        sel = np.arange(32)[j::4]
        xq_rows = x[b].reshape(32, 128, D)[sel].reshape(TPC, D)
        pos_q = (sel[:, None] * 128 + np.arange(128)[None, :]).reshape(-1)
        dm = (np.arange(512)[None, :] >
              (np.arange(128)[:, None] + 128 * j)).astype(np.float32)
        in_maps.append({
            "x_kv": np.ascontiguousarray(x[b]),
            "x_q": np.ascontiguousarray(xq_rows),
            "w_qkv": w_qkv, "w_o": w_o, "W_T": W_T, "biasv": biasv,
            "rw1_T": rw1_T, "rw2_T": rw2_T, "rb1": rb1, "rb2": rb2,
            "gates": gates, "qgain": qgain, "A_T": A_T, "B_mov": B_mov,
            "cos_q": np.ascontiguousarray(cos_k[pos_q]),
            "sin_q": np.ascontiguousarray(sin_k[pos_q]),
            "cos_k": cos_k, "sin_k": sin_k,
            "eyef": eyef, "dmask": np.ascontiguousarray(dm),
            "constsv": constsv,
        })

    res = run_bass_kernel_spmd(nc, in_maps, core_ids=list(range(8)))
    outv = np.empty((B, S, D), dtype=np.float32)
    for core in range(8):
        b, j = divmod(core, 4)
        sel = np.arange(32)[j::4]
        outv[b].reshape(32, 128, D)[sel] = \
            res.results[core]["o"].reshape(NT, 128, D)
    return outv


if __name__ == "__main__":
    nc = build_program()
    print("built ok")


# version 08 (final)
# speedup vs baseline: 1.0661x; 1.0145x over previous
import contextlib
import numpy as np
import concourse.bass as bass
import concourse.mybir as mybir
from concourse.bass_utils import run_bass_kernel_spmd

F32 = mybir.dt.float32
BF16 = mybir.dt.bfloat16
ALU = mybir.AluOpType
AX = mybir.AxisListType
ACT = mybir.ActivationFunctionType

B, S, D = 2, 4096, 2048
H, KVH, HD = 16, 4, 128
KVD = KVH * HD
TPC, NT, NC, NCH = 1024, 8, 32, 8
RND = 12582912.0
QEPS = 1e-5
LNEPS = 1e-5
RMSEPS = float(np.finfo(np.float32).eps)
SMSCALE = float(1.0 / np.sqrt(np.float32(128.0)))
NEGBIG = -1.0e30


class Sem:
    def __init__(self, es, nc, name):
        self.h = es.enter_context(nc.semaphore(name))
        self.n = 0


class Prog:

    def __init__(self):
        self.ops = []
        self.chain_sems = ()
        self.auto_chain = {}

    def op(self, eng, emit, waits=(), incs=()):
        waits = list(waits)
        incs = list(incs)
        for s, amt in incs:
            if s in self.chain_sems:
                waits.append((s, s.n))
        ac = self.auto_chain.get(eng)
        if ac is not None:
            waits.append((ac, max(0, ac.n - 1)))
            if not any(s is ac for s, _ in incs):
                incs.append((ac, 1))
        for s, amt in incs:
            s.n += amt
        self.ops.append((eng, tuple(waits), emit, tuple(incs)))

    def replay(self, eng_name, eng):
        for name, waits, emit, incs in self.ops:
            if name != eng_name:
                continue
            for s, v in waits:
                eng.wait_ge(s.h, v)
            ins = emit(eng)
            for s, amt in incs:
                ins = ins.then_inc(s.h, amt)


def build_program(debug=False, stop_early=False):
    nc = bass.Bass("TRN2", target_bir_lowering=False)

    def din(name, shape, dtype=F32):
        return nc.dram_tensor(name, shape, dtype, kind="ExternalInput").ap()

    x_kv = din("x_kv", [S, D])
    x_q = din("x_q", [TPC, D])
    w_qkv = din("w_qkv", [3072, D])
    w_o = din("w_o", [D, D])
    W_T = din("W_T", [D, D])
    biasv = din("biasv", [128, D])
    rw1_T = din("rw1_T", [D, 64])
    rw2_T = din("rw2_T", [64, 32])
    rb1 = din("rb1", [128, 64])
    rb2 = din("rb2", [128, 32])
    gates = din("gates", [128, 16])
    qgain = din("qgain", [128, 16])
    A_T = din("A_T", [D, 256])
    B_mov = din("B_mov", [256, D])
    cos_q = din("cos_q", [TPC, HD])
    sin_q = din("sin_q", [TPC, HD])
    cos_k = din("cos_k", [S, HD])
    sin_k = din("sin_k", [S, HD])
    eyef = din("eyef", [128, 128])
    constsv = din("constsv", [128, 8])
    dmask = din("dmask", [128, 512])
    out = nc.dram_tensor("o", [TPC, D], F32, kind="ExternalOutput").ap()

    kk = dict(kind="ExternalOutput") if debug else {}
    K_scr = nc.dram_tensor("K_scr", [KVH, S, HD], F32, **kk).ap()
    V_scr = nc.dram_tensor("V_scr", [KVH, S, HD], F32, **kk).ap()
    Q_scr = nc.dram_tensor("Q_scr", [H, TPC, HD], F32, **kk).ap()
    if debug:
        dbg_y = nc.dram_tensor("dbg_y", [TPC, D], F32,
                               kind="ExternalOutput").ap()
        dbg_ws = nc.dram_tensor("dbg_ws", [128, 3072], F32,
                                kind="ExternalOutput").ap()
        dbg_wkv = nc.dram_tensor("dbg_wkv", [128, 16 * 1024], BF16,
                                 kind="ExternalOutput").ap()
        dbg_xq = nc.dram_tensor("dbg_xq", [128, D], F32,
                                kind="ExternalOutput").ap()
        dbg_kvd = nc.dram_tensor("dbg_kvd", [128, 1024], F32,
                                 kind="ExternalOutput").ap()
        dbg_xb = nc.dram_tensor("dbg_xb", [128, D], F32,
                                kind="ExternalOutput").ap()
        dbg_x1 = nc.dram_tensor("dbg_x1", [128, D], F32,
                                kind="ExternalOutput").ap()
        dbg_tm = nc.dram_tensor("dbg_tm", [128, D], F32,
                                kind="ExternalOutput").ap()
        dbg_xqt = nc.dram_tensor("dbg_xqt", [128, D], BF16,
                                 kind="ExternalOutput").ap()
        dbg_st = nc.dram_tensor("dbg_st", [128, 64], F32,
                                kind="ExternalOutput").ap()
    ws3_dram = nc.dram_tensor("ws3_dram", [1, 3072], F32).ap()
    wso_dram = nc.dram_tensor("wso_dram", [1, D], F32).ap()

    es = contextlib.ExitStack()
    p = Prog()
    sL = Sem(es, nc, "sL")
    sG = Sem(es, nc, "sG")
    sP = Sem(es, nc, "sP")
    sA = Sem(es, nc, "sA")
    sV = Sem(es, nc, "sV")
    sB = Sem(es, nc, "sB")
    p.chain_sems = (sL, sG)
    p.auto_chain = {"vector": sV, "scalar": sA}

    def sbg(st_, name, shape, dtype=F32):
        return st_.enter_context(nc.sbuf_tensor(name, shape, dtype))

    def psg(st_, name, shape, dtype=F32):
        return st_.enter_context(nc.psum_tensor(name, shape, dtype))

    yx = sbg(es, "yx", [128, NT * D])
    eyef_s = sbg(es, "eyef_s", [128, 128])
    stats = sbg(es, "stats", [128, 64])
    (C_S1, C_MU, C_S2, C_VAR, C_RSTD, C_AM, C_QS, C_RA, C_MU2, C_M1, C_M2,
     C_NM1, C_E2, C_DEN, C_RDEN, C_C2) = range(16)
    asc_kv = sbg(es, "asc_kv", [128, NC])
    asc_q = sbg(es, "asc_q", [128, NT])
    asc_y = sbg(es, "asc_y", [128, NT])
    qg_s = sbg(es, "qg_s", [128, 16])
    gt_s = sbg(es, "gt_s", [128, 16])
    csts = sbg(es, "csts", [128, 8])
    nc.const_aps.register(F32, LNEPS, csts[:, 0:1])
    nc.const_aps.register(F32, RMSEPS, csts[:, 1:2])
    nc.const_aps.register(F32, RND, csts[:, 2:3])
    nc.const_aps.register(F32, 0.0, csts[:, 3:4])
    nc.const_aps.register(F32, -RND, csts[:, 4:5])
    phWS = contextlib.ExitStack()
    ws3x = sbg(phWS, "ws3x", [128, 3072])

    p.op("vector", lambda e: e.memset(stats[:], 0.0), incs=[(sV, 1)])
    p.op("sync", lambda e: e.dma_start(out=eyef_s[:], in_=eyef), incs=[(sL, 16)])
    p.op("sync", lambda e: e.dma_start(out=csts[:], in_=constsv),
         incs=[(sL, 16)])
    p.op("sync", lambda e: e.dma_start(out=qg_s[:], in_=qgain),
         incs=[(sL, 16)])
    p.op("sync", lambda e: e.dma_start(out=gt_s[:], in_=gates),
         incs=[(sL, 16)])

    def st(c):
        return stats[:, c:c + 1]

    def bcast(t, off, n):
        return bass.AP(t, off, [[0, 128], [1, n]])

    def barrier():
        bw = [(sP, sP.n), (sL, sL.n), (sG, sG.n), (sA, sA.n), (sV, sV.n)]
        p.op("scalar", lambda e: e.activation(
            out=stats[:, 62:63], in_=stats[:, 62:63], func=ACT.Copy),
            waits=bw)
        p.op("vector", lambda e: e.tensor_copy(
            out=stats[:, 61:62], in_=stats[:, 61:62]), waits=bw)
        return list(bw) + [(sA, sA.n), (sV, sV.n)]

    def ln_quant(xt, tmpa, xq, asc_ap, extra_wait=(), dbg_mid=False):
        p.op("vector", lambda e: e.tensor_reduce(
            out=st(C_S1), in_=xt[:], axis=AX.X, op=ALU.add),
            waits=list(extra_wait) + [(sA, sA.n)], incs=[(sV, 1)])
        p.op("scalar", lambda e: e.activation(
            out=tmpa[:], in_=xt[:], func=ACT.Square, accum_out=st(C_S2)),
            waits=list(extra_wait) + [(sV, sV.n)])
        p.op("scalar", lambda e: e.activation(
            out=st(C_MU), in_=st(C_S1), func=ACT.Copy, scale=1.0 / D),
            waits=[(sV, sV.n)], incs=[(sA, 1)])
        p.op("scalar", lambda e: e.activation(
            out=st(C_MU2), in_=st(C_MU), func=ACT.Square),
            waits=[(sA, sA.n)], incs=[(sA, 1)])
        p.op("vector", lambda e: e.tensor_scalar(
            out=st(C_VAR), in0=st(C_S2), scalar1=1.0 / D, scalar2=st(C_MU2),
            op0=ALU.mult, op1=ALU.subtract), waits=[(sA, sA.n)],
            incs=[(sV, 1)])
        p.op("scalar", lambda e: e.activation(
            out=st(C_RSTD), in_=st(C_VAR), func=ACT.Sqrt, bias=LNEPS),
            waits=[(sV, sV.n)], incs=[(sA, 1)])
        p.op("vector", lambda e: e.reciprocal(
            out=st(C_RSTD), in_=st(C_RSTD)), waits=[(sA, sA.n)],
            incs=[(sV, 1)])
        p.op("vector", lambda e: e.tensor_scalar(
            out=tmpa[:], in0=xt[:], scalar1=st(C_MU), scalar2=st(C_RSTD),
            op0=ALU.subtract, op1=ALU.mult), waits=[(sV, sV.n)])
        p.op("vector", lambda e: e.reduce_max(
            out=st(C_AM), in_=tmpa[:], axis=AX.X, apply_absolute_value=True),
            incs=[(sV, 1)])
        p.op("vector", lambda e: e.tensor_scalar(
            out=st(C_QS), in0=st(C_AM), scalar1=1.0 / 127.0, scalar2=QEPS,
            op0=ALU.mult, op1=ALU.max), waits=[(sV, sV.n)], incs=[(sV, 1)])
        p.op("vector", lambda e: e.tensor_copy(out=asc_ap, in_=st(C_QS)),
             waits=[(sV, sV.n)])
        p.op("vector", lambda e: e.reciprocal(out=st(C_RA), in_=st(C_QS)),
             incs=[(sV, 1)])
        p.op("scalar", lambda e: e.activation(
            out=xq[:], in_=tmpa[:], func=ACT.Copy, scale=st(C_RA), bias=RND),
            waits=[(sV, sV.n)], incs=[(sA, 1)])
        if dbg_mid:
            p.op("gpsimd", lambda e: e.dma_start(out=dbg_x1, in_=xq[:]),
                 waits=[(sA, sA.n)], incs=[(sG, 16)])
            p.op("vector", lambda e: e.tensor_copy(
                out=st(C_S1), in_=st(C_S1)), waits=[(sG, sG.n)],
                incs=[(sV, 1)])
        p.op("scalar", lambda e: e.activation(
            out=tmpa[:], in_=xq[:], func=ACT.Copy, bias=-RND),
            waits=[(sA, sA.n), (sV, sV.n)], incs=[(sA, 1)])
        if dbg_mid:
            p.op("gpsimd", lambda e: e.dma_start(out=dbg_tm, in_=tmpa[:]),
                 waits=[(sA, sA.n)], incs=[(sG, 16)])
            p.op("vector", lambda e: e.tensor_copy(
                out=st(C_S1), in_=st(C_S1)), waits=[(sG, sG.n)],
                incs=[(sV, 1)])
        p.op("scalar", lambda e: e.activation(
            out=xq[:], in_=tmpa[:], func=ACT.Copy),
            waits=[(sV, sV.n)], incs=[(sA, 1)])
        p.op("vector", lambda e: e.tensor_copy(
            out=st(C_S1), in_=st(C_S1)), waits=[(sA, sA.n)], incs=[(sV, 1)])

    def quant_w_tile(wbuf, wtmp, wqb, ws_dram_ap, ws_col, rowlen):
        p.op("vector", lambda e: e.reduce_max(
            out=st(C_AM), in_=wbuf[:], axis=AX.X, apply_absolute_value=True),
            waits=[(sL, sL.n), (sG, sG.n), (sA, sA.n), (sV, sV.n)],
            incs=[(sV, 1)])
        p.op("vector", lambda e: e.tensor_scalar(
            out=st(C_QS), in0=st(C_AM), scalar1=1.0 / 127.0, scalar2=QEPS,
            op0=ALU.mult, op1=ALU.max), waits=[(sV, sV.n)], incs=[(sV, 1)])
        p.op("vector", lambda e: e.reciprocal(out=st(C_RA), in_=st(C_QS)),
             waits=[(sV, sV.n)], incs=[(sV, 1)])
        p.op("gpsimd", lambda e: e.dma_start(
            out=bass.AP(ws_dram_ap.tensor, ws_col,
                        [[rowlen, 1], [1, 128]]), in_=st(C_QS)),
            waits=[(sV, sV.n)], incs=[(sG, 16)])
        p.op("scalar", lambda e: e.activation(
            out=wtmp[:], in_=wbuf[:], func=ACT.Copy, scale=st(C_RA),
            bias=RND), waits=[(sV, sV.n)], incs=[(sA, 1)])
        p.op("scalar", lambda e: e.activation(
            out=wqb[:], in_=wtmp[:], func=ACT.Copy, bias=-RND),
            waits=[(sA, sA.n)], incs=[(sA, 1)])
        p.op("vector", lambda e: e.tensor_copy(
            out=st(C_S1), in_=st(C_S1)), waits=[(sA, sA.n)], incs=[(sV, 1)])

    def transposeN(src, col0, tp, dst_ap, nkb, src_wait):
        for kb in range(nkb):
            p.op("tensor", lambda e, kb=kb: e.transpose(
                tp[:, kb * 128:(kb + 1) * 128],
                src[:, col0 + kb * 128:col0 + (kb + 1) * 128], eyef_s[:]),
                waits=list(src_wait) if kb == 0 else (),
                incs=[(sP, 1)] if kb == nkb - 1 else ())
        p.op("scalar", lambda e: e.activation(
            out=dst_ap, in_=tp[:, 0:nkb * 128], func=ACT.Copy),
            waits=[(sP, sP.n)], incs=[(sA, 1)])

    def rope(src, dst, sc, rp, nh):
        cosx = bass.AP(rp, 0, [[2 * HD, 128], [0, nh], [1, HD]])
        sin0 = bass.AP(rp, HD, [[2 * HD, 128], [0, nh], [1, 64]])
        sin1 = bass.AP(rp, HD + 64, [[2 * HD, 128], [0, nh], [1, 64]])
        w_ = nh * HD

        def v3(t, off, inner):
            return bass.AP(t, off, [[w_ if t is not sc else nh * 64, 128],
                                    [HD if t is not sc else 64, nh],
                                    [1, inner]])

        scv = bass.AP(sc, 0, [[nh * 64, 128], [64, nh], [1, 64]])
        p.op("vector", lambda e: e.tensor_tensor(
            out=v3(dst, 0, HD), in0=v3(src, 0, HD), in1=cosx, op=ALU.mult))
        p.op("vector", lambda e: e.tensor_tensor(
            out=scv, in0=v3(src, 64, 64), in1=sin0, op=ALU.mult))
        p.op("vector", lambda e: e.tensor_tensor(
            out=v3(dst, 0, 64), in0=v3(dst, 0, 64), in1=scv, op=ALU.add))
        p.op("vector", lambda e: e.tensor_tensor(
            out=scv, in0=v3(src, 0, 64), in1=sin1, op=ALU.mult))
        p.op("vector", lambda e: e.tensor_tensor(
            out=v3(dst, 64, 64), in0=v3(dst, 64, 64), in1=scv, op=ALU.add),
            incs=[(sV, 1)])

    phKV = contextlib.ExitStack()
    wkvT = sbg(phKV, "wkvT", [128, 16 * 1024], BF16)
    with contextlib.ExitStack() as ph:
        wt = [sbg(ph, "wt0", [128, D]), sbg(ph, "wt1", [128, D])]
        wtmp = sbg(ph, "wtmp", [128, D])
        wqb = sbg(ph, "wqb", [128, D])
        tpw = psg(ph, "tpw", [128, 2048])
        for r in range(8):
            buf = wt[r % 2]
            p.op("sync", lambda e, r=r, buf=buf: e.dma_start(
                out=buf[:], in_=w_qkv[D + r * 128:D + (r + 1) * 128, :]),
                waits=[(sV, sV.n)] if r >= 2 else [], incs=[(sL, 16)])
            quant_w_tile(buf, wtmp, wqb, ws3_dram, 2048 + r * 128, 3072)
            transposeN(wqb, 0, tpw,
                       bass.AP(wkvT, r * 128,
                               [[16 * 1024, 128], [1024, 16], [1, 128]]),
                       16, src_wait=[(sV, sV.n), (sA, sA.n)])

    barrier()
    if debug:
        p.op("gpsimd", lambda e: e.dma_start(out=dbg_wkv, in_=wkvT[:]),
             waits=[(sA, sA.n)], incs=[(sG, 16)])
    p.op("gpsimd", lambda e: e.dma_start(
        out=ws3x[:, 2048:3072],
        in_=bass.AP(ws3_dram.tensor, 2048, [[3072, 1], [0, 128], [1, 1024]])),
        waits=[(sG, sG.n)], incs=[(sG, 16)])
    ws3_rep1 = sG.n

    with contextlib.ExitStack() as ph:
        xt = [sbg(ph, "xt0", [128, D]), sbg(ph, "xt1", [128, D])]
        rop = [sbg(ph, "rop0", [128, 2 * HD]), sbg(ph, "rop1", [128, 2 * HD])]
        tmpa = sbg(ph, "tmpa", [128, D])
        xqb = sbg(ph, "xqb", [128, D])
        xqT = sbg(ph, "xqT", [128, D], BF16)
        kvd = sbg(ph, "kvd", [128, 2 * KVD])
        krm = sbg(ph, "krm", [128, KVD])
        kro = sbg(ph, "kro", [128, KVD])
        ksc = sbg(ph, "ksc", [128, KVH * 64])
        tpx = psg(ph, "tpx", [128, 2048])
        pKV = psg(ph, "pKV", [128, 1024])
        for t in range(1 if stop_early else NC):
            xb, rp = xt[t % 2], rop[t % 2]
            p.op("sync", lambda e, t=t, xb=xb: e.dma_start(
                out=xb[:], in_=x_kv[t * 128:(t + 1) * 128, :]),
                waits=[(sV, sV.n)] if t >= 2 else
                [(sP, sP.n), (sA, sA.n)], incs=[(sL, 16)])
            p.op("sync", lambda e, t=t, rp=rp: e.dma_start(
                out=rp[:, 0:HD], in_=cos_k[t * 128:(t + 1) * 128, :]),
                incs=[(sL, 16)])
            p.op("sync", lambda e, t=t, rp=rp: e.dma_start(
                out=rp[:, HD:2 * HD], in_=sin_k[t * 128:(t + 1) * 128, :]),
                incs=[(sL, 16)])
            ln_quant(xb, tmpa, xqb, asc_kv[:, t:t + 1],
                     extra_wait=[(sL, sL.n)], dbg_mid=(debug and t == 0))
            if debug and t == 0:
                p.op("gpsimd", lambda e, xqb=xqb: e.dma_start(
                    out=dbg_xq, in_=xqb[:]),
                     waits=[(sV, sV.n)], incs=[(sG, 16)])
                p.op("gpsimd", lambda e, xb=xb: e.dma_start(out=dbg_xb,
                                                            in_=xb[:]),
                     incs=[(sG, 16)])
                p.op("gpsimd", lambda e: e.dma_start(out=dbg_st,
                                                     in_=stats[:]),
                     incs=[(sG, 16)])
            transposeN(xqb, 0, tpx, bass.AP(xqT, 0, [[D, 128], [1, D]]), 16,
                       src_wait=[(sV, sV.n), (sA, sA.n)])
            if debug and t == 0:
                p.op("gpsimd", lambda e, xqT=xqT: e.dma_start(
                    out=dbg_xqt, in_=xqT[:]),
                     waits=[(sA, sA.n)], incs=[(sG, 16)])
                p.op("vector", lambda e: e.tensor_copy(
                    out=st(C_S1), in_=st(C_S1)), waits=[(sG, sG.n)],
                    incs=[(sV, 1)])
            for ot in range(2):
                for kb in range(16):
                    p.op("tensor", lambda e, ot=ot, kb=kb, xqT=xqT,
                         wkvT=wkvT: e.matmul(
                        pKV[:, ot * 512:(ot + 1) * 512],
                        bass.AP(xqT, kb * 128, [[D, 128], [1, 128]]),
                        bass.AP(wkvT, kb * 1024 + ot * 512,
                                [[16 * 1024, 128], [1, 512]]),
                        start=(kb == 0), stop=(kb == 15),
                        skip_group_check=True),
                        waits=[(sA, sA.n)] if (ot == 0 and kb == 0) else (),
                        incs=[(sP, 1)] if (ot == 1 and kb == 15) else ())
            p.op("scalar", lambda e, t=t: e.activation(
                out=kvd[:], in_=pKV[:], func=ACT.Copy,
                scale=asc_kv[:, t:t + 1]), waits=[(sP, sP.n)], incs=[(sA, 1)])
            dq_wait = [(sA, sA.n)]
            if t == 0:
                dq_wait.append((sG, ws3_rep1))
            p.op("vector", lambda e: e.tensor_tensor(
                out=kvd[:], in0=kvd[:], in1=ws3x[:, 2048:3072],
                op=ALU.mult), waits=dq_wait, incs=[(sV, 1)])
            if debug and t == 0:
                p.op("gpsimd", lambda e, kvd=kvd: e.dma_start(
                    out=dbg_kvd, in_=kvd[:]),
                     waits=[(sV, sV.n)], incs=[(sG, 16)])
            p.op("scalar", lambda e: e.activation(
                out=krm[:], in_=kvd[:, 0:KVD], func=ACT.Square),
                waits=[(sV, sV.n)], incs=[(sA, 1)])
            p.op("vector", lambda e: e.tensor_reduce(
                out=stats[:, 16:20],
                in_=bass.AP(krm, 0, [[KVD, 128], [HD, KVH], [1, HD]]),
                axis=AX.X, op=ALU.add), waits=[(sA, sA.n)], incs=[(sV, 1)])
            p.op("scalar", lambda e: e.activation(
                out=stats[:, 20:24], in_=stats[:, 16:20], func=ACT.Sqrt,
                scale=1.0 / HD, bias=RMSEPS), waits=[(sV, sV.n)],
                incs=[(sA, 1)])
            p.op("vector", lambda e: e.reciprocal(
                out=stats[:, 20:24], in_=stats[:, 20:24]),
                waits=[(sA, sA.n)], incs=[(sV, 1)])
            p.op("vector", lambda e: e.tensor_tensor(
                out=krm[:], in0=kvd[:, 0:KVD],
                in1=bass.AP(stats, 20, [[64, 128], [1, KVH], [0, HD]]),
                op=ALU.mult), waits=[(sV, sV.n)])
            rope(krm, kro, ksc, rp, KVH)
            for kh in range(KVH):
                p.op("gpsimd", lambda e, t=t, kh=kh: e.dma_start(
                    out=K_scr[kh, t * 128:(t + 1) * 128, :],
                    in_=kro[:, kh * HD:(kh + 1) * HD]),
                    waits=[(sV, sV.n)] if kh == 0 else (), incs=[(sG, 16)])
                p.op("gpsimd", lambda e, t=t, kh=kh: e.dma_start(
                    out=V_scr[kh, t * 128:(t + 1) * 128, :],
                    in_=kvd[:, KVD + kh * HD:KVD + (kh + 1) * HD]),
                    incs=[(sG, 16)])
            p.op("vector", lambda e: e.tensor_copy(
                out=st(C_S1), in_=st(C_S1)), waits=[(sG, sG.n)],
                incs=[(sV, 1)])

    phKV.close()

    with contextlib.ExitStack() as ph:
        wqT = sbg(ph, "wqT", [128, 16 * 2048], BF16)
        with contextlib.ExitStack() as phw:
            wt = [sbg(phw, "wt0b", [128, D]), sbg(phw, "wt1b", [128, D])]
            wtmp = sbg(phw, "wtmpb", [128, D])
            wqb = sbg(phw, "wqbb", [128, D])
            tpw = psg(phw, "tpwb", [128, 2048])
            for r in range(16):
                buf = wt[r % 2]
                p.op("sync", lambda e, r=r, buf=buf: e.dma_start(
                    out=buf[:], in_=w_qkv[r * 128:(r + 1) * 128, :]),
                    waits=[(sV, sV.n)] if r >= 2 else
                    [(sP, sP.n), (sG, sG.n), (sA, sA.n)], incs=[(sL, 16)])
                quant_w_tile(buf, wtmp, wqb, ws3_dram, r * 128, 3072)
                transposeN(wqb, 0, tpw,
                           bass.AP(wqT, r * 128,
                                   [[16 * 2048, 128], [2048, 16], [1, 128]]),
                           16, src_wait=[(sV, sV.n), (sA, sA.n)])

        barrier()
        p.op("gpsimd", lambda e: e.dma_start(
            out=ws3x[:, 0:2048],
            in_=bass.AP(ws3_dram.tensor, 0,
                        [[3072, 1], [0, 128], [1, 2048]])),
            waits=[(sG, sG.n)], incs=[(sG, 16)])
        ws3_rep2 = sG.n

        xt = [sbg(ph, "xq0", [128, D]), sbg(ph, "xq1", [128, D])]
        rop = [sbg(ph, "ropq0", [128, 2 * HD]),
               sbg(ph, "ropq1", [128, 2 * HD])]
        tmpa = sbg(ph, "tmpaq", [128, D])
        xqb = sbg(ph, "xqbq", [128, D])
        xqT = sbg(ph, "xqTq", [128, D], BF16)
        qd = sbg(ph, "qd", [128, D])
        qn = sbg(ph, "qn", [128, D])
        qro = sbg(ph, "qro", [128, D])
        qsc = sbg(ph, "qsc", [128, H * 64])
        tpx = psg(ph, "tpxq", [128, 2048])
        pQ = psg(ph, "pQ", [128, 2048])
        for t in range(NT):
            xb, rp = xt[t % 2], rop[t % 2]
            p.op("sync", lambda e, t=t, xb=xb: e.dma_start(
                out=xb[:], in_=x_q[t * 128:(t + 1) * 128, :]),
                waits=[(sV, sV.n)] if t >= 2 else [(sV, sV.n), (sA, sA.n)],
                incs=[(sL, 16)])
            p.op("sync", lambda e, t=t, rp=rp: e.dma_start(
                out=rp[:, 0:HD], in_=cos_q[t * 128:(t + 1) * 128, :]),
                incs=[(sL, 16)])
            p.op("sync", lambda e, t=t, rp=rp: e.dma_start(
                out=rp[:, HD:2 * HD], in_=sin_q[t * 128:(t + 1) * 128, :]),
                incs=[(sL, 16)])
            ln_quant(xb, tmpa, xqb, asc_q[:, t:t + 1],
                     extra_wait=[(sL, sL.n)])
            transposeN(xqb, 0, tpx, bass.AP(xqT, 0, [[D, 128], [1, D]]), 16,
                       src_wait=[(sV, sV.n), (sA, sA.n)])
            for ot in range(4):
                for kb in range(16):
                    p.op("tensor", lambda e, ot=ot, kb=kb: e.matmul(
                        pQ[:, ot * 512:(ot + 1) * 512],
                        bass.AP(xqT, kb * 128, [[D, 128], [1, 128]]),
                        bass.AP(wqT, kb * 2048 + ot * 512,
                                [[16 * 2048, 128], [1, 512]]),
                        start=(kb == 0), stop=(kb == 15),
                        skip_group_check=True),
                        waits=[(sA, sA.n)] if (ot == 0 and kb == 0) else (),
                        incs=[(sP, 1)] if (ot == 3 and kb == 15) else ())
            p.op("scalar", lambda e, t=t: e.activation(
                out=qd[:], in_=pQ[:], func=ACT.Copy,
                scale=asc_q[:, t:t + 1]), waits=[(sP, sP.n)], incs=[(sA, 1)])
            dq_wait = [(sA, sA.n)]
            if t == 0:
                dq_wait.append((sG, ws3_rep2))
            p.op("vector", lambda e: e.tensor_tensor(
                out=qd[:], in0=qd[:], in1=ws3x[:, 0:2048], op=ALU.mult),
                waits=dq_wait, incs=[(sV, 1)])
            p.op("scalar", lambda e: e.activation(
                out=tmpa[:], in_=qd[:], func=ACT.Square),
                waits=[(sV, sV.n)], incs=[(sA, 1)])
            p.op("vector", lambda e: e.tensor_reduce(
                out=stats[:, 16:32],
                in_=bass.AP(tmpa, 0, [[D, 128], [HD, H], [1, HD]]),
                axis=AX.X, op=ALU.add), waits=[(sA, sA.n)], incs=[(sV, 1)])
            p.op("scalar", lambda e: e.activation(
                out=stats[:, 32:48], in_=stats[:, 16:32], func=ACT.Sqrt,
                scale=1.0 / HD, bias=RMSEPS), waits=[(sV, sV.n)],
                incs=[(sA, 1)])
            p.op("vector", lambda e: e.reciprocal(
                out=stats[:, 32:48], in_=stats[:, 32:48]),
                waits=[(sA, sA.n)], incs=[(sV, 1)])
            p.op("vector", lambda e: e.tensor_tensor(
                out=stats[:, 32:48], in0=stats[:, 32:48],
                in1=qg_s[:], op=ALU.mult), waits=[(sV, sV.n)],
                incs=[(sV, 1)])
            p.op("vector", lambda e: e.tensor_tensor(
                out=qn[:], in0=qd[:],
                in1=bass.AP(stats, 32, [[64, 128], [1, H], [0, HD]]),
                op=ALU.mult), waits=[(sV, sV.n)])
            rope(qn, qro, qsc, rp, H)
            for h in range(H):
                p.op("gpsimd", lambda e, t=t, h=h: e.dma_start(
                    out=Q_scr[h, t * 128:(t + 1) * 128, :],
                    in_=qro[:, h * HD:(h + 1) * HD]),
                    waits=[(sV, sV.n)] if h == 0 else (), incs=[(sG, 16)])
            p.op("vector", lambda e: e.tensor_copy(
                out=st(C_S1), in_=st(C_S1)), waits=[(sG, sG.n)],
                incs=[(sV, 1)])

    if debug:
        p.op("gpsimd", lambda e: e.dma_start(out=dbg_ws, in_=ws3x[:]),
             waits=[(sV, sV.n), (sA, sA.n)], incs=[(sG, 16)])
    barrier()
    if stop_early:
        p.op("gpsimd", lambda e: e.dma_start(
            out=bass.AP(ws3_dram.tensor, 0, [[3072, 1], [1, 16]]),
            in_=stats[0:1, 0:16]), waits=[(sG, sG.n)], incs=[(sG, 16)])
        phKV.close()
        phWS.close()
        with nc.Block() as block:
            @block.sync
            def _(eng):
                p.replay("sync", eng)

            @block.gpsimd
            def _(eng):
                p.replay("gpsimd", eng)

            @block.tensor
            def _(eng):
                p.replay("tensor", eng)

            @block.scalar
            def _(eng):
                p.replay("scalar", eng)

            @block.vector
            def _(eng):
                p.replay("vector", eng)
        es.close()
        return nc
    barrier()
    phWS.close()

    with contextlib.ExitStack() as ph:
        dmask_s = sbg(ph, "dmask_s", [128, 512])
        p.op("sync", lambda e: e.dma_start(out=dmask_s[:], in_=dmask),
             waits=[(sV, sV.n), (sP, sP.n), (sG, sG.n)], incs=[(sL, 16)])
        qsb = sbg(ph, "qsb", [128, NT * HD])
        ksb = [sbg(ph, "ksb0", [128, 512]), sbg(ph, "ksb1", [128, 512])]
        vsb = [sbg(ph, "vsb0", [128, 512]), sbg(ph, "vsb1", [128, 512])]
        QT_hi = sbg(ph, "QT_hi", [128, TPC], BF16)
        QT_lo = sbg(ph, "QT_lo", [128, TPC], BF16)
        KT_hi = sbg(ph, "KT_hi", [128, 512], BF16)
        KT_lo = sbg(ph, "KT_lo", [128, 512], BF16)
        V_hi = sbg(ph, "V_hi", [128, 512], BF16)
        V_lo = sbg(ph, "V_lo", [128, 512], BF16)
        exs = [sbg(ph, "exs0", [128, 512]), sbg(ph, "exs1", [128, 512])]
        eT_hi = sbg(ph, "eT_hi", [128, 4 * TPC], BF16)
        eT_lo = sbg(ph, "eT_lo", [128, 4 * TPC], BF16)
        rsum = sbg(ph, "rsum", [128, 64])
        rtot = sbg(ph, "rtot", [128, NT])
        rrec = sbg(ph, "rrec", [128, NT])
        ysb = sbg(ph, "ysb", [128, TPC])
        pT1 = psg(ph, "pT1", [128, 1024])
        pE = psg(ph, "pE", [128, 512])
        pS = psg(ph, "pS", [128, 1024])
        pY = psg(ph, "pY", [128, 1024])

        for h in range(H):
            kh = h // (H // KVH)
            qw = [(sP, sP.n), (sA, sA.n)]
            if h == 0:
                qw.append((sG, sG.n))
            p.op("sync", lambda e, h=h: e.dma_start(
                out=qsb[:],
                in_=Q_scr[h].rearrange("(a p) d -> p a d", p=128)),
                waits=qw, incs=[(sL, 16)])
            for g in range(2):
                transposeN(qsb, g * 512, pS,
                           bass.AP(QT_hi, g * 512, [[TPC, 128], [1, 512]]),
                           4, src_wait=[(sL, sL.n), (sV, sV.n), (sA, sA.n)])
                p.op("vector", lambda e, g=g: e.scalar_tensor_tensor(
                    out=bass.AP(QT_lo, g * 512, [[TPC, 128], [1, 512]]),
                    in0=pS[:, 0:512], scalar=1.0,
                    in1=bass.AP(QT_hi, g * 512, [[TPC, 128], [1, 512]]),
                    op0=ALU.mult, op1=ALU.subtract),
                    waits=[(sA, sA.n)], incs=[(sV, 1)])
            av_done_pe = sP.n
            kbuf_free = [sP.n, sP.n]
            vbuf_free = [(sA.n, sV.n), (sA.n, sV.n)]
            for c in range(NCH):
                kb_, vb_ = ksb[c % 2], vsb[c % 2]
                p.op("sync", lambda e, c=c, kh=kh, kb_=kb_: e.dma_start(
                    out=kb_[:],
                    in_=K_scr[kh, c * 512:(c + 1) * 512, :].rearrange(
                        "(a p) d -> p a d", p=128)),
                    waits=[(sP, kbuf_free[c % 2])], incs=[(sL, 16)])
                p.op("gpsimd", lambda e, c=c, kh=kh, vb_=vb_: e.dma_start(
                    out=vb_[:],
                    in_=V_scr[kh, c * 512:(c + 1) * 512, :].rearrange(
                        "(a p) d -> p a d", p=128)),
                    waits=[(sP, av_done_pe)] if True else [],
                    incs=[(sG, 16)])
                transposeN(kb_, 0, pT1,
                           bass.AP(KT_hi, 0, [[512, 128], [1, 512]]), 4,
                           src_wait=[(sL, sL.n), (sV, sV.n), (sA, sA.n)])
                kbuf_free[c % 2] = sP.n
                p.op("vector", lambda e: e.scalar_tensor_tensor(
                    out=KT_lo[:], in0=pT1[:, 0:512], scalar=1.0,
                    in1=KT_hi[:], op0=ALU.mult, op1=ALU.subtract),
                    waits=[(sA, sA.n)], incs=[(sV, 1)])
                p.op("scalar", lambda e, vb_=vb_: e.activation(
                    out=V_hi[:], in_=vb_[:], func=ACT.Copy),
                    waits=[(sG, sG.n), (sP, av_done_pe)], incs=[(sA, 1)])
                p.op("vector", lambda e, vb_=vb_: e.scalar_tensor_tensor(
                    out=V_lo[:], in0=vb_[:], scalar=1.0, in1=V_hi[:],
                    op0=ALU.mult, op1=ALU.subtract),
                    waits=[(sA, sA.n)], incs=[(sV, 1)])
                exp_done = {}
                for i in range(c, NT):
                    sl = pS[:, (i % 2) * 512:(i % 2) * 512 + 512]
                    tpe = pT1[:, 512:1024] if i % 2 == 0 else pE[:, 0:512]
                    wt_ = []
                    if i <= c + 1:
                        wt_ += [(sA, sA.n), (sV, sV.n)]
                    elif (i - 2) in exp_done:
                        wt_ += [(sA, exp_done[i - 2][0]),
                                (sV, exp_done[i - 2][1])]
                    for pi, (lt, rt) in enumerate(
                            ((QT_hi, KT_hi), (QT_lo, KT_hi), (QT_hi, KT_lo))):
                        p.op("tensor", lambda e, sl=sl, lt=lt, rt=rt, i=i,
                             pi=pi: e.matmul(
                                 sl,
                                 bass.AP(lt, i * 128, [[TPC, 128], [1, 128]]),
                                 rt[:], start=(pi == 0), stop=(pi == 2),
                                 skip_group_check=True),
                             waits=wt_ if pi == 0 else (),
                             incs=[(sP, 1)] if pi == 2 else ())
                    if i == c:
                        p.op("vector", lambda e, sl=sl: e.scalar_tensor_tensor(
                            out=sl, in0=dmask_s[:], scalar=NEGBIG, in1=sl,
                            op0=ALU.mult, op1=ALU.add),
                            waits=[(sP, sP.n)], incs=[(sV, 1)])
                        mwait = [(sV, sV.n)]
                    else:
                        mwait = [(sP, sP.n)]
                    eb = exs[i % 2]
                    p.op("scalar", lambda e, sl=sl, eb=eb, i=i, c=c:
                         e.activation(
                             out=eb[:], in_=sl, func=ACT.Exp, scale=SMSCALE,
                             accum_out=rsum[:, i * 8 + c:i * 8 + c + 1]),
                         waits=mwait, incs=[(sA, 1)])
                    twait = [(sA, sA.n)]
                    if i <= c + 1:
                        twait.append((sV, sV.n))
                    elif (i - 2) in exp_done:
                        twait.append((sV, exp_done[i - 2][1]))
                    for s_ in range(4):
                        p.op("tensor", lambda e, tpe=tpe, eb=eb, s_=s_:
                             e.transpose(tpe[:, s_ * 128:(s_ + 1) * 128],
                                         eb[:, s_ * 128:(s_ + 1) * 128],
                                         eyef_s[:]),
                             waits=twait if s_ == 0 else (),
                             incs=[(sP, 1)] if s_ == 3 else ())
                    dst_hi = bass.AP(eT_hi, i * 128,
                                     [[4 * TPC, 128], [TPC, 4], [1, 128]])
                    dst_lo = bass.AP(eT_lo, i * 128,
                                     [[4 * TPC, 128], [TPC, 4], [1, 128]])
                    p.op("scalar", lambda e, tpe=tpe, dst_hi=dst_hi:
                         e.activation(out=dst_hi, in_=tpe[:, 0:512],
                                      func=ACT.Copy),
                         waits=[(sP, sP.n)], incs=[(sA, 1)])
                    p.op("vector", lambda e, tpe=tpe, dst_hi=dst_hi,
                         dst_lo=dst_lo: e.scalar_tensor_tensor(
                             out=dst_lo, in0=tpe[:, 0:512], scalar=1.0,
                             in1=dst_hi, op0=ALU.mult, op1=ALU.subtract),
                         waits=[(sA, sA.n)], incs=[(sV, 1)])
                    exp_done[i] = (sA.n, sV.n)
                avw = [(sA, sA.n), (sV, sV.n)]
                mm_list = []
                for s_ in range(4):
                    for (vt, et) in ((V_hi, eT_hi), (V_lo, eT_hi),
                                     (V_hi, eT_lo)):
                        segs = ([(c * 128, 512)] if c * 128 < 512 else []) + \
                               [(max(512, c * 128), 1024)]
                        for (a0, a1) in segs:
                            mm_list.append((s_, vt, et, a0, a1))
                for n_, (s_, vt, et, a0, a1) in enumerate(mm_list):
                    st_flag = (c == 0 and n_ < 2)
                    last = (n_ == len(mm_list) - 1)
                    p.op("tensor", lambda e, s_=s_, vt=vt, et=et, a0=a0,
                         a1=a1, st_flag=st_flag: e.matmul(
                             pY[:, a0:a1], vt[:, s_ * 128:(s_ + 1) * 128],
                             bass.AP(et, s_ * TPC + a0,
                                     [[4 * TPC, 128], [1, a1 - a0]]),
                             start=st_flag, stop=False,
                             skip_group_check=True),
                         waits=avw if n_ == 0 else (),
                         incs=[(sP, 1)] if last else ())
                av_done_pe = sP.n
            for i in range(NT):
                p.op("vector", lambda e, i=i: e.tensor_reduce(
                    out=rtot[:, i:i + 1],
                    in_=rsum[:, i * 8:i * 8 + i + 1], axis=AX.X, op=ALU.add),
                    waits=[(sA, sA.n)] if i == 0 else (),
                    incs=[(sV, 1)] if i == NT - 1 else ())
            p.op("vector", lambda e: e.reciprocal(out=rrec[:], in_=rtot[:]),
                 waits=[(sV, sV.n)], incs=[(sV, 1)])
            p.op("scalar", lambda e: e.activation(
                out=ysb[:, 0:512], in_=pY[:, 0:512], func=ACT.Copy),
                waits=[(sP, av_done_pe)])
            p.op("scalar", lambda e: e.activation(
                out=ysb[:, 512:1024], in_=pY[:, 512:1024], func=ACT.Copy),
                incs=[(sA, 1)])
            for i in range(NT):
                p.op("tensor", lambda e, i=i: e.transpose(
                    pS[:, i * 128:(i + 1) * 128],
                    ysb[:, i * 128:(i + 1) * 128], eyef_s[:]),
                    waits=[(sA, sA.n)] if i == 0 else (),
                    incs=[(sP, 1)] if i == NT - 1 else ())
            for i in range(NT):
                p.op("scalar", lambda e, i=i, h=h: e.activation(
                    out=yx[:, i * D + h * HD:i * D + h * HD + HD],
                    in_=pS[:, i * 128:(i + 1) * 128], func=ACT.Copy,
                    scale=rrec[:, i:i + 1]),
                    waits=[(sP, sP.n), (sV, sV.n)] if i == 0 else (),
                    incs=[(sA, 1)] if i == NT - 1 else ())

    if debug:
        for t in range(NT):
            p.op("gpsimd", lambda e, t=t: e.dma_start(
                out=dbg_y[t * 128:(t + 1) * 128, :],
                in_=yx[:, t * D:(t + 1) * D]),
                waits=[(sA, sA.n)] if t == 0 else (), incs=[(sG, 16)])

    barrier()

    with contextlib.ExitStack() as ph:
        wsox = sbg(ph, "wsox", [128, D])
        woqT = sbg(ph, "woqT", [128, 16 * 2048], BF16)
        x0b = [sbg(ph, "x0b0", [128, D]), sbg(ph, "x0b1", [128, D])]
        yq = sbg(ph, "yq", [128, D])
        yqT = sbg(ph, "yqT", [128, D], BF16)
        od = sbg(ph, "od", [128, D])
        tpw = psg(ph, "tpwc", [128, 2048])
        pO = psg(ph, "pO", [128, 2048])
        with contextlib.ExitStack() as phw:
            wt = [sbg(phw, "wt0c", [128, D]), sbg(phw, "wt1c", [128, D])]
            wtmp = sbg(phw, "wtmpc", [128, D])
            wqb = sbg(phw, "wqbc", [128, D])
            for r in range(16):
                buf = wt[r % 2]
                p.op("sync", lambda e, r=r, buf=buf: e.dma_start(
                    out=buf[:], in_=w_o[r * 128:(r + 1) * 128, :]),
                    waits=[(sV, sV.n)] if r >= 2 else
                    [(sP, sP.n), (sA, sA.n)], incs=[(sL, 16)])
                quant_w_tile(buf, wtmp, wqb, wso_dram, r * 128, 2048)
                transposeN(wqb, 0, tpw,
                           bass.AP(woqT, r * 128,
                                   [[16 * 2048, 128], [2048, 16], [1, 128]]),
                           16, src_wait=[(sV, sV.n), (sA, sA.n)])
        barrier()
        p.op("gpsimd", lambda e: e.dma_start(
            out=wsox[:],
            in_=bass.AP(wso_dram.tensor, 0, [[2048, 1], [0, 128], [1, 2048]])),
            waits=[(sG, sG.n)], incs=[(sG, 16)])
        wso_rep = sG.n
        for t in range(NT):
            xb = x0b[t % 2]
            p.op("sync", lambda e, t=t, xb=xb: e.dma_start(
                out=xb[:], in_=x_q[t * 128:(t + 1) * 128, :]),
                waits=[(sV, sV.n)] if t >= 2 else [(sV, sV.n), (sA, sA.n)],
                incs=[(sL, 16)])
            ytile = yx[:, t * D:(t + 1) * D]
            p.op("vector", lambda e, ytile=ytile: e.reduce_max(
                out=st(C_AM), in_=ytile, axis=AX.X,
                apply_absolute_value=True), waits=[(sA, sA.n)],
                incs=[(sV, 1)])
            p.op("vector", lambda e: e.tensor_scalar(
                out=st(C_QS), in0=st(C_AM), scalar1=1.0 / 127.0, scalar2=QEPS,
                op0=ALU.mult, op1=ALU.max), waits=[(sV, sV.n)],
                incs=[(sV, 1)])
            p.op("vector", lambda e, t=t: e.tensor_copy(
                out=asc_y[:, t:t + 1], in_=st(C_QS)), waits=[(sV, sV.n)])
            p.op("vector", lambda e: e.reciprocal(out=st(C_RA), in_=st(C_QS)),
                 incs=[(sV, 1)])
            p.op("scalar", lambda e, ytile=ytile: e.activation(
                out=yq[:], in_=ytile, func=ACT.Copy, scale=st(C_RA),
                bias=RND), waits=[(sV, sV.n)], incs=[(sA, 1)])
            p.op("scalar", lambda e: e.activation(
                out=od[:], in_=yq[:], func=ACT.Copy, bias=-RND),
                waits=[(sA, sA.n)], incs=[(sA, 1)])
            p.op("vector", lambda e: e.tensor_copy(
                out=st(C_S1), in_=st(C_S1)), waits=[(sA, sA.n)],
                incs=[(sV, 1)])
            transposeN(od, 0, tpw, bass.AP(yqT, 0, [[D, 128], [1, D]]), 16,
                       src_wait=[(sV, sV.n), (sA, sA.n)])
            for ot in range(4):
                for kb in range(16):
                    p.op("tensor", lambda e, ot=ot, kb=kb: e.matmul(
                        pO[:, ot * 512:(ot + 1) * 512],
                        bass.AP(yqT, kb * 128, [[D, 128], [1, 128]]),
                        bass.AP(woqT, kb * 2048 + ot * 512,
                                [[16 * 2048, 128], [1, 512]]),
                        start=(kb == 0), stop=(kb == 15),
                        skip_group_check=True),
                        waits=[(sA, sA.n)] if (ot == 0 and kb == 0) else (),
                        incs=[(sP, 1)] if (ot == 3 and kb == 15) else ())
            p.op("scalar", lambda e, t=t: e.activation(
                out=od[:], in_=pO[:], func=ACT.Copy,
                scale=asc_y[:, t:t + 1]), waits=[(sP, sP.n)], incs=[(sA, 1)])
            dq_wait = [(sA, sA.n), (sL, sL.n)]
            if t == 0:
                dq_wait.append((sG, wso_rep))
            p.op("vector", lambda e: e.tensor_tensor(
                out=od[:], in0=od[:], in1=wsox[:], op=ALU.mult),
                waits=dq_wait)
            p.op("vector", lambda e, ytile=ytile, xb=xb: e.tensor_tensor(
                out=ytile, in0=od[:], in1=xb[:], op=ALU.add), incs=[(sV, 1)])

    barrier()

    with contextlib.ExitStack() as ph:
        rw1h = sbg(ph, "rw1h", [128, 16 * 64], BF16)
        rw1l = sbg(ph, "rw1l", [128, 16 * 64], BF16)
        rw2s = sbg(ph, "rw2s", [64, 32])
        rbg = sbg(ph, "rbg", [128, 128])
        bvs = sbg(ph, "bvs", [128, 512])
        ath = sbg(ph, "ath", [128, 16 * 256], BF16)
        atl = sbg(ph, "atl", [128, 16 * 256], BF16)
        bmh = sbg(ph, "bmh", [128, 2 * D], BF16)
        bml = sbg(ph, "bml", [128, 2 * D], BF16)
        x2Th = sbg(ph, "x2Th", [128, 16 * 1024], BF16)
        x2Tl = sbg(ph, "x2Tl", [128, 16 * 1024], BF16)
        xn2 = sbg(ph, "xn2", [128, D])
        hsb = sbg(ph, "hsb", [128, 64])
        hT = sbg(ph, "hT", [64, 128])
        lsb = sbg(ph, "lsb", [128, 32])
        lg = sbg(ph, "lg", [128, 16])
        eq1 = sbg(ph, "eq1", [128, 16])
        eq2 = sbg(ph, "eq2", [128, 16])
        lgm = sbg(ph, "lgm", [128, 16])
        cf1 = sbg(ph, "cf1", [128, 16])
        coef = sbg(ph, "coef", [128, NT * 16])
        wxa = sbg(ph, "wxa", [128, 256])
        wxTh = sbg(ph, "wxTh", [128, NT * 256], BF16)
        wxTl = sbg(ph, "wxTl", [128, NT * 256], BF16)
        wmb = [sbg(ph, "wmb0", [128, 512]), sbg(ph, "wmb1", [128, 512])]
        wmh = [sbg(ph, "wmh0", [128, 512], BF16),
               sbg(ph, "wmh1", [128, 512], BF16)]
        wml = [sbg(ph, "wml0", [128, 512], BF16),
               sbg(ph, "wml1", [128, 512], BF16)]
        osb = [sbg(ph, "osb0", [128, 512]), sbg(ph, "osb1", [128, 512])]

        with contextlib.ExitStack() as phs:
            srcb = sbg(phs, "srcb", [128, 2048])
            p.op("sync", lambda e: e.dma_start(out=rw2s[:], in_=rw2_T),
                 waits=[(sV, sV.n), (sP, sP.n), (sA, sA.n)], incs=[(sL, 16)])
            p.op("sync", lambda e: e.dma_start(out=rbg[:, 0:64], in_=rb1),
                 incs=[(sL, 16)])
            p.op("sync", lambda e: e.dma_start(out=rbg[:, 64:96], in_=rb2),
                 incs=[(sL, 16)])
            for (srcdram, dd, na, ca, hi_, lo_) in (
                    (rw1_T, 64, 16, 16, rw1h, rw1l),
                    (A_T, 256, 16, 8, ath, atl),
                    (B_mov, 2048, 2, 1, bmh, bml)):
                for a0 in range(0, na, ca):
                    ncols = ca * dd
                    p.op("sync", lambda e, srcdram=srcdram, dd=dd, a0=a0,
                         ca=ca, ncols=ncols: e.dma_start(
                             out=srcb[:, 0:ncols],
                             in_=bass.AP(srcdram.tensor, a0 * 128 * dd,
                                         [[dd, 128], [128 * dd, ca],
                                          [1, dd]])),
                         waits=[(sV, sV.n)], incs=[(sL, 16)])
                    p.op("scalar", lambda e, ncols=ncols, hi_=hi_, a0=a0,
                         dd=dd: e.activation(
                             out=hi_[:, a0 * dd:a0 * dd + ncols],
                             in_=srcb[:, 0:ncols], func=ACT.Copy),
                         waits=[(sL, sL.n)], incs=[(sA, 1)])
                    p.op("vector", lambda e, ncols=ncols, hi_=hi_, lo_=lo_,
                         a0=a0, dd=dd: e.scalar_tensor_tensor(
                             out=lo_[:, a0 * dd:a0 * dd + ncols],
                             in0=srcb[:, 0:ncols], scalar=1.0,
                             in1=hi_[:, a0 * dd:a0 * dd + ncols],
                             op0=ALU.mult, op1=ALU.subtract),
                         waits=[(sA, sA.n)], incs=[(sV, 1)])

        with contextlib.ExitStack() as php:
            tpf = psg(php, "tpf", [128, 2048])
            pR = psg(php, "pR", [128, 512])
            for t in range(NT):
                x1t = yx[:, t * D:(t + 1) * D]
                p.op("vector", lambda e, x1t=x1t: e.tensor_reduce(
                    out=st(C_S1), in_=x1t, axis=AX.X, op=ALU.add),
                    incs=[(sV, 1)])
                p.op("scalar", lambda e, x1t=x1t: e.activation(
                    out=xn2[:], in_=x1t, func=ACT.Square,
                    accum_out=st(C_S2)), waits=[(sV, sV.n)])
                p.op("scalar", lambda e: e.activation(
                    out=st(C_MU), in_=st(C_S1), func=ACT.Copy,
                    scale=1.0 / D), waits=[(sV, sV.n)], incs=[(sA, 1)])
                p.op("scalar", lambda e: e.activation(
                    out=st(C_MU2), in_=st(C_MU), func=ACT.Square),
                    waits=[(sA, sA.n)], incs=[(sA, 1)])
                p.op("vector", lambda e: e.tensor_scalar(
                    out=st(C_VAR), in0=st(C_S2), scalar1=1.0 / D,
                    scalar2=st(C_MU2), op0=ALU.mult, op1=ALU.subtract),
                    waits=[(sA, sA.n)], incs=[(sV, 1)])
                p.op("scalar", lambda e: e.activation(
                    out=st(C_RSTD), in_=st(C_VAR), func=ACT.Sqrt,
                    bias=LNEPS), waits=[(sV, sV.n)], incs=[(sA, 1)])
                p.op("vector", lambda e: e.reciprocal(
                    out=st(C_RSTD), in_=st(C_RSTD)), waits=[(sA, sA.n)],
                    incs=[(sV, 1)])
                p.op("vector", lambda e, x1t=x1t: e.tensor_scalar(
                    out=xn2[:], in0=x1t, scalar1=st(C_MU),
                    scalar2=st(C_RSTD), op0=ALU.subtract, op1=ALU.mult),
                    waits=[(sV, sV.n)], incs=[(sV, 1)])
                transposeN(xn2, 0, tpf,
                           bass.AP(x2Th, t * 128,
                                   [[16 * 1024, 128], [1024, 16], [1, 128]]),
                           16, src_wait=[(sV, sV.n), (sA, sA.n)])
                p.op("vector", lambda e, t=t: e.scalar_tensor_tensor(
                    out=bass.AP(x2Tl, t * 128,
                                [[16 * 1024, 128], [1024, 16], [1, 128]]),
                    in0=tpf[:, 0:2048], scalar=1.0,
                    in1=bass.AP(x2Th, t * 128,
                                [[16 * 1024, 128], [1024, 16], [1, 128]]),
                    op0=ALU.mult, op1=ALU.subtract),
                    waits=[(sA, sA.n)], incs=[(sV, 1)])
                for pi, (lt, rt) in enumerate(((x2Th, rw1h), (x2Tl, rw1h),
                                               (x2Th, rw1l))):
                    for kb in range(16):
                        p.op("tensor", lambda e, pi=pi, lt=lt, rt=rt, kb=kb,
                             t=t: e.matmul(
                                 pR[:, 0:64],
                                 bass.AP(lt, kb * 1024 + t * 128,
                                         [[16 * 1024, 128], [1, 128]]),
                                 bass.AP(rt, kb * 64,
                                         [[16 * 64, 128], [1, 64]]),
                                 start=(pi == 0 and kb == 0),
                                 stop=(pi == 2 and kb == 15),
                                 skip_group_check=True),
                             waits=[(sV, sV.n)] if (pi == 0 and kb == 0)
                             else (),
                             incs=[(sP, 1)] if (pi == 2 and kb == 15)
                             else ())
                p.op("vector", lambda e: e.scalar_tensor_tensor(
                    out=pR[:, 0:64], in0=rbg[:, 0:64], scalar=1.0,
                    in1=pR[:, 0:64], op0=ALU.mult, op1=ALU.add),
                    waits=[(sP, sP.n)], incs=[(sV, 1)])
                p.op("scalar", lambda e: e.activation(
                    out=hsb[:], in_=pR[:, 0:64], func=ACT.Silu),
                    waits=[(sV, sV.n)], incs=[(sA, 1)])
                p.op("tensor", lambda e: e.transpose(
                    tpf[0:64, 0:128], hsb[:], eyef_s[:]),
                    waits=[(sA, sA.n)], incs=[(sP, 1)])
                p.op("scalar", lambda e: e.activation(
                    out=hT[:], in_=tpf[0:64, 0:128], func=ACT.Copy),
                    waits=[(sP, sP.n)], incs=[(sA, 1)])
                p.op("tensor", lambda e: e.matmul(
                    pR[:, 128:160], hT[:], rw2s[:], start=True, stop=True,
                    skip_group_check=True), waits=[(sA, sA.n)],
                    incs=[(sP, 1)])
                p.op("vector", lambda e: e.scalar_tensor_tensor(
                    out=lsb[:], in0=rbg[:, 64:96], scalar=1.0,
                    in1=pR[:, 128:160], op0=ALU.mult, op1=ALU.add),
                    waits=[(sP, sP.n)])
                p.op("vector", lambda e: e.tensor_tensor(
                    out=lg[:], in0=lsb[:, 0:16], in1=gt_s[:],
                    op=ALU.add))
                p.op("vector", lambda e: e.reduce_max(
                    out=st(C_M1), in_=lg[:], axis=AX.X), incs=[(sV, 1)])
                p.op("vector", lambda e: e.tensor_scalar(
                    out=eq1[:], in0=lg[:], scalar1=st(C_M1), scalar2=None,
                    op0=ALU.is_equal), waits=[(sV, sV.n)], incs=[(sV, 1)])
                p.op("vector", lambda e: e.scalar_tensor_tensor(
                    out=lgm[:], in0=eq1[:], scalar=NEGBIG, in1=lg[:],
                    op0=ALU.mult, op1=ALU.add), waits=[(sV, sV.n)],
                    incs=[(sV, 1)])
                p.op("vector", lambda e: e.reduce_max(
                    out=st(C_M2), in_=lgm[:], axis=AX.X),
                    waits=[(sV, sV.n)], incs=[(sV, 1)])
                p.op("vector", lambda e: e.tensor_scalar_mul(
                    out=st(C_NM1), in0=st(C_M1), scalar1=-1.0),
                    incs=[(sV, 1)])
                p.op("scalar", lambda e: e.activation(
                    out=st(C_E2), in_=st(C_M2), func=ACT.Exp,
                    bias=st(C_NM1)), waits=[(sV, sV.n)], incs=[(sA, 1)])
                p.op("vector", lambda e: e.tensor_scalar_add(
                    out=st(C_DEN), in0=st(C_E2), scalar1=1.0),
                    waits=[(sA, sA.n)], incs=[(sV, 1)])
                p.op("vector", lambda e: e.reciprocal(
                    out=st(C_RDEN), in_=st(C_DEN)), waits=[(sV, sV.n)],
                    incs=[(sV, 1)])
                p.op("vector", lambda e: e.tensor_scalar(
                    out=st(C_C2), in0=st(C_E2), scalar1=st(C_RDEN),
                    scalar2=None, op0=ALU.mult), waits=[(sV, sV.n)],
                    incs=[(sV, 1)])
                p.op("vector", lambda e: e.tensor_scalar(
                    out=eq2[:], in0=lg[:], scalar1=st(C_M2), scalar2=None,
                    op0=ALU.is_equal))
                p.op("vector", lambda e: e.tensor_scalar(
                    out=cf1[:], in0=eq1[:], scalar1=st(C_RDEN), scalar2=None,
                    op0=ALU.mult), incs=[(sV, 1)])
                p.op("vector", lambda e, t=t: e.scalar_tensor_tensor(
                    out=coef[:, t * 16:(t + 1) * 16], in0=eq2[:],
                    scalar=st(C_C2), in1=cf1[:], op0=ALU.mult, op1=ALU.add),
                    waits=[(sV, sV.n)], incs=[(sV, 1)])
                for pi, (lt, rt) in enumerate(((x2Th, ath), (x2Tl, ath),
                                               (x2Th, atl))):
                    for kb in range(16):
                        p.op("tensor", lambda e, pi=pi, lt=lt, rt=rt, kb=kb,
                             t=t: e.matmul(
                                 pR[:, 256:512],
                                 bass.AP(lt, kb * 1024 + t * 128,
                                         [[16 * 1024, 128], [1, 128]]),
                                 bass.AP(rt, kb * 256,
                                         [[16 * 256, 128], [1, 256]]),
                                 start=(pi == 0 and kb == 0),
                                 stop=(pi == 2 and kb == 15),
                                 skip_group_check=True),
                             waits=[(sV, sV.n)] if (pi == 0 and kb == 0)
                             else (),
                             incs=[(sP, 1)] if (pi == 2 and kb == 15)
                             else ())
                p.op("vector", lambda e, t=t: e.scalar_tensor_tensor(
                    out=wxa[:], in0=pR[:, 256:512], scalar=1.0,
                    in1=bass.AP(coef, t * 16,
                                [[NT * 16, 128], [1, 16], [0, 16]]),
                    op0=ALU.mult, op1=ALU.mult), waits=[(sP, sP.n)],
                    incs=[(sV, 1)])
                transposeN(wxa, 0, tpf,
                           bass.AP(wxTh, t * 256,
                                   [[NT * 256, 128], [128, 2], [1, 128]]),
                           2, src_wait=[(sV, sV.n), (sA, sA.n)])
                p.op("vector", lambda e, t=t: e.scalar_tensor_tensor(
                    out=bass.AP(wxTl, t * 256,
                                [[NT * 256, 128], [128, 2], [1, 128]]),
                    in0=tpf[:, 0:256], scalar=1.0,
                    in1=bass.AP(wxTh, t * 256,
                                [[NT * 256, 128], [128, 2], [1, 128]]),
                    op0=ALU.mult, op1=ALU.subtract),
                    waits=[(sA, sA.n)], incs=[(sV, 1)])

        pBig = psg(ph, "pBig", [128, 4096])
        for ot in range(4):
            evac_sA_before = sA.n
            p.op("sync", lambda e, ot=ot: e.dma_start(
                out=bvs[:], in_=biasv[:, ot * 512:(ot + 1) * 512]),
                waits=[(sV, sV.n)], incs=[(sL, 16)])
            bvs_loaded = sL.n
            for kb in range(16):
                buf, bh, bl = wmb[kb % 2], wmh[kb % 2], wml[kb % 2]
                p.op("sync", lambda e, ot=ot, kb=kb, buf=buf: e.dma_start(
                    out=buf[:], in_=W_T[kb * 128:(kb + 1) * 128,
                                        ot * 512:(ot + 1) * 512]),
                    waits=[(sV, sV.n)], incs=[(sL, 16)])
                p.op("scalar", lambda e, buf=buf, bh=bh: e.activation(
                    out=bh[:], in_=buf[:], func=ACT.Copy),
                    waits=[(sL, sL.n), (sP, sP.n)], incs=[(sA, 1)])
                p.op("vector", lambda e, buf=buf, bh=bh, bl=bl:
                     e.scalar_tensor_tensor(out=bl[:], in0=buf[:],
                                            scalar=1.0, in1=bh[:],
                                            op0=ALU.mult, op1=ALU.subtract),
                     waits=[(sA, sA.n)], incs=[(sV, 1)])
                for t in range(NT):
                    for pi, (lt, rt) in enumerate(((x2Th, bh), (x2Tl, bh),
                                                   (x2Th, bl))):
                        p.op("tensor", lambda e, t=t, pi=pi, lt=lt, rt=rt,
                             kb=kb: e.matmul(
                                 pBig[:, t * 512:(t + 1) * 512],
                                 bass.AP(lt, kb * 1024 + t * 128,
                                         [[16 * 1024, 128], [1, 128]]),
                                 rt[:], start=(pi == 0 and kb == 0),
                                 stop=False, skip_group_check=True),
                             waits=([(sV, sV.n)] +
                                    ([(sA, evac_sA_before)] if ot > 0
                                     else []))
                             if (t == 0 and pi == 0) else (),
                             incs=[(sP, 1)] if (t == NT - 1 and pi == 2)
                             else ())
            for t in range(NT):
                for pi, (lt, rt) in enumerate(((wxTh, bmh), (wxTl, bmh),
                                               (wxTh, bml))):
                    for kb2 in range(2):
                        p.op("tensor", lambda e, t=t, pi=pi, lt=lt, rt=rt,
                             kb2=kb2, ot=ot: e.matmul(
                                 pBig[:, t * 512:(t + 1) * 512],
                                 bass.AP(lt, t * 256 + kb2 * 128,
                                         [[NT * 256, 128], [1, 128]]),
                                 bass.AP(rt, kb2 * 2048 + ot * 512,
                                         [[2 * 2048, 128], [1, 512]]),
                                 start=False, stop=(pi == 2 and kb2 == 1),
                                 skip_group_check=True),
                             incs=[(sP, 1)] if (t == NT - 1 and pi == 2 and
                                                kb2 == 1) else ())
            for t in range(NT):
                ob = osb[t % 2]
                ew = [(sP, sP.n), (sG, sG.n)]
                p.op("scalar", lambda e, t=t, ob=ob: e.activation(
                    out=ob[:], in_=pBig[:, t * 512:(t + 1) * 512],
                    func=ACT.Copy), waits=ew, incs=[(sA, 1)])
                p.op("vector", lambda e, ob=ob: e.tensor_tensor(
                    out=ob[:], in0=ob[:], in1=bvs[:],
                    op=ALU.add), waits=[(sA, sA.n), (sL, bvs_loaded)])
                p.op("vector", lambda e, t=t, ot=ot, ob=ob: e.tensor_tensor(
                    out=ob[:], in0=ob[:],
                    in1=yx[:, t * D + ot * 512:t * D + ot * 512 + 512],
                    op=ALU.add), incs=[(sV, 1)])
                p.op("gpsimd", lambda e, t=t, ot=ot, ob=ob: e.dma_start(
                    out=out[t * 128:(t + 1) * 128, ot * 512:(ot + 1) * 512],
                    in_=ob[:]), waits=[(sV, sV.n)], incs=[(sG, 16)])
        p.op("gpsimd", lambda e: e.dma_start(
            out=bass.AP(ws3_dram.tensor, 0, [[3072, 1], [1, 16]]),
            in_=stats[0:1, 0:16]), waits=[(sG, sG.n)], incs=[(sG, 16)])

    with nc.Block() as block:
        @block.sync
        def _(eng):
            p.replay("sync", eng)

        @block.gpsimd
        def _(eng):
            p.replay("gpsimd", eng)

        @block.tensor
        def _(eng):
            p.replay("tensor", eng)

        @block.scalar
        def _(eng):
            p.replay("scalar", eng)

        @block.vector
        def _(eng):
            p.replay("vector", eng)

    es.close()
    return nc


def kernel(L0_attn_x, L0_attn_w_q, L0_attn_w_k, L0_attn_w_v, L0_attn_w_o,
           L0_attn_q_gain, L0_ffn_W, L0_ffn_bias, L0_ffn_rw1, L0_ffn_rb1,
           L0_ffn_rw2, L0_ffn_rb2, L0_ffn_A, L0_ffn_B, L0_ffn_gates):
    x = np.ascontiguousarray(np.asarray(L0_attn_x, dtype=np.float32))
    w_qkv = np.ascontiguousarray(np.concatenate(
        [np.asarray(L0_attn_w_q), np.asarray(L0_attn_w_k),
         np.asarray(L0_attn_w_v)], axis=0).astype(np.float32))
    w_o = np.ascontiguousarray(np.asarray(L0_attn_w_o, dtype=np.float32))
    W_T = np.ascontiguousarray(np.asarray(L0_ffn_W, dtype=np.float32).T)
    biasv = np.ascontiguousarray(np.broadcast_to(
        np.asarray(L0_ffn_bias, dtype=np.float32).reshape(1, D), (128, D)))
    rw1_T = np.ascontiguousarray(np.asarray(L0_ffn_rw1, dtype=np.float32).T)
    rw2_T = np.ascontiguousarray(np.asarray(L0_ffn_rw2, dtype=np.float32).T)
    rb1 = np.ascontiguousarray(np.broadcast_to(
        np.asarray(L0_ffn_rb1, dtype=np.float32).reshape(1, 64), (128, 64)))
    rb2 = np.ascontiguousarray(np.broadcast_to(
        np.asarray(L0_ffn_rb2, dtype=np.float32).reshape(1, 32), (128, 32)))
    gates = np.ascontiguousarray(np.broadcast_to(
        np.asarray(L0_ffn_gates, dtype=np.float32).reshape(1, 16), (128, 16)))
    qgain = np.ascontiguousarray(np.broadcast_to(
        np.asarray(L0_attn_q_gain, dtype=np.float32).reshape(1, 16),
        (128, 16)))
    A_T = np.ascontiguousarray(
        np.asarray(L0_ffn_A, dtype=np.float32).reshape(256, D).T)
    B_mov = np.ascontiguousarray(
        np.asarray(L0_ffn_B, dtype=np.float32).transpose(0, 2, 1)
        .reshape(256, D))
    eyef = np.eye(128, dtype=np.float32)
    constsv = np.ascontiguousarray(np.broadcast_to(
        np.array([[LNEPS, RMSEPS, RND, 0.0, -RND, 0, 0, 0]],
                 dtype=np.float32), (128, 8)))

    inv_freq = 1.0 / (10000.0 ** (np.arange(0, HD, 2, dtype=np.float64) / HD))
    fr = np.outer(np.arange(S, dtype=np.float64), inv_freq)
    cos_all = np.cos(fr).astype(np.float32)
    sin_all = np.sin(fr).astype(np.float32)
    cos_k = np.ascontiguousarray(np.concatenate([cos_all, cos_all], axis=1))
    sin_k = np.ascontiguousarray(np.concatenate([sin_all, -sin_all], axis=1))

    nc = build_program()

    in_maps = []
    for core in range(8):
        b, j = divmod(core, 4)
        sel = np.arange(32)[j::4]
        xq_rows = x[b].reshape(32, 128, D)[sel].reshape(TPC, D)
        pos_q = (sel[:, None] * 128 + np.arange(128)[None, :]).reshape(-1)
        dm = (np.arange(512)[None, :] >
              (np.arange(128)[:, None] + 128 * j)).astype(np.float32)
        in_maps.append({
            "x_kv": np.ascontiguousarray(x[b]),
            "x_q": np.ascontiguousarray(xq_rows),
            "w_qkv": w_qkv, "w_o": w_o, "W_T": W_T, "biasv": biasv,
            "rw1_T": rw1_T, "rw2_T": rw2_T, "rb1": rb1, "rb2": rb2,
            "gates": gates, "qgain": qgain, "A_T": A_T, "B_mov": B_mov,
            "cos_q": np.ascontiguousarray(cos_k[pos_q]),
            "sin_q": np.ascontiguousarray(sin_k[pos_q]),
            "cos_k": cos_k, "sin_k": sin_k,
            "eyef": eyef, "dmask": np.ascontiguousarray(dm),
            "constsv": constsv,
        })

    res = run_bass_kernel_spmd(nc, in_maps, core_ids=list(range(8)))
    outv = np.empty((B, S, D), dtype=np.float32)
    for core in range(8):
        b, j = divmod(core, 4)
        sel = np.arange(32)[j::4]
        outv[b].reshape(32, 128, D)[sel] = \
            res.results[core]["o"].reshape(NT, 128, D)
    return outv


if __name__ == "__main__":
    nc = build_program()
    print("built ok")
